# version 41
# baseline (speedup 1.0000x reference)
import numpy as np
import concourse.bass as bass
import concourse.bacc as bacc
import concourse.mybir as mybir
from concourse.bass_utils import run_bass_kernel_spmd
from contextlib import ExitStack

F32 = mybir.dt.float32
BF16 = mybir.dt.bfloat16
U32 = mybir.dt.uint32
I32 = mybir.dt.int32
AF = mybir.ActivationFunctionType
OP = mybir.AluOpType
AX = mybir.AxisListType

ENGS = ("T", "V", "S", "P", "Q")


def full_cfg():
    return dict(T=4096, D=1024, NH=4, d=128, C=128, KC=4, E=32, EBLK=4,
                HME=128, HS=512, n_cores=8)


class Op:
    __slots__ = ("eng", "fn", "inc", "waits", "marked", "semval", "idx",
                 "slot", "lane_k", "pre_wait")

    def __init__(self, eng, fn, inc, idx):
        self.eng, self.fn, self.inc, self.idx = eng, fn, inc, idx
        self.waits = []
        self.marked = inc != 1
        self.semval = None
        self.slot = eng
        self.lane_k = None
        self.pre_wait = None


NLANES = 6


class Sched:
    def __init__(self):
        self.q = {e: [] for e in ENGS}
        self.tiles = {}
        self.waited = {e: {} for e in ENGS}
        self.lane_rr = {"Q": 0, "P": 0}
        self.lane_cnt = {}

    def _t(self, name):
        if name not in self.tiles:
            self.tiles[name] = {"w": {}, "r": {}}
        return self.tiles[name]

    def emit(self, eng, fn, reads=(), writes=(), inc=1, lane=None):
        op = Op(eng, fn, inc, len(self.q[eng]))
        if inc != 1:
            if lane is None:
                lane = self.lane_rr[eng]
                self.lane_rr[eng] = (lane + 1) % NLANES
            slot = (eng, lane)
            k = self.lane_cnt.get(slot, 0) + 1
            self.lane_cnt[slot] = k
            op.slot = slot
            op.lane_k = k
            if k > 1:
                op.pre_wait = (slot, 16 * (k - 1))
        need = {}

        def consider(p):
            if p.eng == eng and eng == "T":
                return
            cur = need.get(p.slot)
            if cur is None or p.idx > cur.idx:
                need[p.slot] = p

        for b in reads:
            for p in self._t(b)["w"].values():
                consider(p)
        for b in writes:
            t = self._t(b)
            for p in t["w"].values():
                consider(p)
            for p in t["r"].values():
                consider(p)
        wd = self.waited[eng]
        for ps, p in need.items():
            if p.idx > wd.get(ps, -1):
                p.marked = True
                op.waits.append(p)
                wd[ps] = p.idx
        self.q[eng].append(op)
        for b in writes:
            t = self._t(b)
            t["w"][op.slot] = op
            t["r"] = {}
        for b in reads:
            self._t(b)["r"][op.slot] = op
        return op

    def merge_deps(self, new_key, old_keys):
        t = self._t(new_key)
        for ok in old_keys:
            o = self._t(ok)
            for s, p in o["w"].items():
                if s not in t["w"] or t["w"][s].idx < p.idx:
                    t["w"][s] = p
            for s, p in o["r"].items():
                if s not in t["r"] or t["r"][s].idx < p.idx:
                    t["r"][s] = p

    def barrier_wait(self, eng, producer_op):
        op = Op(eng, None, 0, len(self.q[eng]))
        producer_op.marked = True
        op.waits.append(producer_op)
        self.waited[eng][producer_op.slot] = max(
            self.waited[eng].get(producer_op.slot, -1), producer_op.idx)
        self.q[eng].append(op)
        return op

    def finalize(self):
        for e in ENGS:
            cum = 0
            for op in self.q[e]:
                if op.lane_k is not None:
                    op.semval = 16 * op.lane_k
                else:
                    if op.marked:
                        cum += op.inc
                    op.semval = cum


def build_nc(cfg, mode="fused"):
    T, D, NH, d, C = cfg["T"], cfg["D"], cfg["NH"], cfg["d"], cfg["C"]
    KC, E, EBLK, HME, HS = cfg["KC"], cfg["E"], cfg["EBLK"], cfg["HME"], cfg["HS"]
    ND = D // 128
    NCH = T // C
    OWN = T // 2
    NOT = OWN // 128
    NHS = HS // 128
    NGR = OWN // 512
    NTI = T // 128
    scale = float(d) ** -0.5
    HD = NH * d
    n_cores = cfg["n_cores"]
    NBLK = E // EBLK
    NEB = max(EBLK, NHS)
    NHALF = max(1, D // 512)
    HSZ = D // NHALF

    nc = bacc.Bacc("TRN2", target_bir_lowering=False)

    dI = lambda n, s, dt=F32: nc.dram_tensor(n, s, dt, kind="ExternalInput")
    x_b = dI("x_b", [T, D])
    x_own = dI("x_own", [OWN, D])
    Wqh = dI("Wqh", [D, HD]); Wkh = dI("Wkh", [D, HD])
    Wvh = dI("Wvh", [D, HD]); Wwh = dI("Wwh", [D, HD])
    Wg4 = dI("Wg4", [D, NH]); Wb4 = dI("Wb4", [D, NH])
    conv4 = dI("conv4", [HD, KC])
    qn_in = dI("qn", [1, d]); kn_in = dI("kn", [1, d])
    Wo_in = dI("Wo", [D, D])
    routerT = dI("routerT", [D, E])
    rbias = dI("rbias", [1, E])
    gateT = dI("gateT", [E, D, HME])
    upT = dI("upT", [E, D, HME])
    downT = dI("downT", [E, HME, D])
    sgT = dI("sgT", [D, HS]); suT = dI("suT", [D, HS])
    sdT = dI("sdT", [HS, D])
    gidx = dI("gidx", [128, 3 * NOT], U32)
    out_d = nc.dram_tensor("out", [OWN, D], F32, kind="ExternalOutput")

    xnT_dram = nc.dram_tensor("xnT_dram", [NTI, 128, ND * 128], BF16)
    o_bounce = nc.dram_tensor("o_bounce", [T, HD], BF16,
                              kind="ExternalOutput" if mode == "p1" else "Internal")
    o_gath = nc.dram_tensor("o_gath", [2 * T, HD], BF16)
    gdram = nc.dram_tensor("gdram", [4, NH, T], F32)
    w4dram = nc.dram_tensor("w4dram", [E, OWN], BF16)
    if mode == "p1":
        g_out = nc.dram_tensor("g_out", [OWN, D], BF16, kind="ExternalOutput")
    if mode == "p2":
        o_own_in = nc.dram_tensor("o_own", [OWN, D], BF16, kind="ExternalInput")
        g_in = nc.dram_tensor("g_in", [OWN, D], BF16, kind="ExternalInput")

    replica_groups = [[2 * i, 2 * i + 1] for i in range(n_cores // 2)]

    sch = Sched()

    with ExitStack() as st:
        ec = st.enter_context
        sems = {e: ec(nc.semaphore(f"sem_{e}")) for e in ENGS}
        for qe in ("Q", "P"):
            for l in range(NLANES):
                sems[(qe, l)] = ec(nc.semaphore(f"sem_{qe}{l}"))
        sems[("P", "cc")] = ec(nc.semaphore("sem_cc"))
        sems["rsem"] = ec(nc.semaphore("rsem"))
        sems["lsem"] = ec(nc.semaphore("lsem"))

        def sbuf(name, shape, dt):
            return ec(nc.sbuf_tensor(name, shape, dt))

        a2 = sbuf("a2", [128, NCH * HD], BF16)
        ffn_acc = sbuf("ffn_acc", [128, NOT * D], BF16)
        vt = sbuf("vt", [128, NCH * 132], BF16)
        khat = sbuf("khat", [128, T], BF16)
        wcb = sbuf("wcb", [128, T], BF16)
        wpre = sbuf("wpre", [128, 131], F32)
        cvs = sbuf("cvs", [128, 256], F32)
        xnt = [sbuf(f"xnt{i}", [128, ND * 128], BF16) for i in range(2)]
        xin = [sbuf(f"xin{i}", [128, D], BF16) for i in range(2)]
        names128 = ["q_tok", "k_tok", "qT", "kT", "w_tok",
                    "bw_tok", "wT", "A1", "A1T", "R1", "Ak", "AkT", "Ak2", "Ak2T",
                    "PTa", "PTb", "Wbar", "Xc", "WQm", "scrbf"]
        t128 = {}
        for nm in names128:
            for p in range(2):
                t128[(nm, p)] = sbuf(f"{nm}{p}", [128, 128], BF16)
        names8 = ["QtilT", "KtilT", "PhiT", "Phi", "expD"]
        t8 = {}
        for nm in names8:
            for sl in range(8):
                t8[(nm, sl)] = sbuf(f"{nm}s{sl}", [128, 128], BF16)
        expS = [sbuf(f"expS{i}", [128, 512], BF16) for i in range(3)]
        tmpS1 = sbuf("tmpS0", [128, 512], F32)
        tmpS = [tmpS1, tmpS1]
        expI = [sbuf(f"expI{i}", [128, 128], BF16) for i in range(2)]
        tmpI = [sbuf(f"tmpI{i}", [128, 128], F32) for i in range(2)]
        tmpD = [sbuf(f"tmpD{i}", [128, 128], F32) for i in range(2)]
        qch = sbuf("qch", [128, 128], BF16)
        qch2 = sbuf("qch2", [128, 128], BF16)
        qhat = [sbuf(f"qhat{i}", [128, 512], BF16) for i in range(2)]
        khat_loc = sbuf("khat_loc", [128, 384], BF16)
        psT = sbuf("psT", [128, 128], BF16)
        psT2 = sbuf("psT2", [128, 128], BF16)
        gbc = [sbuf(f"gbc{i}", [128, 512], F32) for i in range(2)]
        vpre = sbuf("vpre", [128, 512], BF16)
        nrm = [sbuf(f"nrm{i}", [128, 4], F32) for i in range(2)]
        nsc = [sbuf(f"nsc{i}", [128, 4], F32) for i in range(2)]
        nt2 = [sbuf(f"nt2{i}", [128, 4], F32) for i in range(2)]
        gsg = [sbuf(f"gsg{i}", [1, 128], F32) for i in range(2)]
        ident_bf = sbuf("ident_bf", [128, 128], BF16)
        ident_f = sbuf("ident_f", [128, 128], F32)
        maskSL = sbuf("maskSL", [128, 128], BF16)
        triuK = sbuf("triuK", [128, 128], BF16)
        maskneg = sbuf("maskneg", [128, 128], F32)
        ones1 = sbuf("ones1", [1, 128], F32)
        ones1b = sbuf("ones1b", [1, 128], BF16)
        qn_row = sbuf("qn_row", [1, d], F32)
        kn_row = sbuf("kn_row", [1, d], F32)
        qn_bc = sbuf("qn_bc", [128, d], F32)
        kn_bc = sbuf("kn_bc", [128, d], F32)
        rb_row = sbuf("rb_row", [1, E], F32)
        rb_bc = sbuf("rb_bc", [128, E], F32)
        bnst4 = sbuf("bnst4", [128, 48], F32)
        bnag4 = sbuf("bnag4", [128, 8], F32)
        lnn = [sbuf(f"lnn{i}", [128, 1], F32) for i in range(4)]
        lnsc = [sbuf(f"lnsc{i}", [128, 1], F32) for i in range(4)]
        lnt = [sbuf(f"lnt{i}", [128, 1], F32) for i in range(4)]
        lnscr = sbuf("lnscr", [128, 512], BF16)
        qst = sbuf("qst", [128, 2], F32)
        kst = sbuf("kst", [128, 2], F32)
        wst = sbuf("wst", [128, 2], F32)
        orc = sbuf("orc", [128, 1], F32)
        rts = sbuf("rts", [128, 4], F32)
        GnP = [sbuf(f"GnC{i}", [128, NCH], F32) for i in range(2)]
        btP = [sbuf(f"btC{i}", [128, NCH], F32) for i in range(2)]
        bnP = [sbuf(f"bnC{i}", [128, NCH], F32) for i in range(2)]
        gseg = sbuf("gseg", [NCH, 128], F32)
        wg_sb = sbuf("wg_sb", [128, ND * NH], BF16)
        wb_sb = sbuf("wb_sb", [128, ND * NH], BF16)
        cw_sb = sbuf("cw_sb", [128, NH * KC], F32)
        idx_sb = sbuf("idx_sb", [128, 3 * NOT], U32)
        w4buf = sbuf("w4buf", [128, E], F32)
        rtf = [sbuf(f"rtf{i}", [128, E], F32) for i in range(3)]
        rt_sb = sbuf("rt_sb", [128, ND * E], BF16)

        AR_BYTES = 62 * 1024
        arena = sbuf("arena", [128, AR_BYTES], mybir.dt.uint8)
        ar_f32 = arena.ap().bitcast(F32)
        ar_bf = arena.ap().bitcast(BF16)

        def av(off, cols, dt):
            if dt == F32:
                assert off % 4 == 0
                return ar_f32[:, off // 4: off // 4 + cols]
            assert off % 2 == 0
            return ar_bf[:, off // 2: off // 2 + cols]

        AT_W = {}
        offa = 0
        for nm in ("wq", "wk", "wv", "ww"):
            for p in range(2):
                AT_W[(nm, p)] = av(offa, ND * 128, BF16)
                offa += ND * 128 * 2
        AT_KEYS = [f"{nm}{p}" for nm in ("wq", "wk", "wv", "ww") for p in range(2)]
        stripe = av(0, 3 * T, F32)
        assert 3 * T * 4 <= AR_BYTES
        assert offa <= AR_BYTES

        offf = 0

        def fav(cols, dt):
            nonlocal offf
            r = av(offf, cols, dt)
            offf += cols * (4 if dt == F32 else 2)
            return r

        eWg = [[fav(ND * HME, BF16) for _ in range(NEB)] for _ in range(2)]
        eWu1 = [fav(ND * HME, BF16) for _ in range(NEB)]
        eWu = [eWu1, eWu1]
        eWd1 = [fav(D, BF16) for _ in range(NEB)]
        eWd = [eWd1, eWd1]
        hw_all = fav(NEB * OWN, BF16)
        w4T = fav(OWN, BF16)
        w4row = [fav(OWN, BF16) for _ in range(2)]
        FFN_KEYS = ["eWg0", "eWg1", "eWu0", "eWd0",
                    "hw_all", "w4T", "w4row0", "w4row1"]
        assert offf <= AR_BYTES, offf

        offz = 0

        def zav(cols, dt):
            nonlocal offz
            r = av(offz, cols, dt)
            offz += cols * (4 if dt == F32 else 2)
            return r

        wo_sb = zav(ND * D, BF16)
        og = [zav(D, BF16) for _ in range(2)]
        oTt = [zav(ND * 128, BF16) for _ in range(2)]
        xo = [zav(D, F32) for _ in range(2)]
        sfin = zav(D, F32)
        outt = [zav(D, F32) for _ in range(2)]
        FIN_KEYS = ["wo_sb", "og0", "og1", "oTt0", "oTt1", "xo0", "xo1",
                    "sfin", "outt0", "outt1"]
        assert offz <= AR_BYTES, offz

        pm = [ec(nc.psum_tensor(f"pm{i}", [128, 512], F32)) for i in range(4)]
        pb = [ec(nc.psum_tensor(f"pb{i}", [128, 1024], BF16)) for i in range(2)]
        po2 = [ec(nc.psum_tensor(f"po2{i}", [128, 288], F32)) for i in range(2)]

        cvals = (0.0, 1.0, 1e-5, 1e-6)
        cap = sbuf("caps", [128, len(cvals)], F32)
        for i_, v_ in enumerate(cvals):
            nc.const_aps.register(F32, v_, cap[:, i_:i_ + 1])

        COLLECT = [None]

        def op(eng, meth, *args, reads=(), writes=(), inc=1, lane=None, **kw):
            if meth == "activation":
                reads = tuple(reads) + ("consts",)
            def fn(EE, _m=meth, _a=args, _k=kw, _r=reads, _w=writes):
                try:
                    return getattr(EE, _m)(*_a, **_k)
                except Exception:
                    import sys
                    shapes = [getattr(x, "shape", None) for x in _a]
                    print(f"FAILING OP: {_m} args={shapes} kw={list(_k)} "
                          f"reads={_r} writes={_w}", file=sys.stderr)
                    raise
            if COLLECT[0] is not None:
                COLLECT[0].append(
                    lambda _f=fn, _r=reads, _w=writes, _i=inc, _l=lane, _e=eng:
                    sch.emit(_e, _f, _r, _w, _i, lane=_l))
                return None
            return sch.emit(eng, fn, reads, writes, inc, lane=lane)

        def collect(builder):
            lst = []
            COLLECT[0] = lst
            builder()
            COLLECT[0] = None
            return lst

        PIPELINE = True

        def newton_rsqrt(nr, sc, tt_, key, skey, tkey, n):
            x = nr[:, 0:n]
            xi = nr.bitcast(I32)[:, 0:n]
            yi = sc.bitcast(I32)[:, 0:n]
            y = sc[:, 0:n]
            t = tt_[:, 0:n]
            op("V", "tensor_scalar", yi, xi, 1, None, OP.arith_shift_right,
               reads=(key,), writes=(skey,))
            op("V", "tensor_scalar", yi, yi, float(0x5F3759DF), -1.0,
               OP.subtract, op1=OP.mult, reads=(skey,), writes=(skey,))
            for _ in range(3):
                op("V", "tensor_tensor", t, y, y, OP.mult,
                   reads=(skey,), writes=(tkey,))
                op("V", "tensor_tensor", t, t, x, OP.mult,
                   reads=(tkey, key), writes=(tkey,))
                op("V", "tensor_scalar", t, t, -0.5, 1.5, OP.mult, op1=OP.add,
                   reads=(tkey,), writes=(tkey,))
                op("V", "tensor_tensor", y, y, t, OP.mult,
                   reads=(skey, tkey), writes=(skey,))

        def interleave_emit(lb, la):
            if not PIPELINE:
                for f_ in lb:
                    f_()
                for f_ in la:
                    f_()
                return
            nb, na = len(lb), len(la)
            ib = ia = 0
            while ib < nb or ia < na:
                if ia >= na or (ib < nb and ib * (na + 1) <= ia * (nb + 1)):
                    lb[ib](); ib += 1
                else:
                    la[ia](); ia += 1

        def mm(ps, lhsT, rhs, reads, writes, start=True, stop=True,
               skip_group_check=False):
            return op("T", "matmul", ps, lhsT, rhs, start=start, stop=stop,
                      skip_group_check=skip_group_check,
                      reads=reads, writes=writes)

        def tr(ps, in_ap, ident_ap, reads, writes):
            return op("T", "transpose", ps, in_ap, ident_ap,
                      reads=tuple(reads) + ("ident",), writes=writes)

        def dma(q, out_ap, in_ap, reads, writes):
            return op(q, "dma_start", reads=reads, writes=writes, inc=16,
                      out=out_ap, in_=in_ap)

        def dma_w(q, out_ap, in_2d, nch, reads, writes, slow=False):
            o3 = out_ap.rearrange("p (c n) -> p c n", c=nch)
            i3 = in_2d.rearrange("(c p) n -> p c n", p=128)
            kw = {"allow_slow_non_contiguous": True} if slow else {}
            return op(q, "dma_start", reads=reads, writes=writes, inc=16,
                      out=o3, in_=i3, **kw)

        def mask_init(tile, key, setval, cmpop, fill, cm=1, pat=-1):
            op("P", "memset", tile[:, :], setval, writes=(key,))
            op("P", "affine_select", tile[:, :], tile[:, :], [[pat, 128]],
               cmpop, fill, base=0, channel_multiplier=cm,
               reads=(key,), writes=(key,))

        mask_init(ident_bf, "ident", 0.0, OP.not_equal, 1.0)
        mask_init(ident_f, "ident", 0.0, OP.not_equal, 1.0)
        mask_init(maskSL, "maskSL", 1.0, OP.is_gt, 0.0)
        mask_init(triuK, "triuK", 1.0, OP.is_ge, 0.0, cm=-1, pat=1)
        mask_init(maskneg, "maskneg", 0.0, OP.is_ge, -1e30, cm=-1, pat=1)
        op("P", "memset", ones1[:, :], 1.0, writes=("ones1",))
        op("P", "memset", ones1b[:, :], 1.0, writes=("ones1",))
        for i_, v_ in enumerate(cvals):
            op("P", "memset", cap[:, i_:i_ + 1], v_, writes=("consts",))


        dma("Q", qn_row[:, :], qn_in.ap(), reads=(), writes=("qn_row",))
        dma("Q", kn_row[:, :], kn_in.ap(), reads=(), writes=("kn_row",))
        dma("Q", rb_row[:, :], rbias.ap(), reads=(), writes=("rb_row",))
        dma("Q", idx_sb[:, :], gidx.ap(), reads=(), writes=("idx_sb",))
        mm(pm[0][:, 0:d], ones1[:, 0:128], qn_row[:, :], reads=("ones1", "qn_row"), writes=("pm0",))
        op("V", "tensor_copy", qn_bc[:, :], pm[0][:, 0:d], reads=("pm0",), writes=("qn_bc",))
        mm(pm[0][:, 0:d], ones1[:, 0:128], kn_row[:, :], reads=("ones1", "kn_row"), writes=("pm0",))
        op("V", "tensor_copy", kn_bc[:, :], pm[0][:, 0:d], reads=("pm0",), writes=("kn_bc",))
        mm(pm[0][:, 0:E], ones1[:, 0:128], rb_row[:, :], reads=("ones1", "rb_row"), writes=("pm0",))
        op("V", "tensor_copy", rb_bc[:, :], pm[0][:, 0:E], reads=("pm0",), writes=("rb_bc",))

        dma_w("P", wg_sb[:, :], Wg4.ap(), ND, reads=(), writes=("wg_sb",), slow=True)
        dma_w("P", wb_sb[:, :], Wb4.ap(), ND, reads=(), writes=("wb_sb",), slow=True)
        for h in range(NH):
            dma("P", cw_sb[:, h * KC:(h + 1) * KC], conv4.ap()[h * 128:(h + 1) * 128, :],
                reads=(), writes=("cw_sb",))

        SKIP_MAIN = (mode == "p2")
        def s_zg(c0, c1):
            return stripe[0:NH, c0:c1]

        def s_zb(c0, c1):
            return stripe[0:NH, T + c0:T + c1]

        def s_sc(c0, c1):
            return stripe[0:NH, 2 * T + c0:2 * T + c1]

        wcb_bf = wcb[:, :]
        khat_bf = khat[:, :]
        xinX = [xin[0][:, :], xin[1][:, :], wcb_bf[:, 0:D], wcb_bf[:, D:2 * D]]
        stgX = [vt[:, 0:D], wcb_bf[:, 2 * D:3 * D]]
        xntX = [xnt[0][:, :], xnt[1][:, :], khat_bf[:, 0:D], khat_bf[:, D:2 * D]]
        trbank = [(pb[0], "pb0"), (pb[1], "pb1")]

        def ln_tile(src_dram, ti, full):
            b = ti % 4
            b2 = ti % 2
            xk = f"xin{b}"
            sk = f"stg_tok{b2}"
            dma("P", xinX[b], src_dram.ap()[ti * 128:(ti + 1) * 128, :],
                reads=(), writes=(xk,))
            nbst = (D + 511) // 512
            bns, bnsk = bnst4[:, b * 12:(b + 1) * 12], f"bnst{b}"
            bna, bnak = bnag4[:, b * 2:(b + 1) * 2], f"bnag{b}"
            for j in range(nbst):
                c0, c1 = j * 512, min(D, (j + 1) * 512)
                op("V", "bn_stats", bns[:, j * 6:(j + 1) * 6], xinX[b][:, c0:c1],
                   reads=(xk,), writes=(bnsk,))
            op("V", "bn_aggr", bna[:, :], bns[:, 0:6 * nbst],
               reads=(bnsk,), writes=(bnak,))
            op("V", "tensor_scalar", lnn[b][:, :], bna[:, 1:2], 1e-5, None, OP.add,
               reads=(bnak,), writes=(f"lnn{b}",))
            newton_rsqrt(lnn[b], lnsc[b], lnt[b], f"lnn{b}", f"lnsc{b}", f"lnt{b}", 1)
            op("V", "tensor_scalar", stgX[b2][:, 0:D], xinX[b], bna[:, 0:1],
               lnsc[b][:, 0:1], OP.subtract, op1=OP.mult,
               reads=(xk, bnak, f"lnsc{b}"), writes=(sk,))
            for c in range(ND):
                pbt, pbk = trbank[c % 2]
                tr(pbt[:, 0:128], stgX[b2][:, c * 128:(c + 1) * 128], ident_bf[:, :],
                   reads=(sk,), writes=(pbk,))
                if full:
                    op("V", "tensor_copy", xntX[b][:, c * 128:(c + 1) * 128],
                       pbt[:, 0:128], reads=(pbk,), writes=(f"xnt{b}",))
                else:
                    op("V", "tensor_copy", a2[:, c * OWN + ti * 128: c * OWN + (ti + 1) * 128],
                       pbt[:, 0:128], reads=(pbk,), writes=("a2",))
            if full:
                dma("Q", xnT_dram.ap()[ti], xntX[b], reads=(f"xnt{b}",),
                    writes=("xnT_dram",))
                for c in range(ND):
                    mm(pm[1][0:NH, 0:128], wg_sb[:, c * NH:(c + 1) * NH],
                       xntX[b][:, c * 128:(c + 1) * 128],
                       reads=("wg_sb", f"xnt{b}"), writes=("pm1",),
                       start=(c == 0), stop=(c == ND - 1))
                op("V", "tensor_copy", s_zg(ti * 128, (ti + 1) * 128), pm[1][0:NH, 0:128],
                   reads=("pm1",), writes=("stripe",))
                for c in range(ND):
                    mm(pm[3][0:NH, 0:128], wb_sb[:, c * NH:(c + 1) * NH],
                       xntX[b][:, c * 128:(c + 1) * 128],
                       reads=("wb_sb", f"xnt{b}"), writes=("pm3",),
                       start=(c == 0), stop=(c == ND - 1))
                op("V", "tensor_copy", s_zb(ti * 128, (ti + 1) * 128), pm[3][0:NH, 0:128],
                   reads=("pm3",), writes=("stripe",))

        N_EARLY = min(NOT, (NEB * OWN) // D)

        def ln_early(ti):
            b = 2 + ti % 2
            xk = f"xin{b}"
            dma("P", xinX[b], x_own.ap()[ti * 128:(ti + 1) * 128, :],
                reads=(), writes=(xk,))
            bns, bnsk = bnst4[:, b * 12:(b + 1) * 12], f"bnst{b}"
            bna, bnak = bnag4[:, b * 2:(b + 1) * 2], f"bnag{b}"
            nbst = (D + 511) // 512
            for j in range(nbst):
                c0, c1 = j * 512, min(D, (j + 1) * 512)
                op("V", "bn_stats", bns[:, j * 6:(j + 1) * 6], xinX[b][:, c0:c1],
                   reads=(xk,), writes=(bnsk,))
            op("V", "bn_aggr", bna[:, :], bns[:, 0:6 * nbst],
               reads=(bnsk,), writes=(bnak,))
            op("V", "tensor_scalar", lnn[b][:, :], bna[:, 1:2], 1e-5, None, OP.add,
               reads=(bnak,), writes=(f"lnn{b}",))
            newton_rsqrt(lnn[b], lnsc[b], lnt[b], f"lnn{b}", f"lnsc{b}", f"lnt{b}", 1)
            op("V", "tensor_scalar", stgX[1][:, 0:D], xinX[b], bna[:, 0:1],
               lnsc[b][:, 0:1], OP.subtract, op1=OP.mult,
               reads=(xk, bnak, f"lnsc{b}"), writes=("stg_tok1",))
            for c in range(ND):
                pbt, pbk = trbank[c % 2]
                tr(pbt[:, 0:128], stgX[1][:, c * 128:(c + 1) * 128], ident_bf[:, :],
                   reads=("stg_tok1",), writes=(pbk,))
                op("V", "tensor_copy",
                   hw_all[:, ti * D + c * 128: ti * D + (c + 1) * 128],
                   pbt[:, 0:128], reads=(pbk,), writes=("hw_all",))

        for ti in range(NTI if not SKIP_MAIN else 0):
            ln_tile(x_b, ti, True)

        zg, zb, sc = s_zg(0, T), s_zb(0, T), s_sc(0, T)
        if not SKIP_MAIN:
            op("S", "activation", sc, zg, AF.Abs, reads=("stripe",), writes=("stripe",))
            op("S", "activation", sc, sc, AF.Exp, scale=-1.0, reads=("stripe",), writes=("stripe",))
            op("S", "activation", sc, sc, AF.Ln, bias=1.0, reads=("stripe",), writes=("stripe",))
            op("V", "tensor_scalar", zg, zg, 0.0, None, OP.min, reads=("stripe",), writes=("stripe",))
            op("V", "tensor_tensor", zg, zg, sc, OP.subtract, reads=("stripe",), writes=("stripe",))
            op("V", "tensor_tensor_scan", sc, zg, zg, 0.0, OP.add, OP.bypass,
               reads=("stripe",), writes=("stripe",))
            dma("Q", gdram.ap()[0], sc, reads=("stripe",), writes=("gdram",))
            op("V", "tensor_scalar", zg, sc, -1.0, None, OP.mult, reads=("stripe",), writes=("stripe",))
            dma("Q", gdram.ap()[1], zg, reads=("stripe",), writes=("gdram",))
            dma("Q", gdram.ap()[2], zb, reads=("stripe",), writes=("gdram",))

        for k in AT_KEYS:
            sch.merge_deps(k, ["stripe"])
        sch.merge_deps("wcb", ["xin2", "xin3", "stg_tok1"])
        sch.merge_deps("khat", ["xnt2", "xnt3"])
        sch.merge_deps("vt", ["stg_tok0"])
        for ci in range(NCH):
            op("P", "memset", vt[:, ci * 132 + 128: ci * 132 + 129], 1.0, writes=("vt",))

        def head_weights(h):
            hp = h % 2
            for nm, src in (("wq", Wqh), ("wk", Wkh), ("wv", Wvh), ("ww", Wwh)):
                dma_w("P", AT_W[(nm, hp)], src.ap()[:, h * 128:(h + 1) * 128],
                      ND, reads=(), writes=(f"{nm}{hp}",))

        def head_gcols(h, pmx, pmxk):
            hg = h % 2
            for gi, dstc, dck in ((1, GnP[hg], f"GnC{hg}"), (2, bnP[hg], f"bnC{hg}")):
                dma("P", gseg[:, :],
                    gdram.ap()[gi, h].rearrange("(c t) -> c t", c=NCH),
                    reads=("gdram",), writes=("gseg",))
                tr(pmx[0:128, 0:NCH], gseg[:, :], ident_f[0:NCH, 0:NCH],
                   reads=("gseg",), writes=(pmxk,))
                op("V", "tensor_copy", dstc[:, :], pmx[0:128, 0:NCH],
                   reads=(pmxk,), writes=(dck,))
            bnc, btc = bnP[hg], btP[hg]
            bnk, btk = f"bnC{hg}", f"btC{hg}"
            op("S", "activation", bnc[:, :], bnc[:, :], AF.Exp, scale=-1.0,
               reads=(bnk,), writes=(bnk,))
            op("V", "tensor_scalar", bnc[:, :], bnc[:, :], 1.0, None, OP.add,
               reads=(bnk,), writes=(bnk,))
            op("V", "reciprocal", bnc[:, :], bnc[:, :], reads=(bnk,), writes=(bnk,))
            op("V", "tensor_scalar", btc[:, :], bnc[:, :], 2.0, None, OP.mult,
               reads=(bnk,), writes=(btk,))
            op("V", "tensor_scalar", bnc[:, :], bnc[:, :], -2.0, None, OP.mult,
               reads=(bnk,), writes=(bnk,))

        def pass1_convs(h):
            hp = h % 2
            for ci in range(NCH):
                b = ci % 2
                xk = f"xnt{b}"
                pmx, pmxk = (pm[0], "pm0") if b == 0 else (pm[1], "pm1")
                dma("Q", xnt[b][:, :], xnT_dram.ap()[ci], reads=("xnT_dram",), writes=(xk,))
                for c in range(ND):
                    mm(pmx[:, 0:128], AT_W[("ww", hp)][:, c * 128:(c + 1) * 128],
                       xnt[b][:, c * 128:(c + 1) * 128],
                       reads=(f"ww{hp}", xk), writes=(pmxk,),
                       start=(c == 0), stop=(c == ND - 1))
                if ci == 0:
                    op("V", "memset", wpre[:, 0:3], 0.0, writes=("wpre",))
                else:
                    op("V", "tensor_copy", wpre[:, 0:3], wpre[:, 128:131],
                       reads=("wpre",), writes=("wpre",))
                op("V", "tensor_copy", wpre[:, 3:131], pmx[:, 0:128],
                   reads=(pmxk,), writes=("wpre",))
                ca0, ca1 = cvs[:, 0:128], cvs[:, 128:256]
                op("V", "tensor_scalar", ca0, wpre[:, 0:128],
                   cw_sb[:, h * KC: h * KC + 1], None, OP.mult,
                   reads=("wpre", "cw_sb"), writes=("cvs",))
                for i in (1, 2):
                    src_, dst_ = (ca0, ca1) if i % 2 == 1 else (ca1, ca0)
                    op("V", "scalar_tensor_tensor", dst_, wpre[:, i:i + 128],
                       cw_sb[:, h * KC + i: h * KC + i + 1], src_, OP.mult, OP.add,
                       reads=("wpre", "cw_sb", "cvs"), writes=("cvs",))
                op("V", "scalar_tensor_tensor", wcb[:, ci * 128:(ci + 1) * 128],
                   wpre[:, 3:131], cw_sb[:, h * KC + 3: h * KC + 4], ca0,
                   OP.mult, OP.add,
                   reads=("wpre", "cw_sb", "cvs"), writes=("wcb",))

        def pass1_silus(h):
            for ci in range(NCH):
                op("S", "activation", wcb[:, ci * 128:(ci + 1) * 128],
                   wcb[:, ci * 128:(ci + 1) * 128], AF.Silu,
                   reads=("wcb",), writes=("wcb",))

        def head_prefetch(h, pmx, pmxk):
            head_weights(h)
            head_gcols(h, pmx, pmxk)
            pass1_convs(h)
            pass1_silus(h)

        for h in range(NH if not SKIP_MAIN else 0):
            hp = h % 2
            hg = h % 2
            if h == 0:
                head_prefetch(0, pm[2], "pm2")

            SS = 4
            NSUP = NCH // SS

            def stage_A(ah, ci, vpr=False):
                hp = ah % 2
                hg = ah % 2
                p = ci % 2
                sp = (ci // SS) % 2
                li = ci % SS
                slot = li + SS * sp
                tk = lambda nm: t128[(nm, p)]
                kk = lambda nm: f"{nm}{p}"
                t8k = lambda nm: f"{nm}s{slot}"
                xk = f"xnt{p}"
                dma("Q", xnt[p][:, :], xnT_dram.ap()[ci], reads=("xnT_dram",), writes=(xk,))
                dma("Q", gsg[p][:, :], gdram.ap()[0, ah:ah + 1, ci * 128:(ci + 1) * 128],
                    reads=("gdram",), writes=(f"gsg{p}",))
                mm(pm[1][:, 0:128], ones1[:, 0:128], gsg[p][:, :],
                   reads=("ones1", f"gsg{p}"), writes=("pm1",))
                op("V", "tensor_copy", gbc[sp][:, li * 128:(li + 1) * 128], pm[1][:, 0:128],
                   reads=("pm1",), writes=(f"gbc{sp}",))
                for c in range(ND):
                    mm(pm[0][:, 0:128], xnt[p][:, c * 128:(c + 1) * 128],
                       AT_W[("wq", hp)][:, c * 128:(c + 1) * 128],
                       reads=(xk, f"wq{hp}"), writes=("pm0",),
                       start=(c == 0), stop=(c == ND - 1))
                nr, sc_, tt_ = nrm[p], nsc[p], nt2[p]
                nrk, sck, ttk = f"nrm{p}", f"nsc{p}", f"nt2{p}"
                op("S", "activation", tk("scrbf")[:, :], pm[0][:, 0:128], AF.Square,
                   scale=float(d) ** -0.5, accum_out=nr[:, 0:1],
                   reads=("pm0",), writes=(kk("scrbf"), nrk))
                for c in range(ND):
                    mm(pm[1][:, 0:128], xnt[p][:, c * 128:(c + 1) * 128],
                       AT_W[("wk", hp)][:, c * 128:(c + 1) * 128],
                       reads=(xk, f"wk{hp}"), writes=("pm1",),
                       start=(c == 0), stop=(c == ND - 1))
                op("S", "activation", tk("scrbf")[:, :], pm[1][:, 0:128], AF.Square,
                   scale=float(d) ** -0.5, accum_out=nr[:, 1:2],
                   reads=("pm1",), writes=(kk("scrbf"), nrk))
                tr(pb[0][:, 0:128], wcb[:, ci * 128:(ci + 1) * 128], ident_bf[:, :],
                   reads=("wcb",), writes=("pb0",))
                op("S", "activation", tk("scrbf")[:, :], pb[0][:, 0:128], AF.Square,
                   accum_out=nr[:, 2:3], reads=("pb0",), writes=(kk("scrbf"), nrk))
                op("V", "tensor_scalar", nr[:, 0:3], nr[:, 0:3], 1e-6, None, OP.add,
                   reads=(nrk,), writes=(nrk,))
                newton_rsqrt(nr, sc_, tt_, nrk, sck, ttk, 3)
                op("V", "scalar_tensor_tensor", tk("q_tok")[:, :], pm[0][:, 0:128],
                   sc_[:, 0:1], qn_bc[:, :], OP.mult, OP.mult,
                   reads=("pm0", sck, "qn_bc"), writes=(kk("q_tok"),))
                op("V", "scalar_tensor_tensor", tk("k_tok")[:, :], pm[1][:, 0:128],
                   sc_[:, 1:2], kn_bc[:, :], OP.mult, OP.mult,
                   reads=("pm1", sck, "kn_bc"), writes=(kk("k_tok"),))
                op("V", "tensor_scalar", tk("w_tok")[:, :], pb[0][:, 0:128],
                   sc_[:, 2:3], None, OP.mult, reads=("pb0", sck), writes=(kk("w_tok"),))
                for c in range(ND):
                    mm(pm[0][:, 0:128], xnt[p][:, c * 128:(c + 1) * 128],
                       AT_W[("wv", hp)][:, c * 128:(c + 1) * 128],
                       reads=(xk, f"wv{hp}"), writes=("pm0",),
                       start=(c == 0), stop=(c == ND - 1))
                if vpr:
                    op("V", "tensor_copy", vpre[:, li * 128:(li + 1) * 128], pm[0][:, 0:128],
                       reads=("pm0",), writes=("vpre",))
                else:
                    op("V", "tensor_copy", vt[:, ci * 132: ci * 132 + 128], pm[0][:, 0:128],
                       reads=("pm0",), writes=("vt",))
                op("V", "tensor_scalar", tk("bw_tok")[:, :], tk("w_tok")[:, :],
                   btP[hg][:, ci:ci + 1], None, OP.mult,
                   reads=(kk("w_tok"), f"btC{hg}"), writes=(kk("bw_tok"),))
                tr(pb[0][:, 0:128], tk("q_tok")[:, :], ident_bf[:, :],
                   reads=(kk("q_tok"),), writes=("pb0",))
                op("V", "tensor_copy", tk("qT")[:, :], pb[0][:, 0:128],
                   reads=("pb0",), writes=(kk("qT"),))
                tr(pb[1][:, 0:128], tk("k_tok")[:, :], ident_bf[:, :],
                   reads=(kk("k_tok"),), writes=("pb1",))
                op("V", "tensor_copy", tk("kT")[:, :], pb[1][:, 0:128],
                   reads=("pb1",), writes=(kk("kT"),))
                tr(pb[0][:, 0:128], tk("w_tok")[:, :], ident_bf[:, :],
                   reads=(kk("w_tok"),), writes=("pb0",))
                op("V", "tensor_copy", tk("wT")[:, :], pb[0][:, 0:128],
                   reads=("pb0",), writes=(kk("wT"),))
                mm(pm[0][:, 0:128], tk("wT")[:, :], tk("wT")[:, :],
                   reads=(kk("wT"),), writes=("pm0",))
                op("V", "scalar_tensor_tensor", tk("A1")[:, :], pm[0][:, 0:128],
                   bnP[hg][:, ci:ci + 1], maskSL[:, :], OP.mult, OP.mult,
                   reads=("pm0", f"bnC{hg}", "maskSL"), writes=(kk("A1"),))
                mm(pm[1][:, 0:128], tk("wT")[:, :], tk("kT")[:, :],
                   reads=(kk("wT"), kk("kT")), writes=("pm1",))
                op("V", "scalar_tensor_tensor", tk("R1")[:, :], pm[1][:, 0:128],
                   btP[hg][:, ci:ci + 1], maskSL[:, :], OP.mult, OP.mult,
                   reads=("pm1", f"btC{hg}", "maskSL"), writes=(kk("R1"),))
                tr(pb[0][:, 0:128], tk("A1")[:, :], ident_bf[:, :],
                   reads=(kk("A1"),), writes=("pb0",))
                op("V", "tensor_copy", tk("A1T")[:, :], pb[0][:, 0:128],
                   reads=("pb0",), writes=(kk("A1T"),))
                op("V", "tensor_tensor", tk("PTa")[:, :], pb[0][:, 0:128], ident_bf[:, :],
                   OP.add, reads=("pb0", "ident"), writes=(kk("PTa"),))
                curA, curAT = "A1", "A1T"
                pt_cur, pt_nxt = "PTa", "PTb"
                lvl = 2
                while lvl <= C // 2:
                    nA = "Ak2" if curA != "Ak2" else "Ak"
                    nAT = "Ak2T" if curA != "Ak2" else "AkT"
                    mm(pm[0][:, 0:128], tk(curAT)[:, :], tk(curA)[:, :],
                       reads=(kk(curAT), kk(curA)), writes=("pm0",))
                    op("V", "tensor_copy", tk(nA)[:, :], pm[0][:, 0:128],
                       reads=("pm0",), writes=(kk(nA),))
                    if lvl < C // 2:
                        mm(pm[1][:, 0:128], tk(curA)[:, :], tk(curAT)[:, :],
                           reads=(kk(curA), kk(curAT)), writes=("pm1",))
                        op("V", "tensor_copy", tk(nAT)[:, :], pm[1][:, 0:128],
                           reads=("pm1",), writes=(kk(nAT),))
                    mm(pm[0][:, 0:128], tk(nA)[:, :], tk(pt_cur)[:, :],
                       reads=(kk(nA), kk(pt_cur)), writes=("pm0",))
                    op("V", "tensor_tensor", tk(pt_nxt)[:, :], pm[0][:, 0:128],
                       tk(pt_cur)[:, :], OP.add,
                       reads=("pm0", kk(pt_cur)), writes=(kk(pt_nxt),))
                    curA, curAT = nA, nAT
                    pt_cur, pt_nxt = pt_nxt, pt_cur
                    lvl *= 2
                TinvT, TinvTk = tk(pt_cur), kk(pt_cur)
                mm(pm[0][:, 0:128], TinvT[:, :], tk("bw_tok")[:, :],
                   reads=(TinvTk, kk("bw_tok")), writes=("pm0",))
                op("V", "tensor_copy", tk("Wbar")[:, :], pm[0][:, 0:128],
                   reads=("pm0",), writes=(kk("Wbar"),))
                mm(pm[1][:, 0:128], TinvT[:, :], tk("R1")[:, :],
                   reads=(TinvTk, kk("R1")), writes=("pm1",))
                op("V", "tensor_copy", tk("Xc")[:, :], pm[1][:, 0:128],
                   reads=("pm1",), writes=(kk("Xc"),))
                mm(pm[0][:, 0:128], tk("wT")[:, :], tk("qT")[:, :],
                   reads=(kk("wT"), kk("qT")), writes=("pm0",))
                op("V", "scalar_tensor_tensor", tk("WQm")[:, :], pm[0][:, 0:128],
                   -1.0, triuK[:, :], OP.mult, OP.mult,
                   reads=("pm0", "triuK"), writes=(kk("WQm"),))
                mm(pm[1][:, 0:128], tk("Wbar")[:, :], tk("WQm")[:, :],
                   reads=(kk("Wbar"), kk("WQm")), writes=("pm1",))
                op("V", "scalar_tensor_tensor", t8[("QtilT", slot)][:, :], pm[1][:, 0:128],
                   1.0, tk("qT")[:, :], OP.mult, OP.add,
                   reads=("pm1", kk("qT")), writes=(t8k("QtilT"),))
                mm(pm[0][:, 0:128], tk("w_tok")[:, :], tk("Xc")[:, :],
                   reads=(kk("w_tok"), kk("Xc")), writes=("pm0",))
                op("V", "scalar_tensor_tensor", t8[("KtilT", slot)][:, :], pm[0][:, 0:128],
                   -1.0, tk("kT")[:, :], OP.mult, OP.add,
                   reads=("pm0", kk("kT")), writes=(t8k("KtilT"),))
                mm(pm[1][:, 0:128], tk("Wbar")[:, :], tk("w_tok")[:, :],
                   reads=(kk("Wbar"), kk("w_tok")), writes=("pm1",))
                op("V", "scalar_tensor_tensor", t8[("PhiT", slot)][:, :], pm[1][:, 0:128],
                   -1.0, ident_bf[:, :], OP.mult, OP.add,
                   reads=("pm1", "ident"), writes=(t8k("PhiT"),))
                if li < SS - 1 or True:
                    mm(pm[0][:, 0:128], tk("w_tok")[:, :], tk("Wbar")[:, :],
                       reads=(kk("w_tok"), kk("Wbar")), writes=("pm0",))
                    op("V", "scalar_tensor_tensor", t8[("Phi", slot)][:, :], pm[0][:, 0:128],
                       -1.0, ident_bf[:, :], OP.mult, OP.add,
                       reads=("pm0", "ident"), writes=(t8k("Phi"),))
                mm(pm[1][:, 0:128], tk("kT")[:, :], tk("qT")[:, :],
                   reads=(kk("kT"), kk("qT")), writes=("pm1",), start=True, stop=False)
                mm(pm[1][:, 0:128], tk("Xc")[:, :], tk("WQm")[:, :],
                   reads=(kk("Xc"), kk("WQm")), writes=("pm1",), start=False, stop=True)
                op("V", "scalar_tensor_tensor", tmpD[p][:, :], pm[1][:, 0:128],
                   scale, gbc[sp][:, li * 128:(li + 1) * 128], OP.mult, OP.add,
                   reads=("pm1", f"gbc{sp}"), writes=(f"tmpD{p}",))
                op("V", "tensor_tensor", tmpD[p][:, :], tmpD[p][:, :], maskneg[:, :],
                   OP.add, reads=(f"tmpD{p}", "maskneg"), writes=(f"tmpD{p}",))
                op("S", "activation", t8[("expD", slot)][:, :], tmpD[p][:, :], AF.Exp,
                   bias=GnP[hg][:, ci:ci + 1],
                   reads=(f"tmpD{p}", f"GnC{hg}"), writes=(t8k("expD"),))

            def stage_B(bh, s):
                hg = bh % 2
                sp = s % 2
                slx = lambda li: li + SS * sp
                t8s = lambda nm, li: t8[(nm, slx(li))]
                t8sk = lambda nm, li: f"{nm}s{slx(li)}"
                qh, qhk = qhat[sp], f"qhat{sp}"
                op("V", "tensor_copy", qh[:, 0:128], t8s("QtilT", 0)[:, :],
                   reads=(t8sk("QtilT", 0),), writes=(qhk,))
                mm(pm[2][:, 0:128], t8s("Phi", 0)[:, :], t8s("QtilT", 1)[:, :],
                   reads=(t8sk("Phi", 0), t8sk("QtilT", 1)), writes=("pm2",))
                op("V", "tensor_copy", qh[:, 128:256], pm[2][:, 0:128],
                   reads=("pm2",), writes=(qhk,))
                mm(pm[3][:, 0:128], t8s("Phi", 1)[:, :], t8s("QtilT", 2)[:, :],
                   reads=(t8sk("Phi", 1), t8sk("QtilT", 2)), writes=("pm3",))
                op("V", "tensor_copy", qch[:, :], pm[3][:, 0:128],
                   reads=("pm3",), writes=("qch",))
                mm(pm[2][:, 0:128], t8s("Phi", 0)[:, :], qch[:, :],
                   reads=(t8sk("Phi", 0), "qch"), writes=("pm2",))
                op("V", "tensor_copy", qh[:, 256:384], pm[2][:, 0:128],
                   reads=("pm2",), writes=(qhk,))
                mm(pm[3][:, 0:128], t8s("Phi", 2)[:, :], t8s("QtilT", 3)[:, :],
                   reads=(t8sk("Phi", 2), t8sk("QtilT", 3)), writes=("pm3",))
                op("V", "tensor_copy", qch[:, :], pm[3][:, 0:128],
                   reads=("pm3",), writes=("qch",))
                mm(pm[2][:, 0:128], t8s("Phi", 1)[:, :], qch[:, :],
                   reads=(t8sk("Phi", 1), "qch"), writes=("pm2",))
                op("V", "tensor_copy", qch2[:, :], pm[2][:, 0:128],
                   reads=("pm2",), writes=("qch2",))
                mm(pm[3][:, 0:128], t8s("Phi", 0)[:, :], qch2[:, :],
                   reads=(t8sk("Phi", 0), "qch2"), writes=("pm3",))
                op("V", "tensor_copy", qh[:, 384:512], pm[3][:, 0:128],
                   reads=("pm3",), writes=(qhk,))

                bank_started = [False, False]

                def po_region(li):
                    return po2[li // 2][:, (li % 2) * 144: (li % 2) * 144 + 129], f"po2{li // 2}"

                def po_flags(li, last=False):
                    bk = li // 2
                    st = not bank_started[bk]
                    bank_started[bk] = True
                    return st, last

                def ammS(sb):
                    pmx, pmxk = (pm[2], "pm2") if sb % 2 == 0 else (pm[3], "pm3")
                    mm(pmx[:, 0:512], khat[:, sb * 128:(sb + 1) * 128], qh[:, :],
                       reads=("khat", qhk), writes=(pmxk,))
                    op("V", "scalar_tensor_tensor", tmpS[sb % 2][:, :], pmx[:, 0:512],
                       scale, gbc[sp][:, :], OP.mult, OP.add,
                       reads=(pmxk, f"gbc{sp}"), writes=("tmpS0",))
                    op("S", "activation", expS[sb % 3][:, :], tmpS[sb % 2][:, :], AF.Exp,
                       bias=GnP[hg][:, sb:sb + 1],
                       reads=("tmpS0", f"GnC{hg}"), writes=(f"expS{sb % 3}",))

                def pvS(sb):
                    for li in range(SS):
                        reg, rk = po_region(li)
                        st, _ = po_flags(li)
                        mm(reg, expS[sb % 3][:, li * 128:(li + 1) * 128],
                           vt[:, sb * 132: sb * 132 + 129],
                           reads=(f"expS{sb % 3}", "vt"), writes=(rk,),
                           start=st, stop=False, skip_group_check=True)

                npre = SS * s
                if npre > 0:
                    ammS(0)
                    if npre > 1:
                        ammS(1)
                    for sb in range(2, npre):
                        ammS(sb)
                        pvS(sb - 2)
                    for sb in range(max(0, npre - 2), npre):
                        pvS(sb)

                for li in range(SS):
                    ci = SS * s + li
                    for lj in range(li):
                        pmx, pmxk = (pm[2], "pm2") if lj % 2 == 0 else (pm[3], "pm3")
                        mm(pmx[:, 0:128], khat_loc[:, lj * 128:(lj + 1) * 128],
                           t8s("QtilT", li)[:, :],
                           reads=("khat_loc", t8sk("QtilT", li)), writes=(pmxk,))
                        op("V", "scalar_tensor_tensor", tmpI[lj % 2][:, :], pmx[:, 0:128],
                           scale, gbc[sp][:, li * 128:(li + 1) * 128], OP.mult, OP.add,
                           reads=(pmxk, f"gbc{sp}"), writes=(f"tmpI{lj % 2}",))
                        op("S", "activation", expI[lj % 2][:, :], tmpI[lj % 2][:, :],
                           AF.Exp, bias=GnP[hg][:, SS * s + lj: SS * s + lj + 1],
                           reads=(f"tmpI{lj % 2}", f"GnC{hg}"), writes=(f"expI{lj % 2}",))
                        reg, rk = po_region(li)
                        st, _ = po_flags(li)
                        mm(reg, expI[lj % 2][:, :],
                           vt[:, (SS * s + lj) * 132: (SS * s + lj) * 132 + 129],
                           reads=(f"expI{lj % 2}", "vt"), writes=(rk,),
                           start=st, stop=False, skip_group_check=True)
                    reg, rk = po_region(li)
                    st, _ = po_flags(li)
                    mm(reg, t8s("expD", li)[:, :], vt[:, ci * 132: ci * 132 + 129],
                       reads=(t8sk("expD", li), "vt"), writes=(rk,),
                       start=st, stop=(li % 2 == 1), skip_group_check=True)
                    if li > 0:
                        mm(pm[2][:, 0:li * 128], t8s("PhiT", li)[:, :],
                           khat_loc[:, 0:li * 128],
                           reads=(t8sk("PhiT", li), "khat_loc"), writes=("pm2",))
                        op("V", "tensor_copy", khat_loc[:, 0:li * 128],
                           pm[2][:, 0:li * 128], reads=("pm2",), writes=("khat_loc",))
                    if li < SS - 1:
                        op("V", "tensor_copy", khat_loc[:, li * 128:(li + 1) * 128],
                           t8s("KtilT", li)[:, :],
                           reads=(t8sk("KtilT", li),), writes=("khat_loc",))
                for li in range(SS):
                    ci = SS * s + li
                    reg, rk = po_region(li)
                    op("V", "reciprocal", orc[:, :], reg[:, 128:129],
                       reads=(rk,), writes=("orc",))
                    op("V", "tensor_scalar",
                       a2[:, ci * HD + h * 128: ci * HD + (h + 1) * 128],
                       reg[:, 0:128], orc[:, :], None, OP.mult,
                       reads=(rk, "orc"), writes=("a2",))
                if s < NSUP - 1:
                    mm(pm[2][:, 0:128], t8s("Phi", 2)[:, :], t8s("PhiT", 3)[:, :],
                       reads=(t8sk("Phi", 2), t8sk("PhiT", 3)), writes=("pm2",))
                    op("V", "tensor_copy", psT[:, :], pm[2][:, 0:128],
                       reads=("pm2",), writes=("psT",))
                    mm(pm[3][:, 0:128], t8s("Phi", 1)[:, :], psT[:, :],
                       reads=(t8sk("Phi", 1), "psT"), writes=("pm3",))
                    op("V", "tensor_copy", psT2[:, :], pm[3][:, 0:128],
                       reads=("pm3",), writes=("psT2",))
                    mm(pm[2][:, 0:128], t8s("Phi", 0)[:, :], psT2[:, :],
                       reads=(t8sk("Phi", 0), "psT2"), writes=("pm2",))
                    op("V", "tensor_copy", psT[:, :], pm[2][:, 0:128],
                       reads=("pm2",), writes=("psT",))
                    for tt in range(s):
                        pmx, pmxk = (pm[3], "pm3") if tt % 2 == 0 else (pm[2], "pm2")
                        mm(pmx[:, 0:512], psT[:, :], khat[:, tt * 512:(tt + 1) * 512],
                           reads=("psT", "khat"), writes=(pmxk,))
                        op("V", "tensor_copy", khat[:, tt * 512:(tt + 1) * 512],
                           pmx[:, 0:512], reads=(pmxk,), writes=("khat",))
                    op("V", "tensor_copy", khat[:, s * 512: s * 512 + 384],
                       khat_loc[:, :], reads=("khat_loc",), writes=("khat",))
                    op("V", "tensor_copy", khat[:, s * 512 + 384: (s + 1) * 512],
                       t8s("KtilT", 3)[:, :], reads=(t8sk("KtilT", 3),), writes=("khat",))

            def vpre_restore():
                for cci in range(SS):
                    op("V", "tensor_copy", vt[:, cci * 132: cci * 132 + 128],
                       vpre[:, cci * 128:(cci + 1) * 128],
                       reads=("vpre",), writes=("vt",))

            if h == 0:
                for f_ in collect(lambda: [stage_A(h, c) for c in range(SS)]):
                    f_()
            for s in range(NSUP):
                if s == 0 and h > 0:
                    lb = collect(lambda: (vpre_restore(), stage_B(h, 0)))
                else:
                    lb = collect(lambda s=s: stage_B(h, s))
                if s + 1 < NSUP:
                    la = collect(lambda s=s: [stage_A(h, SS * (s + 1) + l) for l in range(SS)])
                elif h + 1 < NH:
                    la = collect(lambda h2=h + 1: (head_prefetch(h2, pm[0], "pm0"),
                                                   [stage_A(h2, c, vpr=True)
                                                    for c in range(SS)]))
                elif mode == "fused" and not SKIP_MAIN:
                    for k in ("xin2", "xin3", "stg_tok1"):
                        sch.merge_deps(k, ["wcb"])
                    sch.merge_deps("hw_all", ["stripe"])
                    la = collect(lambda: [ln_early(t) for t in range(N_EARLY)])
                else:
                    la = []
                interleave_emit(lb, la)

        for k in FFN_KEYS:
            sch.merge_deps(k, AT_KEYS + ["stripe"])
        if not SKIP_MAIN and mode == "fused":
            dma_w("P", rt_sb[:, :], routerT.ap(), ND, reads=(), writes=("rt_sb",))
            for i in range(NHS):
                dma_w("P", eWg[0][i], sgT.ap()[:, i * 128:(i + 1) * 128],
                      ND, reads=(), writes=("eWg0",))
                dma_w("P", eWu[0][i], suT.ap()[:, i * 128:(i + 1) * 128],
                      ND, reads=(), writes=("eWu0",))
                dma("P", eWd[0][i], sdT.ap()[i * 128:(i + 1) * 128, :],
                    reads=(), writes=("eWd0",))

        if mode == "fused":
            for ci in range(NCH):
                dma("Q", o_gath.ap()[ci * 128:(ci + 1) * 128, :],
                    a2[:, ci * HD:(ci + 1) * HD], reads=("a2",), writes=("o_gath",))
            def rdb_fn(EE):
                return EE.remote_dma_broadcast(
                    ffn_acc[:, :], a2[:, :], sems["rsem"], sems["lsem"],
                    rdests=[(0, 1)] + [None] * 7)
            sch.emit("P", rdb_fn, reads=("a2",), writes=("rdbring",), inc=1)
            op("P", "trigger_dma", reads=("rdbring",), writes=("rdbtrig",))
            nop1 = op("P", "engine_nop", reads=("a2", "rdbtrig"), writes=())
            nop1.pre_wait = ("lsem", 16)
            nop2 = op("P", "engine_nop", reads=("rdbtrig",), writes=("ffn_acc",))
            nop2.pre_wait = ("rsem", 2)
            for ci in range(NCH):
                dma("Q", o_gath.ap()[T + ci * 128: T + (ci + 1) * 128, :],
                    ffn_acc[:, ci * HD:(ci + 1) * HD],
                    reads=("ffn_acc",), writes=("o_gath",))
        elif mode == "p1":
            for ci in range(NCH):
                dma("Q", o_bounce.ap()[ci * 128:(ci + 1) * 128, :],
                    a2[:, ci * HD:(ci + 1) * HD], reads=("a2",), writes=("o_bounce",))
        cc_op = None

        if not SKIP_MAIN:
            if mode != "fused":
                dma_w("P", rt_sb[:, :], routerT.ap(), ND, reads=(), writes=("rt_sb",))
        for k in ("xin2", "xin3", "stg_tok1"):
            sch.merge_deps(k, ["wcb"])
        sch.merge_deps("stg_tok0", ["vt"])
        ti0 = 0
        if mode == "fused" and not SKIP_MAIN:
            ti0 = N_EARLY
            for ti in range(N_EARLY):
                for c in range(ND):
                    op("V", "tensor_copy",
                       a2[:, c * OWN + ti * 128: c * OWN + (ti + 1) * 128],
                       hw_all[:, ti * D + c * 128: ti * D + (c + 1) * 128],
                       reads=("hw_all",), writes=("a2",))
        for ti in range(ti0, NOT if not SKIP_MAIN else 0):
            ln_tile(x_own, ti, False)

        for ti in range(NOT if not SKIP_MAIN else 0):
            for c in range(ND):
                mm(pm[0][:, 0:E], a2[:, c * OWN + ti * 128: c * OWN + (ti + 1) * 128],
                   rt_sb[:, c * E:(c + 1) * E],
                   reads=("a2", "rt_sb"), writes=("pm0",),
                   start=(c == 0), stop=(c == ND - 1))
            op("V", "tensor_tensor", rtf[0][:, 0:E], pm[0][:, 0:E], rb_bc[:, :],
               OP.add, reads=("pm0", "rb_bc"), writes=("rtf0",))
            op("V", "reduce_max", rts[:, 0:1], rtf[0][:, 0:E], AX.X,
               negate=True, reads=("rtf0",), writes=("rts",))
            op("S", "activation", rtf[1][:, 0:E], rtf[0][:, 0:E], AF.Exp,
               bias=rts[:, 0:1], reads=("rtf0", "rts"), writes=("rtf1",))
            op("V", "tensor_copy", rtf[2][:, 0:E], rtf[1][:, 0:E],
               reads=("rtf1",), writes=("rtf2",))
            for _ in range(3):
                op("V", "reduce_max", rts[:, 1:2], rtf[2][:, 0:E], AX.X,
                   reads=("rtf2",), writes=("rts",))
                op("V", "tensor_scalar", rtf[0][:, 0:E], rtf[2][:, 0:E],
                   rts[:, 1:2], None, OP.is_ge, reads=("rtf2", "rts"), writes=("rtf0",))
                op("V", "scalar_tensor_tensor", rtf[2][:, 0:E], rtf[0][:, 0:E],
                   -1e30, rtf[2][:, 0:E], OP.mult, OP.add,
                   reads=("rtf0", "rtf2"), writes=("rtf2",))
            op("V", "reduce_max", rts[:, 1:2], rtf[2][:, 0:E], AX.X,
               reads=("rtf2",), writes=("rts",))
            op("V", "tensor_scalar", rtf[0][:, 0:E], rtf[1][:, 0:E],
               rts[:, 1:2], None, OP.is_ge, reads=("rtf1", "rts"), writes=("rtf0",))
            op("V", "tensor_tensor", rtf[1][:, 0:E], rtf[1][:, 0:E], rtf[0][:, 0:E],
               OP.mult, reads=("rtf1", "rtf0"), writes=("rtf1",))
            op("V", "reduce_sum", rts[:, 2:3], rtf[1][:, 0:E], AX.X,
               reads=("rtf1",), writes=("rts",))
            op("V", "reciprocal", rts[:, 2:3], rts[:, 2:3], reads=("rts",), writes=("rts",))
            op("V", "tensor_scalar", w4buf[:, :], rtf[1][:, 0:E],
               rts[:, 2:3], None, OP.mult, reads=("rtf1", "rts"), writes=("w4buf",))
            tr(pm[2][0:E, 0:128], w4buf[:, :], ident_f[:, :],
               reads=("w4buf",), writes=("pm2",))
            op("V", "tensor_copy", w4T[0:E, ti * 128:(ti + 1) * 128], pm[2][0:E, 0:128],
               reads=("pm2",), writes=("w4T",))
        if not SKIP_MAIN:
            dma("Q", w4dram.ap(), w4T[0:E, :], reads=("w4T",), writes=("w4dram",))

        blocks = ([("sh", NHS)] + [(bb, EBLK) for bb in range(NBLK)]) if not SKIP_MAIN else []
        for bi, (tag, nE) in enumerate(blocks):
            bp = bi % 2
            for i in range(nE):
                if tag == "sh":
                    if mode == "fused":
                        continue
                    dma_w("P", eWg[bp][i], sgT.ap()[:, i * 128:(i + 1) * 128],
                          ND, reads=(), writes=(f"eWg{bp}",))
                    dma_w("P", eWu[bp][i], suT.ap()[:, i * 128:(i + 1) * 128],
                          ND, reads=(), writes=("eWu0",))
                    dma("P", eWd[bp][i], sdT.ap()[i * 128:(i + 1) * 128, :],
                        reads=(), writes=("eWd0",))
                else:
                    e = tag * EBLK + i
                    dma_w("P", eWg[bp][i], gateT.ap()[e], ND, reads=(), writes=(f"eWg{bp}",))
                    dma_w("P", eWu[bp][i], upT.ap()[e], ND, reads=(), writes=("eWu0",))
                    dma("P", eWd[bp][i], downT.ap()[e], reads=(), writes=("eWd0",))
            for i in range(nE):
                rb_ = i % 2
                if tag != "sh":
                    e = tag * EBLK + i
                    dma("Q", w4row[rb_][0:1, :], w4dram.ap()[e:e + 1, :],
                        reads=("w4dram",), writes=(f"w4row{rb_}",))
                for tg in range(NGR):
                    for c in range(ND):
                        mm(pm[0][:, 0:512], eWg[bp][i][:, c * HME: c * HME + 128],
                           a2[:, c * OWN + tg * 512: c * OWN + (tg + 1) * 512],
                           reads=(f"eWg{bp}", "a2"), writes=("pm0",),
                           start=(c == 0), stop=(c == ND - 1))
                    for c in range(ND):
                        mm(pm[1][:, 0:512], eWu[bp][i][:, c * HME: c * HME + 128],
                           a2[:, c * OWN + tg * 512: c * OWN + (tg + 1) * 512],
                           reads=("eWu0", "a2"), writes=("pm1",),
                           start=(c == 0), stop=(c == ND - 1))
                    op("S", "activation", lnscr[:, 0:512], pm[0][:, 0:512], AF.Silu,
                       reads=("pm0",), writes=("lnscr",))
                    op("V", "scalar_tensor_tensor",
                       hw_all[:, i * OWN + tg * 512: i * OWN + (tg + 1) * 512],
                       pm[1][:, 0:512], 1.0, lnscr[:, 0:512], OP.mult, OP.mult,
                       reads=("pm1", "lnscr"), writes=("hw_all",))
                    if tag != "sh":
                        mm(pm[3][:, 0:512], ones1b[:, :],
                           w4row[rb_][0:1, tg * 512:(tg + 1) * 512],
                           reads=("ones1", f"w4row{rb_}"), writes=("pm3",))
                        op("V", "tensor_tensor",
                           hw_all[:, i * OWN + tg * 512: i * OWN + (tg + 1) * 512],
                           hw_all[:, i * OWN + tg * 512: i * OWN + (tg + 1) * 512],
                           pm[3][:, 0:512], OP.mult,
                           reads=("hw_all", "pm3"), writes=("hw_all",))
            for tt in range(NOT):
                for half in range(NHALF):
                    pma, pk = (pm[2], "pm2") if half % 2 == 0 else (pm[3], "pm3")
                    for i in range(nE):
                        mm(pma[:, 0:HSZ],
                           hw_all[:, i * OWN + tt * 128: i * OWN + (tt + 1) * 128],
                           eWd[bp][i][:, half * HSZ:(half + 1) * HSZ],
                           reads=("hw_all", "eWd0"), writes=(pk,),
                           start=(i == 0), stop=(i == nE - 1))
                    dst = ffn_acc[:, tt * D + half * HSZ: tt * D + (half + 1) * HSZ]
                    if bi == 0:
                        op("V", "tensor_copy", dst, pma[:, 0:HSZ],
                           reads=(pk,), writes=("ffn_acc",))
                    else:
                        op("V", "tensor_tensor", dst, pma[:, 0:HSZ], dst, OP.add,
                           reads=(pk, "ffn_acc"), writes=("ffn_acc",))

        for tt in range(NOT if not SKIP_MAIN else 0):
            for half in range(NHALF):
                sl_ = ffn_acc[:, tt * D + half * HSZ: tt * D + (half + 1) * HSZ]
                op("S", "activation", sl_, sl_, AF.Silu,
                   reads=("ffn_acc",), writes=("ffn_acc",))

        if mode == "p1":
            for ti in range(NOT):
                dma("Q", g_out.ap()[ti * 128:(ti + 1) * 128, :],
                    ffn_acc[:, ti * D:(ti + 1) * D], reads=("ffn_acc",), writes=("g_out",))

        for k in FIN_KEYS:
            sch.merge_deps(k, FFN_KEYS)
        if mode != "p1":
            dma_w("P", wo_sb, Wo_in.ap(), ND, reads=(), writes=("wo_sb",))
        if cc_op is not None:
            sch.barrier_wait("P", cc_op)
        for ti in range(NOT if mode != "p1" else 0):
            b = ti % 2
            ogk = f"og{b}"
            if mode == "p2":
                dma("Q", og[b][:, :], o_own_in.ap()[ti * 128:(ti + 1) * 128, :],
                    reads=(), writes=(ogk,))
            else:
                op("P", "indirect_dma_start", og[b][:, 0:HD], None, o_gath.ap(),
                   bass.IndirectOffsetOnAxis(ap=idx_sb[:, 2 * ti:2 * ti + 1], axis=0),
                   reads=("o_gath", "idx_sb"), writes=(ogk,), inc=16)
                op("P", "indirect_dma_start", og[b][:, HD:2 * HD], None, o_gath.ap(),
                   bass.IndirectOffsetOnAxis(ap=idx_sb[:, 2 * ti + 1:2 * ti + 2], axis=0),
                   reads=("o_gath", "idx_sb"), writes=(ogk,), inc=16)
            dma("Q", xo[b][:, :], x_own.ap()[ti * 128:(ti + 1) * 128, :],
                reads=(), writes=(f"xo{b}",))
            for c in range(ND):
                pbk = c % 2
                tr(pb[pbk][:, 0:128], og[b][:, c * 128:(c + 1) * 128], ident_bf[:, :],
                   reads=(ogk,), writes=(f"pb{pbk}",))
                op("V", "tensor_copy", oTt[b][:, c * 128:(c + 1) * 128], pb[pbk][:, 0:128],
                   reads=(f"pb{pbk}",), writes=(f"oTt{b}",))
            if mode == "p2" and ti < 2:
                pass
            if mode == "p2":
                dma("Q", ffn_acc[:, ti * D:(ti + 1) * D],
                    g_in.ap()[ti * 128:(ti + 1) * 128, :],
                    reads=(), writes=("ffn_acc",))
            for half in range(NHALF):
                pma, pk = pm[half % 2], f"pm{half % 2}"
                for c in range(ND):
                    mm(pma[:, 0:HSZ], oTt[b][:, c * 128:(c + 1) * 128],
                       wo_sb[:, c * D + half * HSZ: c * D + (half + 1) * HSZ],
                       reads=(f"oTt{b}", "wo_sb"), writes=(pk,),
                       start=(c == 0), stop=(c == ND - 1))
                op("V", "tensor_tensor", sfin[:, half * HSZ:(half + 1) * HSZ],
                   pma[:, 0:HSZ],
                   ffn_acc[:, ti * D + half * HSZ: ti * D + (half + 1) * HSZ],
                   OP.mult, reads=(pk, "ffn_acc"), writes=("sfin",))
            op("V", "tensor_tensor", outt[b][:, :], sfin[:, 0:D], xo[b][:, :],
               OP.add, reads=("sfin", f"xo{b}"), writes=(f"outt{b}",))
            dma("Q", out_d.ap()[ti * 128:(ti + 1) * 128, :], outt[b][:, :],
                reads=(f"outt{b}",), writes=("out_d",))

        for l in range(NLANES):
            slot = ("Q", l)
            k = sch.lane_cnt.get(slot, 0)
            if k:
                def wfn(EE, _s=slot, _v=16 * k):
                    return None
                wop = Op("P", None, 0, len(sch.q["P"]))
                wop.pre_wait = (slot, 16 * k)
                sch.q["P"].append(wop)

        sch.finalize()

        def runner(eng_obj, ename):
            for o in sch.q[ename]:
                if o.pre_wait is not None:
                    eng_obj.wait_ge(sems[o.pre_wait[0]], o.pre_wait[1])
                for pop in o.waits:
                    eng_obj.wait_ge(sems[pop.slot], pop.semval)
                if o.fn is not None:
                    ins = o.fn(eng_obj)
                    if o.marked:
                        ins.then_inc(sems[o.slot], o.inc)

        if mode == "fused" and not globals().get("_STUB_BARRIER_OFF", True):
            nc.bir_kernel_barrier_wait(replica_groups, engines=[nc.gpsimd])
        with nc.Block() as block:
            @block.tensor
            def _(e):
                runner(e, "T")

            @block.vector
            def _(e):
                runner(e, "V")

            @block.scalar
            def _(e):
                runner(e, "S")

            @block.gpsimd
            def _(e):
                runner(e, "P")

            @block.sync
            def _(e):
                runner(e, "Q")

    nc.finalize()
    return nc


def make_inmaps(inputs, cfg):
    T, D, NH, d = cfg["T"], cfg["D"], cfg["NH"], cfg["d"]
    E, KC = cfg["E"], cfg["KC"]
    n_cores = cfg["n_cores"]
    OWN = T // 2
    NOT = OWN // 128
    HD = NH * d
    x = np.asarray(inputs["L0_attn_x"], np.float32)
    maps = []
    for c in range(n_cores):
        b, j = c // 2, c % 2
        cols = slice(j * HD, (j + 1) * HD)
        hs = slice(j * NH, (j + 1) * NH)
        idx = np.zeros((128, 3 * NOT), np.uint32)
        for ti in range(NOT):
            base = j * OWN + ti * 128
            idx[:, 2 * ti] = (j * T) + base + np.arange(128)
            idx[:, 2 * ti + 1] = ((1 - j) * T) + base + np.arange(128)
            idx[:, 2 * NOT + ti] = (j * NOT + ti) * 128 + np.arange(128)
        m = {
            "x_b": x[b],
            "x_own": x[b, j * OWN:(j + 1) * OWN],
            "Wqh": np.asarray(inputs["L0_attn_Wq"])[:, cols],
            "Wkh": np.asarray(inputs["L0_attn_Wk"])[:, cols],
            "Wvh": np.asarray(inputs["L0_attn_Wv"])[:, cols],
            "Wwh": np.asarray(inputs["L0_attn_Ww"])[:, cols],
            "Wg4": np.asarray(inputs["L0_attn_Wg"])[:, hs],
            "Wb4": np.asarray(inputs["L0_attn_Wbeta"])[:, hs],
            "conv4": np.asarray(inputs["L0_attn_conv_w"])[cols, :],
            "qn": np.asarray(inputs["L0_attn_qn_w"]).reshape(1, d),
            "kn": np.asarray(inputs["L0_attn_kn_w"]).reshape(1, d),
            "Wo": np.asarray(inputs["L0_attn_Wo"]),
            "routerT": np.asarray(inputs["L0_ffn_router_w"], np.float32).T,
            "rbias": np.asarray(inputs["L0_ffn_router_bias"]).reshape(1, E),
            "gateT": np.asarray(inputs["L0_ffn_gate_w"], np.float32).transpose(0, 2, 1),
            "upT": np.asarray(inputs["L0_ffn_up_w"], np.float32).transpose(0, 2, 1),
            "downT": np.asarray(inputs["L0_ffn_down_w"], np.float32).transpose(0, 2, 1),
            "sgT": np.asarray(inputs["L0_ffn_sgate_w"], np.float32).T,
            "suT": np.asarray(inputs["L0_ffn_sup_w"], np.float32).T,
            "sdT": np.asarray(inputs["L0_ffn_sdown_w"], np.float32).T,
            "gidx": idx,
        }
        m = {k: np.ascontiguousarray(np.asarray(
            v, dtype=(np.uint32 if k == "gidx" else np.float32))) for k, v in m.items()}
        maps.append(m)
    return maps


KERNEL_MODE = "fused"


def p2_inmaps(in_maps, res1, cfg):
    OWN = cfg["T"] // 2
    nc_n = cfg["n_cores"]
    o_list = [np.asarray(res1[c]["o_bounce"]) for c in range(nc_n)]
    maps2 = []
    for c in range(nc_n):
        b, j = c // 2, c % 2
        o_own = np.concatenate(
            [o_list[2 * b][j * OWN:(j + 1) * OWN, :],
             o_list[2 * b + 1][j * OWN:(j + 1) * OWN, :]], axis=1)
        m = dict(in_maps[c])
        m["o_own"] = np.ascontiguousarray(o_own)
        m["g_in"] = np.ascontiguousarray(np.asarray(res1[c]["g_out"]))
        maps2.append(m)
    return maps2


def kernel(**inputs):
    cfg = full_cfg()
    T, D = cfg["T"], cfg["D"]
    OWN = T // 2
    in_maps = make_inmaps(inputs, cfg)
    cores = list(range(cfg["n_cores"]))
    if KERNEL_MODE == "fused":
        nc = build_nc(cfg, "fused")
        res = run_bass_kernel_spmd(nc, in_maps, core_ids=cores)
        final = res.results
    else:
        nc1 = build_nc(cfg, "p1")
        res1 = run_bass_kernel_spmd(nc1, in_maps, core_ids=cores)
        maps2 = p2_inmaps(in_maps, res1.results, cfg)
        nc2 = build_nc(cfg, "p2")
        res2 = run_bass_kernel_spmd(nc2, maps2, core_ids=cores)
        final = res2.results
    B = np.asarray(inputs["L0_attn_x"]).shape[0]
    out = np.zeros((B, T, D), np.float32)
    for c in range(cfg["n_cores"]):
        b, j = c // 2, c % 2
        out[b, j * OWN:(j + 1) * OWN] = final[c]["out"]
    return out


# version 42
# speedup vs baseline: 1.2972x; 1.2972x over previous
import numpy as np
import concourse.bass as bass
import concourse.bacc as bacc
import concourse.mybir as mybir
from concourse.bass_utils import run_bass_kernel_spmd
from contextlib import ExitStack

F32 = mybir.dt.float32
BF16 = mybir.dt.bfloat16
U32 = mybir.dt.uint32
I32 = mybir.dt.int32
AF = mybir.ActivationFunctionType
OP = mybir.AluOpType
AX = mybir.AxisListType

ENGS = ("T", "V", "S", "P", "Q")


def full_cfg():
    return dict(T=4096, D=1024, NH=4, d=128, C=128, KC=4, E=32, EBLK=4,
                HME=128, HS=512, n_cores=8)


class Op:
    __slots__ = ("eng", "fn", "inc", "waits", "marked", "semval", "idx",
                 "slot", "lane_k", "pre_wait")

    def __init__(self, eng, fn, inc, idx):
        self.eng, self.fn, self.inc, self.idx = eng, fn, inc, idx
        self.waits = []
        self.marked = inc != 1
        self.semval = None
        self.slot = eng
        self.lane_k = None
        self.pre_wait = None


NLANES = 6


class Sched:
    def __init__(self):
        self.q = {e: [] for e in ENGS}
        self.tiles = {}
        self.waited = {e: {} for e in ENGS}
        self.lane_rr = {"Q": 0, "P": 0}
        self.lane_cnt = {}

    def _t(self, name):
        if name not in self.tiles:
            self.tiles[name] = {"w": {}, "r": {}}
        return self.tiles[name]

    def emit(self, eng, fn, reads=(), writes=(), inc=1, lane=None):
        op = Op(eng, fn, inc, len(self.q[eng]))
        if inc != 1:
            if lane is None:
                lane = self.lane_rr[eng]
                self.lane_rr[eng] = (lane + 1) % NLANES
            slot = (eng, lane)
            k = self.lane_cnt.get(slot, 0) + 1
            self.lane_cnt[slot] = k
            op.slot = slot
            op.lane_k = k
            if k > 1:
                op.pre_wait = (slot, 16 * (k - 1))
        need = {}

        def consider(p):
            if p.eng == eng and eng == "T":
                return
            cur = need.get(p.slot)
            if cur is None or p.idx > cur.idx:
                need[p.slot] = p

        for b in reads:
            for p in self._t(b)["w"].values():
                consider(p)
        for b in writes:
            t = self._t(b)
            for p in t["w"].values():
                consider(p)
            for p in t["r"].values():
                consider(p)
        wd = self.waited[eng]
        for ps, p in need.items():
            if p.idx > wd.get(ps, -1):
                p.marked = True
                op.waits.append(p)
                wd[ps] = p.idx
        self.q[eng].append(op)
        for b in writes:
            t = self._t(b)
            t["w"][op.slot] = op
            t["r"] = {}
        for b in reads:
            self._t(b)["r"][op.slot] = op
        return op

    def merge_deps(self, new_key, old_keys):
        t = self._t(new_key)
        for ok in old_keys:
            o = self._t(ok)
            for s, p in o["w"].items():
                if s not in t["w"] or t["w"][s].idx < p.idx:
                    t["w"][s] = p
            for s, p in o["r"].items():
                if s not in t["r"] or t["r"][s].idx < p.idx:
                    t["r"][s] = p

    def barrier_wait(self, eng, producer_op):
        op = Op(eng, None, 0, len(self.q[eng]))
        producer_op.marked = True
        op.waits.append(producer_op)
        self.waited[eng][producer_op.slot] = max(
            self.waited[eng].get(producer_op.slot, -1), producer_op.idx)
        self.q[eng].append(op)
        return op

    def finalize(self):
        for e in ENGS:
            cum = 0
            for op in self.q[e]:
                if op.lane_k is not None:
                    op.semval = 16 * op.lane_k
                else:
                    if op.marked:
                        cum += op.inc
                    op.semval = cum


def build_nc(cfg, mode="fused"):
    T, D, NH, d, C = cfg["T"], cfg["D"], cfg["NH"], cfg["d"], cfg["C"]
    KC, E, EBLK, HME, HS = cfg["KC"], cfg["E"], cfg["EBLK"], cfg["HME"], cfg["HS"]
    ND = D // 128
    NCH = T // C
    OWN = T // 2
    NOT = OWN // 128
    NHS = HS // 128
    NGR = OWN // 512
    NTI = T // 128
    scale = float(d) ** -0.5
    HD = NH * d
    n_cores = cfg["n_cores"]
    NBLK = E // EBLK
    NEB = max(EBLK, NHS)
    NHALF = max(1, D // 512)
    HSZ = D // NHALF

    nc = bacc.Bacc("TRN2", target_bir_lowering=False)

    dI = lambda n, s, dt=F32: nc.dram_tensor(n, s, dt, kind="ExternalInput")
    x_b = dI("x_b", [T, D])
    x_own = dI("x_own", [OWN, D])
    Wqh = dI("Wqh", [D, HD]); Wkh = dI("Wkh", [D, HD])
    Wvh = dI("Wvh", [D, HD]); Wwh = dI("Wwh", [D, HD])
    Wg4 = dI("Wg4", [D, NH]); Wb4 = dI("Wb4", [D, NH])
    conv4 = dI("conv4", [HD, KC])
    qn_in = dI("qn", [1, d]); kn_in = dI("kn", [1, d])
    Wo_in = dI("Wo", [D, D])
    routerT = dI("routerT", [D, E])
    rbias = dI("rbias", [1, E])
    gateT = dI("gateT", [E, D, HME])
    upT = dI("upT", [E, D, HME])
    downT = dI("downT", [E, HME, D])
    sgT = dI("sgT", [D, HS]); suT = dI("suT", [D, HS])
    sdT = dI("sdT", [HS, D])
    gidx = dI("gidx", [128, 3 * NOT], U32)
    out_d = nc.dram_tensor("out", [OWN, D], F32, kind="ExternalOutput")

    xnT_dram = nc.dram_tensor("xnT_dram", [NTI, 128, ND * 128], BF16)
    o_bounce = nc.dram_tensor("o_bounce", [T, HD], BF16,
                              kind="ExternalOutput" if mode == "p1" else "Internal")
    o_gath = nc.dram_tensor("o_gath", [2 * T, HD], BF16)
    gdram = nc.dram_tensor("gdram", [4, NH, T], F32)
    w4dram = nc.dram_tensor("w4dram", [E, OWN], BF16)
    if mode == "p1":
        g_out = nc.dram_tensor("g_out", [OWN, D], BF16, kind="ExternalOutput")
    if mode == "p2":
        o_own_in = nc.dram_tensor("o_own", [OWN, D], BF16, kind="ExternalInput")
        g_in = nc.dram_tensor("g_in", [OWN, D], BF16, kind="ExternalInput")

    replica_groups = [[2 * i, 2 * i + 1] for i in range(n_cores // 2)]

    sch = Sched()

    with ExitStack() as st:
        ec = st.enter_context
        sems = {e: ec(nc.semaphore(f"sem_{e}")) for e in ENGS}
        for qe in ("Q", "P"):
            for l in range(NLANES):
                sems[(qe, l)] = ec(nc.semaphore(f"sem_{qe}{l}"))
        sems[("P", "cc")] = ec(nc.semaphore("sem_cc"))
        sems["rsem"] = ec(nc.semaphore("rsem"))
        sems["lsem"] = ec(nc.semaphore("lsem"))

        def sbuf(name, shape, dt):
            return ec(nc.sbuf_tensor(name, shape, dt))

        a2 = sbuf("a2", [128, NCH * HD], BF16)
        ffn_acc = sbuf("ffn_acc", [128, NOT * D], BF16)
        vt = sbuf("vt", [128, NCH * 132], BF16)
        khat = sbuf("khat", [128, T], BF16)
        wcb = sbuf("wcb", [128, T], BF16)
        wpre = sbuf("wpre", [128, 131], F32)
        cvs = sbuf("cvs", [128, 256], F32)
        xnt = [sbuf(f"xnt{i}", [128, ND * 128], BF16) for i in range(2)]
        xin = [sbuf(f"xin{i}", [128, D], BF16) for i in range(2)]
        names128 = ["q_tok", "k_tok", "qT", "kT", "w_tok",
                    "bw_tok", "wT", "A1", "A1T", "R1", "Ak", "AkT", "Ak2", "Ak2T",
                    "PTa", "PTb", "Wbar", "Xc", "WQm", "scrbf"]
        t128 = {}
        for nm in names128:
            for p in range(2):
                t128[(nm, p)] = sbuf(f"{nm}{p}", [128, 128], BF16)
        names8 = ["QtilT", "KtilT", "PhiT", "Phi", "expD"]
        t8 = {}
        for nm in names8:
            for sl in range(8):
                t8[(nm, sl)] = sbuf(f"{nm}s{sl}", [128, 128], BF16)
        expS = [sbuf(f"expS{i}", [128, 512], BF16) for i in range(3)]
        tmpS1 = sbuf("tmpS0", [128, 512], F32)
        tmpS = [tmpS1, tmpS1]
        expI = [sbuf(f"expI{i}", [128, 128], BF16) for i in range(2)]
        tmpI = [sbuf(f"tmpI{i}", [128, 128], F32) for i in range(2)]
        tmpD = [sbuf(f"tmpD{i}", [128, 128], F32) for i in range(2)]
        qch = sbuf("qch", [128, 128], BF16)
        qch2 = sbuf("qch2", [128, 128], BF16)
        qhat = [sbuf(f"qhat{i}", [128, 512], BF16) for i in range(2)]
        khat_loc = sbuf("khat_loc", [128, 384], BF16)
        psT = sbuf("psT", [128, 128], BF16)
        psT2 = sbuf("psT2", [128, 128], BF16)
        gbc = [sbuf(f"gbc{i}", [128, 512], F32) for i in range(2)]
        vpre = sbuf("vpre", [128, 512], BF16)
        nrm = [sbuf(f"nrm{i}", [128, 4], F32) for i in range(2)]
        nsc = [sbuf(f"nsc{i}", [128, 4], F32) for i in range(2)]
        nt2 = [sbuf(f"nt2{i}", [128, 4], F32) for i in range(2)]
        gsg = [sbuf(f"gsg{i}", [1, 128], F32) for i in range(2)]
        ident_bf = sbuf("ident_bf", [128, 128], BF16)
        ident_f = sbuf("ident_f", [128, 128], F32)
        maskSL = sbuf("maskSL", [128, 128], BF16)
        triuK = sbuf("triuK", [128, 128], BF16)
        maskneg = sbuf("maskneg", [128, 128], F32)
        ones1 = sbuf("ones1", [1, 128], F32)
        ones1b = sbuf("ones1b", [1, 128], BF16)
        qn_row = sbuf("qn_row", [1, d], F32)
        kn_row = sbuf("kn_row", [1, d], F32)
        qn_bc = sbuf("qn_bc", [128, d], F32)
        kn_bc = sbuf("kn_bc", [128, d], F32)
        rb_row = sbuf("rb_row", [1, E], F32)
        rb_bc = sbuf("rb_bc", [128, E], F32)
        bnst4 = sbuf("bnst4", [128, 48], F32)
        bnag4 = sbuf("bnag4", [128, 8], F32)
        lnn = [sbuf(f"lnn{i}", [128, 1], F32) for i in range(4)]
        lnsc = [sbuf(f"lnsc{i}", [128, 1], F32) for i in range(4)]
        lnt = [sbuf(f"lnt{i}", [128, 1], F32) for i in range(4)]
        lnscr = sbuf("lnscr", [128, 512], BF16)
        qst = sbuf("qst", [128, 2], F32)
        kst = sbuf("kst", [128, 2], F32)
        wst = sbuf("wst", [128, 2], F32)
        orc = sbuf("orc", [128, 1], F32)
        rts = sbuf("rts", [128, 4], F32)
        GnP = [sbuf(f"GnC{i}", [128, NCH], F32) for i in range(2)]
        btP = [sbuf(f"btC{i}", [128, NCH], F32) for i in range(2)]
        bnP = [sbuf(f"bnC{i}", [128, NCH], F32) for i in range(2)]
        gseg = sbuf("gseg", [NCH, 128], F32)
        wg_sb = sbuf("wg_sb", [128, ND * NH], BF16)
        wb_sb = sbuf("wb_sb", [128, ND * NH], BF16)
        cw_sb = sbuf("cw_sb", [128, NH * KC], F32)
        idx_sb = sbuf("idx_sb", [128, 3 * NOT], U32)
        w4buf = sbuf("w4buf", [128, E], F32)
        rtf = [sbuf(f"rtf{i}", [128, E], F32) for i in range(3)]
        rt_sb = sbuf("rt_sb", [128, ND * E], BF16)

        AR_BYTES = 62 * 1024
        arena = sbuf("arena", [128, AR_BYTES], mybir.dt.uint8)
        ar_f32 = arena.ap().bitcast(F32)
        ar_bf = arena.ap().bitcast(BF16)

        def av(off, cols, dt):
            if dt == F32:
                assert off % 4 == 0
                return ar_f32[:, off // 4: off // 4 + cols]
            assert off % 2 == 0
            return ar_bf[:, off // 2: off // 2 + cols]

        AT_W = {}
        offa = 0
        for nm in ("wq", "wk", "wv", "ww"):
            for p in range(2):
                AT_W[(nm, p)] = av(offa, ND * 128, BF16)
                offa += ND * 128 * 2
        AT_KEYS = [f"{nm}{p}" for nm in ("wq", "wk", "wv", "ww") for p in range(2)]
        stripe = av(0, 3 * T, F32)
        assert 3 * T * 4 <= AR_BYTES
        assert offa <= AR_BYTES

        offf = 0

        def fav(cols, dt):
            nonlocal offf
            r = av(offf, cols, dt)
            offf += cols * (4 if dt == F32 else 2)
            return r

        eWg = [[fav(ND * HME, BF16) for _ in range(NEB)] for _ in range(2)]
        eWu1 = [fav(ND * HME, BF16) for _ in range(NEB)]
        eWu = [eWu1, eWu1]
        eWd1 = [fav(D, BF16) for _ in range(NEB)]
        eWd = [eWd1, eWd1]
        hw_all = fav(NEB * OWN, BF16)
        w4T = fav(OWN, BF16)
        w4row = [fav(OWN, BF16) for _ in range(2)]
        FFN_KEYS = ["eWg0", "eWg1", "eWu0", "eWd0",
                    "hw_all", "w4T", "w4row0", "w4row1"]
        assert offf <= AR_BYTES, offf

        offz = 0

        def zav(cols, dt):
            nonlocal offz
            r = av(offz, cols, dt)
            offz += cols * (4 if dt == F32 else 2)
            return r

        wo_sb = zav(ND * D, BF16)
        og = [zav(D, BF16) for _ in range(2)]
        oTt = [zav(ND * 128, BF16) for _ in range(2)]
        xo = [zav(D, F32) for _ in range(2)]
        sfin = zav(D, F32)
        outt = [zav(D, F32) for _ in range(2)]
        FIN_KEYS = ["wo_sb", "og0", "og1", "oTt0", "oTt1", "xo0", "xo1",
                    "sfin", "outt0", "outt1"]
        assert offz <= AR_BYTES, offz

        pm = [ec(nc.psum_tensor(f"pm{i}", [128, 512], F32)) for i in range(4)]
        pb = [ec(nc.psum_tensor(f"pb{i}", [128, 1024], BF16)) for i in range(2)]
        po2 = [ec(nc.psum_tensor(f"po2{i}", [128, 288], F32)) for i in range(2)]

        cvals = (0.0, 1.0, 1e-5, 1e-6)
        cap = sbuf("caps", [128, len(cvals)], F32)
        for i_, v_ in enumerate(cvals):
            nc.const_aps.register(F32, v_, cap[:, i_:i_ + 1])

        COLLECT = [None]

        def op(eng, meth, *args, reads=(), writes=(), inc=1, lane=None, **kw):
            if meth == "activation":
                reads = tuple(reads) + ("consts",)
            def fn(EE, _m=meth, _a=args, _k=kw, _r=reads, _w=writes):
                try:
                    return getattr(EE, _m)(*_a, **_k)
                except Exception:
                    import sys
                    shapes = [getattr(x, "shape", None) for x in _a]
                    print(f"FAILING OP: {_m} args={shapes} kw={list(_k)} "
                          f"reads={_r} writes={_w}", file=sys.stderr)
                    raise
            if COLLECT[0] is not None:
                COLLECT[0].append(
                    lambda _f=fn, _r=reads, _w=writes, _i=inc, _l=lane, _e=eng:
                    sch.emit(_e, _f, _r, _w, _i, lane=_l))
                return None
            return sch.emit(eng, fn, reads, writes, inc, lane=lane)

        def collect(builder):
            lst = []
            COLLECT[0] = lst
            builder()
            COLLECT[0] = None
            return lst

        PIPELINE = True

        def newton_rsqrt(nr, sc, tt_, key, skey, tkey, n):
            x = nr[:, 0:n]
            xi = nr.bitcast(I32)[:, 0:n]
            yi = sc.bitcast(I32)[:, 0:n]
            y = sc[:, 0:n]
            t = tt_[:, 0:n]
            op("V", "tensor_scalar", yi, xi, 1, None, OP.arith_shift_right,
               reads=(key,), writes=(skey,))
            op("V", "tensor_scalar", yi, yi, float(0x5F3759DF), -1.0,
               OP.subtract, op1=OP.mult, reads=(skey,), writes=(skey,))
            for _ in range(3):
                op("V", "tensor_tensor", t, y, y, OP.mult,
                   reads=(skey,), writes=(tkey,))
                op("V", "tensor_tensor", t, t, x, OP.mult,
                   reads=(tkey, key), writes=(tkey,))
                op("V", "tensor_scalar", t, t, -0.5, 1.5, OP.mult, op1=OP.add,
                   reads=(tkey,), writes=(tkey,))
                op("V", "tensor_tensor", y, y, t, OP.mult,
                   reads=(skey, tkey), writes=(skey,))

        def interleave_emit(lb, la):
            if not PIPELINE:
                for f_ in lb:
                    f_()
                for f_ in la:
                    f_()
                return
            nb, na = len(lb), len(la)
            ib = ia = 0
            while ib < nb or ia < na:
                if ia >= na or (ib < nb and ib * (na + 1) <= ia * (nb + 1)):
                    lb[ib](); ib += 1
                else:
                    la[ia](); ia += 1

        def mm(ps, lhsT, rhs, reads, writes, start=True, stop=True,
               skip_group_check=False):
            return op("T", "matmul", ps, lhsT, rhs, start=start, stop=stop,
                      skip_group_check=skip_group_check,
                      reads=reads, writes=writes)

        def tr(ps, in_ap, ident_ap, reads, writes):
            return op("T", "transpose", ps, in_ap, ident_ap,
                      reads=tuple(reads) + ("ident",), writes=writes)

        def dma(q, out_ap, in_ap, reads, writes):
            return op(q, "dma_start", reads=reads, writes=writes, inc=16,
                      out=out_ap, in_=in_ap)

        def dma_w(q, out_ap, in_2d, nch, reads, writes, slow=False):
            o3 = out_ap.rearrange("p (c n) -> p c n", c=nch)
            i3 = in_2d.rearrange("(c p) n -> p c n", p=128)
            kw = {"allow_slow_non_contiguous": True} if slow else {}
            return op(q, "dma_start", reads=reads, writes=writes, inc=16,
                      out=o3, in_=i3, **kw)

        def mask_init(tile, key, setval, cmpop, fill, cm=1, pat=-1):
            op("P", "memset", tile[:, :], setval, writes=(key,))
            op("P", "affine_select", tile[:, :], tile[:, :], [[pat, 128]],
               cmpop, fill, base=0, channel_multiplier=cm,
               reads=(key,), writes=(key,))

        mask_init(ident_bf, "ident", 0.0, OP.not_equal, 1.0)
        mask_init(ident_f, "ident", 0.0, OP.not_equal, 1.0)
        mask_init(maskSL, "maskSL", 1.0, OP.is_gt, 0.0)
        mask_init(triuK, "triuK", 1.0, OP.is_ge, 0.0, cm=-1, pat=1)
        mask_init(maskneg, "maskneg", 0.0, OP.is_ge, -1e30, cm=-1, pat=1)
        op("P", "memset", ones1[:, :], 1.0, writes=("ones1",))
        op("P", "memset", ones1b[:, :], 1.0, writes=("ones1",))
        for i_, v_ in enumerate(cvals):
            op("P", "memset", cap[:, i_:i_ + 1], v_, writes=("consts",))


        dma("Q", qn_row[:, :], qn_in.ap(), reads=(), writes=("qn_row",))
        dma("Q", kn_row[:, :], kn_in.ap(), reads=(), writes=("kn_row",))
        dma("Q", rb_row[:, :], rbias.ap(), reads=(), writes=("rb_row",))
        dma("Q", idx_sb[:, :], gidx.ap(), reads=(), writes=("idx_sb",))
        mm(pm[0][:, 0:d], ones1[:, 0:128], qn_row[:, :], reads=("ones1", "qn_row"), writes=("pm0",))
        op("V", "tensor_copy", qn_bc[:, :], pm[0][:, 0:d], reads=("pm0",), writes=("qn_bc",))
        mm(pm[0][:, 0:d], ones1[:, 0:128], kn_row[:, :], reads=("ones1", "kn_row"), writes=("pm0",))
        op("V", "tensor_copy", kn_bc[:, :], pm[0][:, 0:d], reads=("pm0",), writes=("kn_bc",))
        mm(pm[0][:, 0:E], ones1[:, 0:128], rb_row[:, :], reads=("ones1", "rb_row"), writes=("pm0",))
        op("V", "tensor_copy", rb_bc[:, :], pm[0][:, 0:E], reads=("pm0",), writes=("rb_bc",))

        dma_w("P", wg_sb[:, :], Wg4.ap(), ND, reads=(), writes=("wg_sb",), slow=True)
        dma_w("P", wb_sb[:, :], Wb4.ap(), ND, reads=(), writes=("wb_sb",), slow=True)
        for h in range(NH):
            dma("P", cw_sb[:, h * KC:(h + 1) * KC], conv4.ap()[h * 128:(h + 1) * 128, :],
                reads=(), writes=("cw_sb",))

        SKIP_MAIN = (mode == "p2")
        def s_zg(c0, c1):
            return stripe[0:NH, c0:c1]

        def s_zb(c0, c1):
            return stripe[0:NH, T + c0:T + c1]

        def s_sc(c0, c1):
            return stripe[0:NH, 2 * T + c0:2 * T + c1]

        wcb_bf = wcb[:, :]
        khat_bf = khat[:, :]
        xinX = [xin[0][:, :], xin[1][:, :], wcb_bf[:, 0:D], wcb_bf[:, D:2 * D]]
        stgX = [vt[:, 0:D], wcb_bf[:, 2 * D:3 * D]]
        xntX = [xnt[0][:, :], xnt[1][:, :], khat_bf[:, 0:D], khat_bf[:, D:2 * D]]
        trbank = [(pb[0], "pb0"), (pb[1], "pb1")]

        def ln_tile(src_dram, ti, full):
            b = ti % 4
            b2 = ti % 2
            xk = f"xin{b}"
            sk = f"stg_tok{b2}"
            dma("P", xinX[b], src_dram.ap()[ti * 128:(ti + 1) * 128, :],
                reads=(), writes=(xk,))
            nbst = (D + 511) // 512
            bns, bnsk = bnst4[:, b * 12:(b + 1) * 12], f"bnst{b}"
            bna, bnak = bnag4[:, b * 2:(b + 1) * 2], f"bnag{b}"
            for j in range(nbst):
                c0, c1 = j * 512, min(D, (j + 1) * 512)
                op("V", "bn_stats", bns[:, j * 6:(j + 1) * 6], xinX[b][:, c0:c1],
                   reads=(xk,), writes=(bnsk,))
            op("V", "bn_aggr", bna[:, :], bns[:, 0:6 * nbst],
               reads=(bnsk,), writes=(bnak,))
            op("V", "tensor_scalar", lnn[b][:, :], bna[:, 1:2], 1e-5, None, OP.add,
               reads=(bnak,), writes=(f"lnn{b}",))
            newton_rsqrt(lnn[b], lnsc[b], lnt[b], f"lnn{b}", f"lnsc{b}", f"lnt{b}", 1)
            op("V", "tensor_scalar", stgX[b2][:, 0:D], xinX[b], bna[:, 0:1],
               lnsc[b][:, 0:1], OP.subtract, op1=OP.mult,
               reads=(xk, bnak, f"lnsc{b}"), writes=(sk,))
            for c in range(ND):
                pbt, pbk = trbank[c % 2]
                tr(pbt[:, 0:128], stgX[b2][:, c * 128:(c + 1) * 128], ident_bf[:, :],
                   reads=(sk,), writes=(pbk,))
                if full:
                    op("V", "tensor_copy", xntX[b][:, c * 128:(c + 1) * 128],
                       pbt[:, 0:128], reads=(pbk,), writes=(f"xnt{b}",))
                else:
                    op("V", "tensor_copy", a2[:, c * OWN + ti * 128: c * OWN + (ti + 1) * 128],
                       pbt[:, 0:128], reads=(pbk,), writes=("a2",))
            if full:
                dma("Q", xnT_dram.ap()[ti], xntX[b], reads=(f"xnt{b}",),
                    writes=("xnT_dram",))
                for c in range(ND):
                    mm(pm[1][0:NH, 0:128], wg_sb[:, c * NH:(c + 1) * NH],
                       xntX[b][:, c * 128:(c + 1) * 128],
                       reads=("wg_sb", f"xnt{b}"), writes=("pm1",),
                       start=(c == 0), stop=(c == ND - 1))
                op("V", "tensor_copy", s_zg(ti * 128, (ti + 1) * 128), pm[1][0:NH, 0:128],
                   reads=("pm1",), writes=("stripe",))
                for c in range(ND):
                    mm(pm[3][0:NH, 0:128], wb_sb[:, c * NH:(c + 1) * NH],
                       xntX[b][:, c * 128:(c + 1) * 128],
                       reads=("wb_sb", f"xnt{b}"), writes=("pm3",),
                       start=(c == 0), stop=(c == ND - 1))
                op("V", "tensor_copy", s_zb(ti * 128, (ti + 1) * 128), pm[3][0:NH, 0:128],
                   reads=("pm3",), writes=("stripe",))

        N_EARLY = min(NOT, (NEB * OWN) // D)

        def ln_early(ti):
            b = 2 + ti % 2
            xk = f"xin{b}"
            dma("P", xinX[b], x_own.ap()[ti * 128:(ti + 1) * 128, :],
                reads=(), writes=(xk,))
            bns, bnsk = bnst4[:, b * 12:(b + 1) * 12], f"bnst{b}"
            bna, bnak = bnag4[:, b * 2:(b + 1) * 2], f"bnag{b}"
            nbst = (D + 511) // 512
            for j in range(nbst):
                c0, c1 = j * 512, min(D, (j + 1) * 512)
                op("V", "bn_stats", bns[:, j * 6:(j + 1) * 6], xinX[b][:, c0:c1],
                   reads=(xk,), writes=(bnsk,))
            op("V", "bn_aggr", bna[:, :], bns[:, 0:6 * nbst],
               reads=(bnsk,), writes=(bnak,))
            op("V", "tensor_scalar", lnn[b][:, :], bna[:, 1:2], 1e-5, None, OP.add,
               reads=(bnak,), writes=(f"lnn{b}",))
            newton_rsqrt(lnn[b], lnsc[b], lnt[b], f"lnn{b}", f"lnsc{b}", f"lnt{b}", 1)
            op("V", "tensor_scalar", stgX[1][:, 0:D], xinX[b], bna[:, 0:1],
               lnsc[b][:, 0:1], OP.subtract, op1=OP.mult,
               reads=(xk, bnak, f"lnsc{b}"), writes=("stg_tok1",))
            for c in range(ND):
                pbt, pbk = trbank[c % 2]
                tr(pbt[:, 0:128], stgX[1][:, c * 128:(c + 1) * 128], ident_bf[:, :],
                   reads=("stg_tok1",), writes=(pbk,))
                op("V", "tensor_copy",
                   hw_all[:, ti * D + c * 128: ti * D + (c + 1) * 128],
                   pbt[:, 0:128], reads=(pbk,), writes=("hw_all",))

        for ti in range(NTI if not SKIP_MAIN else 0):
            ln_tile(x_b, ti, True)

        zg, zb, sc = s_zg(0, T), s_zb(0, T), s_sc(0, T)
        if not SKIP_MAIN:
            op("S", "activation", sc, zg, AF.Abs, reads=("stripe",), writes=("stripe",))
            op("S", "activation", sc, sc, AF.Exp, scale=-1.0, reads=("stripe",), writes=("stripe",))
            op("S", "activation", sc, sc, AF.Ln, bias=1.0, reads=("stripe",), writes=("stripe",))
            op("V", "tensor_scalar", zg, zg, 0.0, None, OP.min, reads=("stripe",), writes=("stripe",))
            op("V", "tensor_tensor", zg, zg, sc, OP.subtract, reads=("stripe",), writes=("stripe",))
            op("V", "tensor_tensor_scan", sc, zg, zg, 0.0, OP.add, OP.bypass,
               reads=("stripe",), writes=("stripe",))
            dma("Q", gdram.ap()[0], sc, reads=("stripe",), writes=("gdram",))
            op("V", "tensor_scalar", zg, sc, -1.0, None, OP.mult, reads=("stripe",), writes=("stripe",))
            dma("Q", gdram.ap()[1], zg, reads=("stripe",), writes=("gdram",))
            dma("Q", gdram.ap()[2], zb, reads=("stripe",), writes=("gdram",))

        for k in AT_KEYS:
            sch.merge_deps(k, ["stripe"])
        sch.merge_deps("wcb", ["xin2", "xin3", "stg_tok1"])
        sch.merge_deps("khat", ["xnt2", "xnt3"])
        sch.merge_deps("vt", ["stg_tok0"])
        for ci in range(NCH):
            op("P", "memset", vt[:, ci * 132 + 128: ci * 132 + 129], 1.0, writes=("vt",))

        def head_weights(h):
            hp = h % 2
            for nm, src in (("wq", Wqh), ("wk", Wkh), ("wv", Wvh), ("ww", Wwh)):
                dma_w("P", AT_W[(nm, hp)], src.ap()[:, h * 128:(h + 1) * 128],
                      ND, reads=(), writes=(f"{nm}{hp}",))

        def head_gcols(h, pmx, pmxk):
            hg = h % 2
            for gi, dstc, dck in ((1, GnP[hg], f"GnC{hg}"), (2, bnP[hg], f"bnC{hg}")):
                dma("P", gseg[:, :],
                    gdram.ap()[gi, h].rearrange("(c t) -> c t", c=NCH),
                    reads=("gdram",), writes=("gseg",))
                tr(pmx[0:128, 0:NCH], gseg[:, :], ident_f[0:NCH, 0:NCH],
                   reads=("gseg",), writes=(pmxk,))
                op("V", "tensor_copy", dstc[:, :], pmx[0:128, 0:NCH],
                   reads=(pmxk,), writes=(dck,))
            bnc, btc = bnP[hg], btP[hg]
            bnk, btk = f"bnC{hg}", f"btC{hg}"
            op("S", "activation", bnc[:, :], bnc[:, :], AF.Exp, scale=-1.0,
               reads=(bnk,), writes=(bnk,))
            op("V", "tensor_scalar", bnc[:, :], bnc[:, :], 1.0, None, OP.add,
               reads=(bnk,), writes=(bnk,))
            op("V", "reciprocal", bnc[:, :], bnc[:, :], reads=(bnk,), writes=(bnk,))
            op("V", "tensor_scalar", btc[:, :], bnc[:, :], 2.0, None, OP.mult,
               reads=(bnk,), writes=(btk,))
            op("V", "tensor_scalar", bnc[:, :], bnc[:, :], -2.0, None, OP.mult,
               reads=(bnk,), writes=(bnk,))

        def pass1_convs(h):
            hp = h % 2
            for ci in range(NCH):
                b = ci % 2
                xk = f"xnt{b}"
                pmx, pmxk = (pm[0], "pm0") if b == 0 else (pm[1], "pm1")
                dma("Q", xnt[b][:, :], xnT_dram.ap()[ci], reads=("xnT_dram",), writes=(xk,))
                for c in range(ND):
                    mm(pmx[:, 0:128], AT_W[("ww", hp)][:, c * 128:(c + 1) * 128],
                       xnt[b][:, c * 128:(c + 1) * 128],
                       reads=(f"ww{hp}", xk), writes=(pmxk,),
                       start=(c == 0), stop=(c == ND - 1))
                if ci == 0:
                    op("V", "memset", wpre[:, 0:3], 0.0, writes=("wpre",))
                else:
                    op("V", "tensor_copy", wpre[:, 0:3], wpre[:, 128:131],
                       reads=("wpre",), writes=("wpre",))
                op("V", "tensor_copy", wpre[:, 3:131], pmx[:, 0:128],
                   reads=(pmxk,), writes=("wpre",))
                ca0, ca1 = cvs[:, 0:128], cvs[:, 128:256]
                op("V", "tensor_scalar", ca0, wpre[:, 0:128],
                   cw_sb[:, h * KC: h * KC + 1], None, OP.mult,
                   reads=("wpre", "cw_sb"), writes=("cvs",))
                for i in (1, 2):
                    src_, dst_ = (ca0, ca1) if i % 2 == 1 else (ca1, ca0)
                    op("V", "scalar_tensor_tensor", dst_, wpre[:, i:i + 128],
                       cw_sb[:, h * KC + i: h * KC + i + 1], src_, OP.mult, OP.add,
                       reads=("wpre", "cw_sb", "cvs"), writes=("cvs",))
                op("V", "scalar_tensor_tensor", wcb[:, ci * 128:(ci + 1) * 128],
                   wpre[:, 3:131], cw_sb[:, h * KC + 3: h * KC + 4], ca0,
                   OP.mult, OP.add,
                   reads=("wpre", "cw_sb", "cvs"), writes=("wcb",))

        def pass1_silus(h):
            for ci in range(NCH):
                op("S", "activation", wcb[:, ci * 128:(ci + 1) * 128],
                   wcb[:, ci * 128:(ci + 1) * 128], AF.Silu,
                   reads=("wcb",), writes=("wcb",))

        def head_prefetch(h, pmx, pmxk):
            head_weights(h)
            head_gcols(h, pmx, pmxk)
            pass1_convs(h)
            pass1_silus(h)

        for h in range(NH if not SKIP_MAIN else 0):
            hp = h % 2
            hg = h % 2
            if h == 0:
                head_prefetch(0, pm[2], "pm2")

            SS = 4
            NSUP = NCH // SS

            def stage_A(ah, ci, vpr=False):
                hp = ah % 2
                hg = ah % 2
                p = ci % 2
                sp = (ci // SS) % 2
                li = ci % SS
                slot = li + SS * sp
                tk = lambda nm: t128[(nm, p)]
                kk = lambda nm: f"{nm}{p}"
                t8k = lambda nm: f"{nm}s{slot}"
                xk = f"xnt{p}"
                dma("Q", xnt[p][:, :], xnT_dram.ap()[ci], reads=("xnT_dram",), writes=(xk,))
                dma("Q", gsg[p][:, :], gdram.ap()[0, ah:ah + 1, ci * 128:(ci + 1) * 128],
                    reads=("gdram",), writes=(f"gsg{p}",))
                mm(pm[1][:, 0:128], ones1[:, 0:128], gsg[p][:, :],
                   reads=("ones1", f"gsg{p}"), writes=("pm1",))
                op("V", "tensor_copy", gbc[sp][:, li * 128:(li + 1) * 128], pm[1][:, 0:128],
                   reads=("pm1",), writes=(f"gbc{sp}",))
                for c in range(ND):
                    mm(pm[0][:, 0:128], xnt[p][:, c * 128:(c + 1) * 128],
                       AT_W[("wq", hp)][:, c * 128:(c + 1) * 128],
                       reads=(xk, f"wq{hp}"), writes=("pm0",),
                       start=(c == 0), stop=(c == ND - 1))
                nr, sc_, tt_ = nrm[p], nsc[p], nt2[p]
                nrk, sck, ttk = f"nrm{p}", f"nsc{p}", f"nt2{p}"
                op("S", "activation", tk("scrbf")[:, :], pm[0][:, 0:128], AF.Square,
                   scale=float(d) ** -0.5, accum_out=nr[:, 0:1],
                   reads=("pm0",), writes=(kk("scrbf"), nrk))
                for c in range(ND):
                    mm(pm[1][:, 0:128], xnt[p][:, c * 128:(c + 1) * 128],
                       AT_W[("wk", hp)][:, c * 128:(c + 1) * 128],
                       reads=(xk, f"wk{hp}"), writes=("pm1",),
                       start=(c == 0), stop=(c == ND - 1))
                op("S", "activation", tk("scrbf")[:, :], pm[1][:, 0:128], AF.Square,
                   scale=float(d) ** -0.5, accum_out=nr[:, 1:2],
                   reads=("pm1",), writes=(kk("scrbf"), nrk))
                tr(pb[0][:, 0:128], wcb[:, ci * 128:(ci + 1) * 128], ident_bf[:, :],
                   reads=("wcb",), writes=("pb0",))
                op("S", "activation", tk("scrbf")[:, :], pb[0][:, 0:128], AF.Square,
                   accum_out=nr[:, 2:3], reads=("pb0",), writes=(kk("scrbf"), nrk))
                op("V", "tensor_scalar", nr[:, 0:3], nr[:, 0:3], 1e-6, None, OP.add,
                   reads=(nrk,), writes=(nrk,))
                newton_rsqrt(nr, sc_, tt_, nrk, sck, ttk, 3)
                op("V", "scalar_tensor_tensor", tk("q_tok")[:, :], pm[0][:, 0:128],
                   sc_[:, 0:1], qn_bc[:, :], OP.mult, OP.mult,
                   reads=("pm0", sck, "qn_bc"), writes=(kk("q_tok"),))
                op("V", "scalar_tensor_tensor", tk("k_tok")[:, :], pm[1][:, 0:128],
                   sc_[:, 1:2], kn_bc[:, :], OP.mult, OP.mult,
                   reads=("pm1", sck, "kn_bc"), writes=(kk("k_tok"),))
                op("V", "tensor_scalar", tk("w_tok")[:, :], pb[0][:, 0:128],
                   sc_[:, 2:3], None, OP.mult, reads=("pb0", sck), writes=(kk("w_tok"),))
                for c in range(ND):
                    mm(pm[0][:, 0:128], xnt[p][:, c * 128:(c + 1) * 128],
                       AT_W[("wv", hp)][:, c * 128:(c + 1) * 128],
                       reads=(xk, f"wv{hp}"), writes=("pm0",),
                       start=(c == 0), stop=(c == ND - 1))
                if vpr:
                    op("V", "tensor_copy", vpre[:, li * 128:(li + 1) * 128], pm[0][:, 0:128],
                       reads=("pm0",), writes=("vpre",))
                else:
                    op("V", "tensor_copy", vt[:, ci * 132: ci * 132 + 128], pm[0][:, 0:128],
                       reads=("pm0",), writes=("vt",))
                op("V", "tensor_scalar", tk("bw_tok")[:, :], tk("w_tok")[:, :],
                   btP[hg][:, ci:ci + 1], None, OP.mult,
                   reads=(kk("w_tok"), f"btC{hg}"), writes=(kk("bw_tok"),))
                tr(pb[0][:, 0:128], tk("q_tok")[:, :], ident_bf[:, :],
                   reads=(kk("q_tok"),), writes=("pb0",))
                op("V", "tensor_copy", tk("qT")[:, :], pb[0][:, 0:128],
                   reads=("pb0",), writes=(kk("qT"),))
                tr(pb[1][:, 0:128], tk("k_tok")[:, :], ident_bf[:, :],
                   reads=(kk("k_tok"),), writes=("pb1",))
                op("V", "tensor_copy", tk("kT")[:, :], pb[1][:, 0:128],
                   reads=("pb1",), writes=(kk("kT"),))
                tr(pb[0][:, 0:128], tk("w_tok")[:, :], ident_bf[:, :],
                   reads=(kk("w_tok"),), writes=("pb0",))
                op("V", "tensor_copy", tk("wT")[:, :], pb[0][:, 0:128],
                   reads=("pb0",), writes=(kk("wT"),))
                mm(pm[0][:, 0:128], tk("wT")[:, :], tk("wT")[:, :],
                   reads=(kk("wT"),), writes=("pm0",))
                op("V", "scalar_tensor_tensor", tk("A1")[:, :], pm[0][:, 0:128],
                   bnP[hg][:, ci:ci + 1], maskSL[:, :], OP.mult, OP.mult,
                   reads=("pm0", f"bnC{hg}", "maskSL"), writes=(kk("A1"),))
                mm(pm[1][:, 0:128], tk("wT")[:, :], tk("kT")[:, :],
                   reads=(kk("wT"), kk("kT")), writes=("pm1",))
                op("V", "scalar_tensor_tensor", tk("R1")[:, :], pm[1][:, 0:128],
                   btP[hg][:, ci:ci + 1], maskSL[:, :], OP.mult, OP.mult,
                   reads=("pm1", f"btC{hg}", "maskSL"), writes=(kk("R1"),))
                tr(pb[0][:, 0:128], tk("A1")[:, :], ident_bf[:, :],
                   reads=(kk("A1"),), writes=("pb0",))
                op("V", "tensor_copy", tk("A1T")[:, :], pb[0][:, 0:128],
                   reads=("pb0",), writes=(kk("A1T"),))
                op("V", "tensor_tensor", tk("PTa")[:, :], pb[0][:, 0:128], ident_bf[:, :],
                   OP.add, reads=("pb0", "ident"), writes=(kk("PTa"),))
                curA, curAT = "A1", "A1T"
                pt_cur, pt_nxt = "PTa", "PTb"
                lvl = 2
                while lvl <= C // 2:
                    nA = "Ak2" if curA != "Ak2" else "Ak"
                    nAT = "Ak2T" if curA != "Ak2" else "AkT"
                    mm(pm[0][:, 0:128], tk(curAT)[:, :], tk(curA)[:, :],
                       reads=(kk(curAT), kk(curA)), writes=("pm0",))
                    op("V", "tensor_copy", tk(nA)[:, :], pm[0][:, 0:128],
                       reads=("pm0",), writes=(kk(nA),))
                    if lvl < C // 2:
                        mm(pm[1][:, 0:128], tk(curA)[:, :], tk(curAT)[:, :],
                           reads=(kk(curA), kk(curAT)), writes=("pm1",))
                        op("V", "tensor_copy", tk(nAT)[:, :], pm[1][:, 0:128],
                           reads=("pm1",), writes=(kk(nAT),))
                    mm(pm[0][:, 0:128], tk(nA)[:, :], tk(pt_cur)[:, :],
                       reads=(kk(nA), kk(pt_cur)), writes=("pm0",))
                    op("V", "tensor_tensor", tk(pt_nxt)[:, :], pm[0][:, 0:128],
                       tk(pt_cur)[:, :], OP.add,
                       reads=("pm0", kk(pt_cur)), writes=(kk(pt_nxt),))
                    curA, curAT = nA, nAT
                    pt_cur, pt_nxt = pt_nxt, pt_cur
                    lvl *= 2
                TinvT, TinvTk = tk(pt_cur), kk(pt_cur)
                mm(pm[0][:, 0:128], TinvT[:, :], tk("bw_tok")[:, :],
                   reads=(TinvTk, kk("bw_tok")), writes=("pm0",))
                op("V", "tensor_copy", tk("Wbar")[:, :], pm[0][:, 0:128],
                   reads=("pm0",), writes=(kk("Wbar"),))
                mm(pm[1][:, 0:128], TinvT[:, :], tk("R1")[:, :],
                   reads=(TinvTk, kk("R1")), writes=("pm1",))
                op("V", "tensor_copy", tk("Xc")[:, :], pm[1][:, 0:128],
                   reads=("pm1",), writes=(kk("Xc"),))
                mm(pm[0][:, 0:128], tk("wT")[:, :], tk("qT")[:, :],
                   reads=(kk("wT"), kk("qT")), writes=("pm0",))
                op("V", "scalar_tensor_tensor", tk("WQm")[:, :], pm[0][:, 0:128],
                   -1.0, triuK[:, :], OP.mult, OP.mult,
                   reads=("pm0", "triuK"), writes=(kk("WQm"),))
                mm(pm[1][:, 0:128], tk("Wbar")[:, :], tk("WQm")[:, :],
                   reads=(kk("Wbar"), kk("WQm")), writes=("pm1",))
                op("V", "scalar_tensor_tensor", t8[("QtilT", slot)][:, :], pm[1][:, 0:128],
                   1.0, tk("qT")[:, :], OP.mult, OP.add,
                   reads=("pm1", kk("qT")), writes=(t8k("QtilT"),))
                mm(pm[0][:, 0:128], tk("w_tok")[:, :], tk("Xc")[:, :],
                   reads=(kk("w_tok"), kk("Xc")), writes=("pm0",))
                op("V", "scalar_tensor_tensor", t8[("KtilT", slot)][:, :], pm[0][:, 0:128],
                   -1.0, tk("kT")[:, :], OP.mult, OP.add,
                   reads=("pm0", kk("kT")), writes=(t8k("KtilT"),))
                mm(pm[1][:, 0:128], tk("Wbar")[:, :], tk("w_tok")[:, :],
                   reads=(kk("Wbar"), kk("w_tok")), writes=("pm1",))
                op("V", "scalar_tensor_tensor", t8[("PhiT", slot)][:, :], pm[1][:, 0:128],
                   -1.0, ident_bf[:, :], OP.mult, OP.add,
                   reads=("pm1", "ident"), writes=(t8k("PhiT"),))
                if li < SS - 1 or True:
                    mm(pm[0][:, 0:128], tk("w_tok")[:, :], tk("Wbar")[:, :],
                       reads=(kk("w_tok"), kk("Wbar")), writes=("pm0",))
                    op("V", "scalar_tensor_tensor", t8[("Phi", slot)][:, :], pm[0][:, 0:128],
                       -1.0, ident_bf[:, :], OP.mult, OP.add,
                       reads=("pm0", "ident"), writes=(t8k("Phi"),))
                mm(pm[1][:, 0:128], tk("kT")[:, :], tk("qT")[:, :],
                   reads=(kk("kT"), kk("qT")), writes=("pm1",), start=True, stop=False)
                mm(pm[1][:, 0:128], tk("Xc")[:, :], tk("WQm")[:, :],
                   reads=(kk("Xc"), kk("WQm")), writes=("pm1",), start=False, stop=True)
                op("V", "scalar_tensor_tensor", tmpD[p][:, :], pm[1][:, 0:128],
                   scale, gbc[sp][:, li * 128:(li + 1) * 128], OP.mult, OP.add,
                   reads=("pm1", f"gbc{sp}"), writes=(f"tmpD{p}",))
                op("V", "tensor_tensor", tmpD[p][:, :], tmpD[p][:, :], maskneg[:, :],
                   OP.add, reads=(f"tmpD{p}", "maskneg"), writes=(f"tmpD{p}",))
                op("S", "activation", t8[("expD", slot)][:, :], tmpD[p][:, :], AF.Exp,
                   bias=GnP[hg][:, ci:ci + 1],
                   reads=(f"tmpD{p}", f"GnC{hg}"), writes=(t8k("expD"),))

            def stage_B(bh, s):
                hg = bh % 2
                sp = s % 2
                slx = lambda li: li + SS * sp
                t8s = lambda nm, li: t8[(nm, slx(li))]
                t8sk = lambda nm, li: f"{nm}s{slx(li)}"
                qh, qhk = qhat[sp], f"qhat{sp}"
                op("V", "tensor_copy", qh[:, 0:128], t8s("QtilT", 0)[:, :],
                   reads=(t8sk("QtilT", 0),), writes=(qhk,))
                mm(pm[2][:, 0:128], t8s("Phi", 0)[:, :], t8s("QtilT", 1)[:, :],
                   reads=(t8sk("Phi", 0), t8sk("QtilT", 1)), writes=("pm2",))
                op("V", "tensor_copy", qh[:, 128:256], pm[2][:, 0:128],
                   reads=("pm2",), writes=(qhk,))
                mm(pm[3][:, 0:128], t8s("Phi", 1)[:, :], t8s("QtilT", 2)[:, :],
                   reads=(t8sk("Phi", 1), t8sk("QtilT", 2)), writes=("pm3",))
                op("V", "tensor_copy", qch[:, :], pm[3][:, 0:128],
                   reads=("pm3",), writes=("qch",))
                mm(pm[2][:, 0:128], t8s("Phi", 0)[:, :], qch[:, :],
                   reads=(t8sk("Phi", 0), "qch"), writes=("pm2",))
                op("V", "tensor_copy", qh[:, 256:384], pm[2][:, 0:128],
                   reads=("pm2",), writes=(qhk,))
                mm(pm[3][:, 0:128], t8s("Phi", 2)[:, :], t8s("QtilT", 3)[:, :],
                   reads=(t8sk("Phi", 2), t8sk("QtilT", 3)), writes=("pm3",))
                op("V", "tensor_copy", qch[:, :], pm[3][:, 0:128],
                   reads=("pm3",), writes=("qch",))
                mm(pm[2][:, 0:128], t8s("Phi", 1)[:, :], qch[:, :],
                   reads=(t8sk("Phi", 1), "qch"), writes=("pm2",))
                op("V", "tensor_copy", qch2[:, :], pm[2][:, 0:128],
                   reads=("pm2",), writes=("qch2",))
                mm(pm[3][:, 0:128], t8s("Phi", 0)[:, :], qch2[:, :],
                   reads=(t8sk("Phi", 0), "qch2"), writes=("pm3",))
                op("V", "tensor_copy", qh[:, 384:512], pm[3][:, 0:128],
                   reads=("pm3",), writes=(qhk,))

                bank_started = [False, False]

                def po_region(li):
                    return po2[li // 2][:, (li % 2) * 144: (li % 2) * 144 + 129], f"po2{li // 2}"

                def po_flags(li, last=False):
                    bk = li // 2
                    st = not bank_started[bk]
                    bank_started[bk] = True
                    return st, last

                def ammS(sb):
                    pmx, pmxk = (pm[2], "pm2") if sb % 2 == 0 else (pm[3], "pm3")
                    mm(pmx[:, 0:512], khat[:, sb * 128:(sb + 1) * 128], qh[:, :],
                       reads=("khat", qhk), writes=(pmxk,))
                    op("V", "scalar_tensor_tensor", tmpS[sb % 2][:, :], pmx[:, 0:512],
                       scale, gbc[sp][:, :], OP.mult, OP.add,
                       reads=(pmxk, f"gbc{sp}"), writes=("tmpS0",))
                    op("S", "activation", expS[sb % 3][:, :], tmpS[sb % 2][:, :], AF.Exp,
                       bias=GnP[hg][:, sb:sb + 1],
                       reads=("tmpS0", f"GnC{hg}"), writes=(f"expS{sb % 3}",))

                def pvS(sb):
                    for li in range(SS):
                        reg, rk = po_region(li)
                        st, _ = po_flags(li)
                        mm(reg, expS[sb % 3][:, li * 128:(li + 1) * 128],
                           vt[:, sb * 132: sb * 132 + 129],
                           reads=(f"expS{sb % 3}", "vt"), writes=(rk,),
                           start=st, stop=False, skip_group_check=True)

                npre = SS * s
                if npre > 0:
                    ammS(0)
                    if npre > 1:
                        ammS(1)
                    for sb in range(2, npre):
                        ammS(sb)
                        pvS(sb - 2)
                    for sb in range(max(0, npre - 2), npre):
                        pvS(sb)

                for li in range(SS):
                    ci = SS * s + li
                    for lj in range(li):
                        pmx, pmxk = (pm[2], "pm2") if lj % 2 == 0 else (pm[3], "pm3")
                        mm(pmx[:, 0:128], khat_loc[:, lj * 128:(lj + 1) * 128],
                           t8s("QtilT", li)[:, :],
                           reads=("khat_loc", t8sk("QtilT", li)), writes=(pmxk,))
                        op("V", "scalar_tensor_tensor", tmpI[lj % 2][:, :], pmx[:, 0:128],
                           scale, gbc[sp][:, li * 128:(li + 1) * 128], OP.mult, OP.add,
                           reads=(pmxk, f"gbc{sp}"), writes=(f"tmpI{lj % 2}",))
                        op("S", "activation", expI[lj % 2][:, :], tmpI[lj % 2][:, :],
                           AF.Exp, bias=GnP[hg][:, SS * s + lj: SS * s + lj + 1],
                           reads=(f"tmpI{lj % 2}", f"GnC{hg}"), writes=(f"expI{lj % 2}",))
                        reg, rk = po_region(li)
                        st, _ = po_flags(li)
                        mm(reg, expI[lj % 2][:, :],
                           vt[:, (SS * s + lj) * 132: (SS * s + lj) * 132 + 129],
                           reads=(f"expI{lj % 2}", "vt"), writes=(rk,),
                           start=st, stop=False, skip_group_check=True)
                    reg, rk = po_region(li)
                    st, _ = po_flags(li)
                    mm(reg, t8s("expD", li)[:, :], vt[:, ci * 132: ci * 132 + 129],
                       reads=(t8sk("expD", li), "vt"), writes=(rk,),
                       start=st, stop=(li % 2 == 1), skip_group_check=True)
                    if li > 0:
                        mm(pm[2][:, 0:li * 128], t8s("PhiT", li)[:, :],
                           khat_loc[:, 0:li * 128],
                           reads=(t8sk("PhiT", li), "khat_loc"), writes=("pm2",))
                        op("V", "tensor_copy", khat_loc[:, 0:li * 128],
                           pm[2][:, 0:li * 128], reads=("pm2",), writes=("khat_loc",))
                    if li < SS - 1:
                        op("V", "tensor_copy", khat_loc[:, li * 128:(li + 1) * 128],
                           t8s("KtilT", li)[:, :],
                           reads=(t8sk("KtilT", li),), writes=("khat_loc",))
                for li in range(SS):
                    ci = SS * s + li
                    reg, rk = po_region(li)
                    op("V", "reciprocal", orc[:, :], reg[:, 128:129],
                       reads=(rk,), writes=("orc",))
                    op("V", "tensor_scalar",
                       a2[:, ci * HD + h * 128: ci * HD + (h + 1) * 128],
                       reg[:, 0:128], orc[:, :], None, OP.mult,
                       reads=(rk, "orc"), writes=("a2",))
                if s < NSUP - 1:
                    mm(pm[2][:, 0:128], t8s("Phi", 2)[:, :], t8s("PhiT", 3)[:, :],
                       reads=(t8sk("Phi", 2), t8sk("PhiT", 3)), writes=("pm2",))
                    op("V", "tensor_copy", psT[:, :], pm[2][:, 0:128],
                       reads=("pm2",), writes=("psT",))
                    mm(pm[3][:, 0:128], t8s("Phi", 1)[:, :], psT[:, :],
                       reads=(t8sk("Phi", 1), "psT"), writes=("pm3",))
                    op("V", "tensor_copy", psT2[:, :], pm[3][:, 0:128],
                       reads=("pm3",), writes=("psT2",))
                    mm(pm[2][:, 0:128], t8s("Phi", 0)[:, :], psT2[:, :],
                       reads=(t8sk("Phi", 0), "psT2"), writes=("pm2",))
                    op("V", "tensor_copy", psT[:, :], pm[2][:, 0:128],
                       reads=("pm2",), writes=("psT",))
                    for tt in range(s):
                        pmx, pmxk = (pm[3], "pm3") if tt % 2 == 0 else (pm[2], "pm2")
                        mm(pmx[:, 0:512], psT[:, :], khat[:, tt * 512:(tt + 1) * 512],
                           reads=("psT", "khat"), writes=(pmxk,))
                        op("V", "tensor_copy", khat[:, tt * 512:(tt + 1) * 512],
                           pmx[:, 0:512], reads=(pmxk,), writes=("khat",))
                    op("V", "tensor_copy", khat[:, s * 512: s * 512 + 384],
                       khat_loc[:, :], reads=("khat_loc",), writes=("khat",))
                    op("V", "tensor_copy", khat[:, s * 512 + 384: (s + 1) * 512],
                       t8s("KtilT", 3)[:, :], reads=(t8sk("KtilT", 3),), writes=("khat",))

            def vpre_restore():
                for cci in range(SS):
                    op("V", "tensor_copy", vt[:, cci * 132: cci * 132 + 128],
                       vpre[:, cci * 128:(cci + 1) * 128],
                       reads=("vpre",), writes=("vt",))

            if h == 0:
                for f_ in collect(lambda: [stage_A(h, c) for c in range(SS)]):
                    f_()
            for s in range(NSUP):
                if s == 0 and h > 0:
                    lb = collect(lambda: (vpre_restore(), stage_B(h, 0)))
                else:
                    lb = collect(lambda s=s: stage_B(h, s))
                if s + 1 < NSUP:
                    la = collect(lambda s=s: [stage_A(h, SS * (s + 1) + l) for l in range(SS)])
                elif h + 1 < NH:
                    la = collect(lambda h2=h + 1: (head_prefetch(h2, pm[0], "pm0"),
                                                   [stage_A(h2, c, vpr=True)
                                                    for c in range(SS)]))
                else:
                    la = []
                interleave_emit(lb, la)

        if not SKIP_MAIN and mode == "fused":
            for k in ("xin2", "xin3", "stg_tok1"):
                sch.merge_deps(k, ["wcb"])
            sch.merge_deps("hw_all", ["stripe"])
            for t_ in range(N_EARLY):
                ln_early(t_)

        for k in FFN_KEYS:
            sch.merge_deps(k, AT_KEYS + ["stripe"])
        if not SKIP_MAIN and mode == "fused":
            dma_w("P", rt_sb[:, :], routerT.ap(), ND, reads=(), writes=("rt_sb",))
            for i in range(NHS):
                dma_w("P", eWg[0][i], sgT.ap()[:, i * 128:(i + 1) * 128],
                      ND, reads=(), writes=("eWg0",))
                dma_w("P", eWu[0][i], suT.ap()[:, i * 128:(i + 1) * 128],
                      ND, reads=(), writes=("eWu0",))
                dma("P", eWd[0][i], sdT.ap()[i * 128:(i + 1) * 128, :],
                    reads=(), writes=("eWd0",))

        if mode == "fused":
            for ci in range(NCH):
                dma("Q", o_gath.ap()[ci * 128:(ci + 1) * 128, :],
                    a2[:, ci * HD:(ci + 1) * HD], reads=("a2",), writes=("o_gath",))
            def rdb_fn(EE):
                return EE.remote_dma_broadcast(
                    ffn_acc[:, :], a2[:, :], sems["rsem"], sems["lsem"],
                    rdests=[(0, 1)] + [None] * 7)
            sch.emit("P", rdb_fn, reads=("a2",), writes=("rdbring",), inc=1)
            op("P", "trigger_dma", reads=("rdbring",), writes=("rdbtrig",))
            nop1 = op("P", "engine_nop", reads=("a2", "rdbtrig"), writes=())
            nop1.pre_wait = ("lsem", 16)
            nop2 = op("P", "engine_nop", reads=("rdbtrig",), writes=("ffn_acc",))
            nop2.pre_wait = ("rsem", 2)
            for ci in range(NCH):
                dma("Q", o_gath.ap()[T + ci * 128: T + (ci + 1) * 128, :],
                    ffn_acc[:, ci * HD:(ci + 1) * HD],
                    reads=("ffn_acc",), writes=("o_gath",))
        elif mode == "p1":
            for ci in range(NCH):
                dma("Q", o_bounce.ap()[ci * 128:(ci + 1) * 128, :],
                    a2[:, ci * HD:(ci + 1) * HD], reads=("a2",), writes=("o_bounce",))
        cc_op = None

        if not SKIP_MAIN:
            if mode != "fused":
                dma_w("P", rt_sb[:, :], routerT.ap(), ND, reads=(), writes=("rt_sb",))
        for k in ("xin2", "xin3", "stg_tok1"):
            sch.merge_deps(k, ["wcb"])
        sch.merge_deps("stg_tok0", ["vt"])
        ti0 = 0
        if mode == "fused" and not SKIP_MAIN:
            ti0 = N_EARLY
            for ti in range(N_EARLY):
                for c in range(ND):
                    op("V", "tensor_copy",
                       a2[:, c * OWN + ti * 128: c * OWN + (ti + 1) * 128],
                       hw_all[:, ti * D + c * 128: ti * D + (c + 1) * 128],
                       reads=("hw_all",), writes=("a2",))
        for ti in range(ti0, NOT if not SKIP_MAIN else 0):
            ln_tile(x_own, ti, False)

        for ti in range(NOT if not SKIP_MAIN else 0):
            for c in range(ND):
                mm(pm[0][:, 0:E], a2[:, c * OWN + ti * 128: c * OWN + (ti + 1) * 128],
                   rt_sb[:, c * E:(c + 1) * E],
                   reads=("a2", "rt_sb"), writes=("pm0",),
                   start=(c == 0), stop=(c == ND - 1))
            op("V", "tensor_tensor", rtf[0][:, 0:E], pm[0][:, 0:E], rb_bc[:, :],
               OP.add, reads=("pm0", "rb_bc"), writes=("rtf0",))
            op("V", "reduce_max", rts[:, 0:1], rtf[0][:, 0:E], AX.X,
               negate=True, reads=("rtf0",), writes=("rts",))
            op("S", "activation", rtf[1][:, 0:E], rtf[0][:, 0:E], AF.Exp,
               bias=rts[:, 0:1], reads=("rtf0", "rts"), writes=("rtf1",))
            op("V", "tensor_copy", rtf[2][:, 0:E], rtf[1][:, 0:E],
               reads=("rtf1",), writes=("rtf2",))
            for _ in range(3):
                op("V", "reduce_max", rts[:, 1:2], rtf[2][:, 0:E], AX.X,
                   reads=("rtf2",), writes=("rts",))
                op("V", "tensor_scalar", rtf[0][:, 0:E], rtf[2][:, 0:E],
                   rts[:, 1:2], None, OP.is_ge, reads=("rtf2", "rts"), writes=("rtf0",))
                op("V", "scalar_tensor_tensor", rtf[2][:, 0:E], rtf[0][:, 0:E],
                   -1e30, rtf[2][:, 0:E], OP.mult, OP.add,
                   reads=("rtf0", "rtf2"), writes=("rtf2",))
            op("V", "reduce_max", rts[:, 1:2], rtf[2][:, 0:E], AX.X,
               reads=("rtf2",), writes=("rts",))
            op("V", "tensor_scalar", rtf[0][:, 0:E], rtf[1][:, 0:E],
               rts[:, 1:2], None, OP.is_ge, reads=("rtf1", "rts"), writes=("rtf0",))
            op("V", "tensor_tensor", rtf[1][:, 0:E], rtf[1][:, 0:E], rtf[0][:, 0:E],
               OP.mult, reads=("rtf1", "rtf0"), writes=("rtf1",))
            op("V", "reduce_sum", rts[:, 2:3], rtf[1][:, 0:E], AX.X,
               reads=("rtf1",), writes=("rts",))
            op("V", "reciprocal", rts[:, 2:3], rts[:, 2:3], reads=("rts",), writes=("rts",))
            op("V", "tensor_scalar", w4buf[:, :], rtf[1][:, 0:E],
               rts[:, 2:3], None, OP.mult, reads=("rtf1", "rts"), writes=("w4buf",))
            tr(pm[2][0:E, 0:128], w4buf[:, :], ident_f[:, :],
               reads=("w4buf",), writes=("pm2",))
            op("V", "tensor_copy", w4T[0:E, ti * 128:(ti + 1) * 128], pm[2][0:E, 0:128],
               reads=("pm2",), writes=("w4T",))
        if not SKIP_MAIN:
            dma("Q", w4dram.ap(), w4T[0:E, :], reads=("w4T",), writes=("w4dram",))

        blocks = ([("sh", NHS)] + [(bb, EBLK) for bb in range(NBLK)]) if not SKIP_MAIN else []
        for bi, (tag, nE) in enumerate(blocks):
            bp = bi % 2
            for i in range(nE):
                if tag == "sh":
                    if mode == "fused":
                        continue
                    dma_w("P", eWg[bp][i], sgT.ap()[:, i * 128:(i + 1) * 128],
                          ND, reads=(), writes=(f"eWg{bp}",))
                    dma_w("P", eWu[bp][i], suT.ap()[:, i * 128:(i + 1) * 128],
                          ND, reads=(), writes=("eWu0",))
                    dma("P", eWd[bp][i], sdT.ap()[i * 128:(i + 1) * 128, :],
                        reads=(), writes=("eWd0",))
                else:
                    e = tag * EBLK + i
                    dma_w("P", eWg[bp][i], gateT.ap()[e], ND, reads=(), writes=(f"eWg{bp}",))
                    dma_w("P", eWu[bp][i], upT.ap()[e], ND, reads=(), writes=("eWu0",))
                    dma("P", eWd[bp][i], downT.ap()[e], reads=(), writes=("eWd0",))
            for i in range(nE):
                rb_ = i % 2
                if tag != "sh":
                    e = tag * EBLK + i
                    dma("Q", w4row[rb_][0:1, :], w4dram.ap()[e:e + 1, :],
                        reads=("w4dram",), writes=(f"w4row{rb_}",))
                for tg in range(NGR):
                    for c in range(ND):
                        mm(pm[0][:, 0:512], eWg[bp][i][:, c * HME: c * HME + 128],
                           a2[:, c * OWN + tg * 512: c * OWN + (tg + 1) * 512],
                           reads=(f"eWg{bp}", "a2"), writes=("pm0",),
                           start=(c == 0), stop=(c == ND - 1))
                    for c in range(ND):
                        mm(pm[1][:, 0:512], eWu[bp][i][:, c * HME: c * HME + 128],
                           a2[:, c * OWN + tg * 512: c * OWN + (tg + 1) * 512],
                           reads=("eWu0", "a2"), writes=("pm1",),
                           start=(c == 0), stop=(c == ND - 1))
                    op("S", "activation", lnscr[:, 0:512], pm[0][:, 0:512], AF.Silu,
                       reads=("pm0",), writes=("lnscr",))
                    op("V", "scalar_tensor_tensor",
                       hw_all[:, i * OWN + tg * 512: i * OWN + (tg + 1) * 512],
                       pm[1][:, 0:512], 1.0, lnscr[:, 0:512], OP.mult, OP.mult,
                       reads=("pm1", "lnscr"), writes=("hw_all",))
                    if tag != "sh":
                        mm(pm[3][:, 0:512], ones1b[:, :],
                           w4row[rb_][0:1, tg * 512:(tg + 1) * 512],
                           reads=("ones1", f"w4row{rb_}"), writes=("pm3",))
                        op("V", "tensor_tensor",
                           hw_all[:, i * OWN + tg * 512: i * OWN + (tg + 1) * 512],
                           hw_all[:, i * OWN + tg * 512: i * OWN + (tg + 1) * 512],
                           pm[3][:, 0:512], OP.mult,
                           reads=("hw_all", "pm3"), writes=("hw_all",))
            for tt in range(NOT):
                for half in range(NHALF):
                    pma, pk = (pm[2], "pm2") if half % 2 == 0 else (pm[3], "pm3")
                    for i in range(nE):
                        mm(pma[:, 0:HSZ],
                           hw_all[:, i * OWN + tt * 128: i * OWN + (tt + 1) * 128],
                           eWd[bp][i][:, half * HSZ:(half + 1) * HSZ],
                           reads=("hw_all", "eWd0"), writes=(pk,),
                           start=(i == 0), stop=(i == nE - 1))
                    dst = ffn_acc[:, tt * D + half * HSZ: tt * D + (half + 1) * HSZ]
                    if bi == 0:
                        op("V", "tensor_copy", dst, pma[:, 0:HSZ],
                           reads=(pk,), writes=("ffn_acc",))
                    else:
                        op("V", "tensor_tensor", dst, pma[:, 0:HSZ], dst, OP.add,
                           reads=(pk, "ffn_acc"), writes=("ffn_acc",))

        for tt in range(NOT if not SKIP_MAIN else 0):
            for half in range(NHALF):
                sl_ = ffn_acc[:, tt * D + half * HSZ: tt * D + (half + 1) * HSZ]
                op("S", "activation", sl_, sl_, AF.Silu,
                   reads=("ffn_acc",), writes=("ffn_acc",))

        if mode == "p1":
            for ti in range(NOT):
                dma("Q", g_out.ap()[ti * 128:(ti + 1) * 128, :],
                    ffn_acc[:, ti * D:(ti + 1) * D], reads=("ffn_acc",), writes=("g_out",))

        for k in FIN_KEYS:
            sch.merge_deps(k, FFN_KEYS)
        if mode != "p1":
            dma_w("P", wo_sb, Wo_in.ap(), ND, reads=(), writes=("wo_sb",))
        if cc_op is not None:
            sch.barrier_wait("P", cc_op)
        for ti in range(NOT if mode != "p1" else 0):
            b = ti % 2
            ogk = f"og{b}"
            if mode == "p2":
                dma("Q", og[b][:, :], o_own_in.ap()[ti * 128:(ti + 1) * 128, :],
                    reads=(), writes=(ogk,))
            else:
                op("P", "indirect_dma_start", og[b][:, 0:HD], None, o_gath.ap(),
                   bass.IndirectOffsetOnAxis(ap=idx_sb[:, 2 * ti:2 * ti + 1], axis=0),
                   reads=("o_gath", "idx_sb"), writes=(ogk,), inc=16)
                op("P", "indirect_dma_start", og[b][:, HD:2 * HD], None, o_gath.ap(),
                   bass.IndirectOffsetOnAxis(ap=idx_sb[:, 2 * ti + 1:2 * ti + 2], axis=0),
                   reads=("o_gath", "idx_sb"), writes=(ogk,), inc=16)
            dma("Q", xo[b][:, :], x_own.ap()[ti * 128:(ti + 1) * 128, :],
                reads=(), writes=(f"xo{b}",))
            for c in range(ND):
                pbk = c % 2
                tr(pb[pbk][:, 0:128], og[b][:, c * 128:(c + 1) * 128], ident_bf[:, :],
                   reads=(ogk,), writes=(f"pb{pbk}",))
                op("V", "tensor_copy", oTt[b][:, c * 128:(c + 1) * 128], pb[pbk][:, 0:128],
                   reads=(f"pb{pbk}",), writes=(f"oTt{b}",))
            if mode == "p2" and ti < 2:
                pass
            if mode == "p2":
                dma("Q", ffn_acc[:, ti * D:(ti + 1) * D],
                    g_in.ap()[ti * 128:(ti + 1) * 128, :],
                    reads=(), writes=("ffn_acc",))
            for half in range(NHALF):
                pma, pk = pm[half % 2], f"pm{half % 2}"
                for c in range(ND):
                    mm(pma[:, 0:HSZ], oTt[b][:, c * 128:(c + 1) * 128],
                       wo_sb[:, c * D + half * HSZ: c * D + (half + 1) * HSZ],
                       reads=(f"oTt{b}", "wo_sb"), writes=(pk,),
                       start=(c == 0), stop=(c == ND - 1))
                op("V", "tensor_tensor", sfin[:, half * HSZ:(half + 1) * HSZ],
                   pma[:, 0:HSZ],
                   ffn_acc[:, ti * D + half * HSZ: ti * D + (half + 1) * HSZ],
                   OP.mult, reads=(pk, "ffn_acc"), writes=("sfin",))
            op("V", "tensor_tensor", outt[b][:, :], sfin[:, 0:D], xo[b][:, :],
               OP.add, reads=("sfin", f"xo{b}"), writes=(f"outt{b}",))
            dma("Q", out_d.ap()[ti * 128:(ti + 1) * 128, :], outt[b][:, :],
                reads=(f"outt{b}",), writes=("out_d",))

        for l in range(NLANES):
            slot = ("Q", l)
            k = sch.lane_cnt.get(slot, 0)
            if k:
                def wfn(EE, _s=slot, _v=16 * k):
                    return None
                wop = Op("P", None, 0, len(sch.q["P"]))
                wop.pre_wait = (slot, 16 * k)
                sch.q["P"].append(wop)

        sch.finalize()

        def runner(eng_obj, ename):
            for o in sch.q[ename]:
                if o.pre_wait is not None:
                    eng_obj.wait_ge(sems[o.pre_wait[0]], o.pre_wait[1])
                for pop in o.waits:
                    eng_obj.wait_ge(sems[pop.slot], pop.semval)
                if o.fn is not None:
                    ins = o.fn(eng_obj)
                    if o.marked:
                        ins.then_inc(sems[o.slot], o.inc)

        if mode == "fused" and not globals().get("_STUB_BARRIER_OFF", True):
            nc.bir_kernel_barrier_wait(replica_groups, engines=[nc.gpsimd])
        with nc.Block() as block:
            @block.tensor
            def _(e):
                runner(e, "T")

            @block.vector
            def _(e):
                runner(e, "V")

            @block.scalar
            def _(e):
                runner(e, "S")

            @block.gpsimd
            def _(e):
                runner(e, "P")

            @block.sync
            def _(e):
                runner(e, "Q")

    nc.finalize()
    return nc


def make_inmaps(inputs, cfg):
    T, D, NH, d = cfg["T"], cfg["D"], cfg["NH"], cfg["d"]
    E, KC = cfg["E"], cfg["KC"]
    n_cores = cfg["n_cores"]
    OWN = T // 2
    NOT = OWN // 128
    HD = NH * d
    x = np.asarray(inputs["L0_attn_x"], np.float32)
    maps = []
    for c in range(n_cores):
        b, j = c // 2, c % 2
        cols = slice(j * HD, (j + 1) * HD)
        hs = slice(j * NH, (j + 1) * NH)
        idx = np.zeros((128, 3 * NOT), np.uint32)
        for ti in range(NOT):
            base = j * OWN + ti * 128
            idx[:, 2 * ti] = (j * T) + base + np.arange(128)
            idx[:, 2 * ti + 1] = ((1 - j) * T) + base + np.arange(128)
            idx[:, 2 * NOT + ti] = (j * NOT + ti) * 128 + np.arange(128)
        m = {
            "x_b": x[b],
            "x_own": x[b, j * OWN:(j + 1) * OWN],
            "Wqh": np.asarray(inputs["L0_attn_Wq"])[:, cols],
            "Wkh": np.asarray(inputs["L0_attn_Wk"])[:, cols],
            "Wvh": np.asarray(inputs["L0_attn_Wv"])[:, cols],
            "Wwh": np.asarray(inputs["L0_attn_Ww"])[:, cols],
            "Wg4": np.asarray(inputs["L0_attn_Wg"])[:, hs],
            "Wb4": np.asarray(inputs["L0_attn_Wbeta"])[:, hs],
            "conv4": np.asarray(inputs["L0_attn_conv_w"])[cols, :],
            "qn": np.asarray(inputs["L0_attn_qn_w"]).reshape(1, d),
            "kn": np.asarray(inputs["L0_attn_kn_w"]).reshape(1, d),
            "Wo": np.asarray(inputs["L0_attn_Wo"]),
            "routerT": np.asarray(inputs["L0_ffn_router_w"], np.float32).T,
            "rbias": np.asarray(inputs["L0_ffn_router_bias"]).reshape(1, E),
            "gateT": np.asarray(inputs["L0_ffn_gate_w"], np.float32).transpose(0, 2, 1),
            "upT": np.asarray(inputs["L0_ffn_up_w"], np.float32).transpose(0, 2, 1),
            "downT": np.asarray(inputs["L0_ffn_down_w"], np.float32).transpose(0, 2, 1),
            "sgT": np.asarray(inputs["L0_ffn_sgate_w"], np.float32).T,
            "suT": np.asarray(inputs["L0_ffn_sup_w"], np.float32).T,
            "sdT": np.asarray(inputs["L0_ffn_sdown_w"], np.float32).T,
            "gidx": idx,
        }
        m = {k: np.ascontiguousarray(np.asarray(
            v, dtype=(np.uint32 if k == "gidx" else np.float32))) for k, v in m.items()}
        maps.append(m)
    return maps


KERNEL_MODE = "fused"


def p2_inmaps(in_maps, res1, cfg):
    OWN = cfg["T"] // 2
    nc_n = cfg["n_cores"]
    o_list = [np.asarray(res1[c]["o_bounce"]) for c in range(nc_n)]
    maps2 = []
    for c in range(nc_n):
        b, j = c // 2, c % 2
        o_own = np.concatenate(
            [o_list[2 * b][j * OWN:(j + 1) * OWN, :],
             o_list[2 * b + 1][j * OWN:(j + 1) * OWN, :]], axis=1)
        m = dict(in_maps[c])
        m["o_own"] = np.ascontiguousarray(o_own)
        m["g_in"] = np.ascontiguousarray(np.asarray(res1[c]["g_out"]))
        maps2.append(m)
    return maps2


def kernel(**inputs):
    cfg = full_cfg()
    T, D = cfg["T"], cfg["D"]
    OWN = T // 2
    in_maps = make_inmaps(inputs, cfg)
    cores = list(range(cfg["n_cores"]))
    if KERNEL_MODE == "fused":
        nc = build_nc(cfg, "fused")
        res = run_bass_kernel_spmd(nc, in_maps, core_ids=cores)
        final = res.results
    else:
        nc1 = build_nc(cfg, "p1")
        res1 = run_bass_kernel_spmd(nc1, in_maps, core_ids=cores)
        maps2 = p2_inmaps(in_maps, res1.results, cfg)
        nc2 = build_nc(cfg, "p2")
        res2 = run_bass_kernel_spmd(nc2, maps2, core_ids=cores)
        final = res2.results
    B = np.asarray(inputs["L0_attn_x"]).shape[0]
    out = np.zeros((B, T, D), np.float32)
    for c in range(cfg["n_cores"]):
        b, j = c // 2, c % 2
        out[b, j * OWN:(j + 1) * OWN] = final[c]["out"]
    return out


# version 44 (final)
# speedup vs baseline: 1.3230x; 1.0199x over previous
import numpy as np
import concourse.bass as bass
import concourse.bacc as bacc
import concourse.mybir as mybir
from concourse.bass_utils import run_bass_kernel_spmd
from contextlib import ExitStack

F32 = mybir.dt.float32
BF16 = mybir.dt.bfloat16
U32 = mybir.dt.uint32
I32 = mybir.dt.int32
AF = mybir.ActivationFunctionType
OP = mybir.AluOpType
AX = mybir.AxisListType

ENGS = ("T", "V", "S", "P", "Q")


def full_cfg():
    return dict(T=4096, D=1024, NH=4, d=128, C=128, KC=4, E=32, EBLK=4,
                HME=128, HS=512, n_cores=8)


class Op:
    __slots__ = ("eng", "fn", "inc", "waits", "marked", "semval", "idx",
                 "slot", "lane_k", "pre_wait")

    def __init__(self, eng, fn, inc, idx):
        self.eng, self.fn, self.inc, self.idx = eng, fn, inc, idx
        self.waits = []
        self.marked = inc != 1
        self.semval = None
        self.slot = eng
        self.lane_k = None
        self.pre_wait = None


NLANES = 6


class Sched:
    def __init__(self):
        self.q = {e: [] for e in ENGS}
        self.tiles = {}
        self.waited = {e: {} for e in ENGS}
        self.lane_rr = {"Q": 0, "P": 0}
        self.lane_cnt = {}

    def _t(self, name):
        if name not in self.tiles:
            self.tiles[name] = {"w": {}, "r": {}}
        return self.tiles[name]

    def emit(self, eng, fn, reads=(), writes=(), inc=1, lane=None):
        op = Op(eng, fn, inc, len(self.q[eng]))
        if inc != 1:
            if lane is None:
                lane = self.lane_rr[eng]
                self.lane_rr[eng] = (lane + 1) % NLANES
            slot = (eng, lane)
            k = self.lane_cnt.get(slot, 0) + 1
            self.lane_cnt[slot] = k
            op.slot = slot
            op.lane_k = k
            if k > 1:
                op.pre_wait = (slot, 16 * (k - 1))
        need = {}

        def consider(p):
            if p.eng == eng and eng == "T":
                return
            cur = need.get(p.slot)
            if cur is None or p.idx > cur.idx:
                need[p.slot] = p

        for b in reads:
            for p in self._t(b)["w"].values():
                consider(p)
        for b in writes:
            t = self._t(b)
            for p in t["w"].values():
                consider(p)
            for p in t["r"].values():
                consider(p)
        wd = self.waited[eng]
        for ps, p in need.items():
            if p.idx > wd.get(ps, -1):
                p.marked = True
                op.waits.append(p)
                wd[ps] = p.idx
        self.q[eng].append(op)
        for b in writes:
            t = self._t(b)
            t["w"][op.slot] = op
            t["r"] = {}
        for b in reads:
            self._t(b)["r"][op.slot] = op
        return op

    def merge_deps(self, new_key, old_keys):
        t = self._t(new_key)
        for ok in old_keys:
            o = self._t(ok)
            for s, p in o["w"].items():
                if s not in t["w"] or t["w"][s].idx < p.idx:
                    t["w"][s] = p
            for s, p in o["r"].items():
                if s not in t["r"] or t["r"][s].idx < p.idx:
                    t["r"][s] = p

    def barrier_wait(self, eng, producer_op):
        op = Op(eng, None, 0, len(self.q[eng]))
        producer_op.marked = True
        op.waits.append(producer_op)
        self.waited[eng][producer_op.slot] = max(
            self.waited[eng].get(producer_op.slot, -1), producer_op.idx)
        self.q[eng].append(op)
        return op

    def finalize(self):
        for e in ENGS:
            cum = 0
            for op in self.q[e]:
                if op.lane_k is not None:
                    op.semval = 16 * op.lane_k
                else:
                    if op.marked:
                        cum += op.inc
                    op.semval = cum


def build_nc(cfg, mode="fused"):
    T, D, NH, d, C = cfg["T"], cfg["D"], cfg["NH"], cfg["d"], cfg["C"]
    KC, E, EBLK, HME, HS = cfg["KC"], cfg["E"], cfg["EBLK"], cfg["HME"], cfg["HS"]
    ND = D // 128
    NCH = T // C
    OWN = T // 2
    NOT = OWN // 128
    NHS = HS // 128
    NGR = OWN // 512
    NTI = T // 128
    scale = float(d) ** -0.5
    HD = NH * d
    n_cores = cfg["n_cores"]
    NBLK = E // EBLK
    NEB = max(EBLK, NHS)
    NHALF = max(1, D // 512)
    HSZ = D // NHALF

    nc = bacc.Bacc("TRN2", target_bir_lowering=False)

    dI = lambda n, s, dt=F32: nc.dram_tensor(n, s, dt, kind="ExternalInput")
    x_b = dI("x_b", [T, D])
    x_own = dI("x_own", [OWN, D])
    Wqh = dI("Wqh", [D, HD]); Wkh = dI("Wkh", [D, HD])
    Wvh = dI("Wvh", [D, HD]); Wwh = dI("Wwh", [D, HD])
    Wg4 = dI("Wg4", [D, NH]); Wb4 = dI("Wb4", [D, NH])
    conv4 = dI("conv4", [HD, KC])
    qn_in = dI("qn", [1, d]); kn_in = dI("kn", [1, d])
    Wo_in = dI("Wo", [D, D])
    routerT = dI("routerT", [D, E])
    rbias = dI("rbias", [1, E])
    gateT = dI("gateT", [E, D, HME])
    upT = dI("upT", [E, D, HME])
    downT = dI("downT", [E, HME, D])
    sgT = dI("sgT", [D, HS]); suT = dI("suT", [D, HS])
    sdT = dI("sdT", [HS, D])
    gidx = dI("gidx", [128, 3 * NOT], U32)
    out_d = nc.dram_tensor("out", [OWN, D], F32, kind="ExternalOutput")

    xnT_dram = nc.dram_tensor("xnT_dram", [NTI, 128, ND * 128], BF16)
    o_bounce = nc.dram_tensor("o_bounce", [T, HD], BF16,
                              kind="ExternalOutput" if mode == "p1" else "Internal")
    o_gath = nc.dram_tensor("o_gath", [2 * T, HD], BF16)
    gdram = nc.dram_tensor("gdram", [4, NH, T], F32)
    w4dram = nc.dram_tensor("w4dram", [E, OWN], BF16)
    if mode == "p1":
        g_out = nc.dram_tensor("g_out", [OWN, D], BF16, kind="ExternalOutput")
    if mode == "p2":
        o_own_in = nc.dram_tensor("o_own", [OWN, D], BF16, kind="ExternalInput")
        g_in = nc.dram_tensor("g_in", [OWN, D], BF16, kind="ExternalInput")

    replica_groups = [[2 * i, 2 * i + 1] for i in range(n_cores // 2)]

    sch = Sched()

    with ExitStack() as st:
        ec = st.enter_context
        sems = {e: ec(nc.semaphore(f"sem_{e}")) for e in ENGS}
        for qe in ("Q", "P"):
            for l in range(NLANES):
                sems[(qe, l)] = ec(nc.semaphore(f"sem_{qe}{l}"))
        sems[("P", "cc")] = ec(nc.semaphore("sem_cc"))
        sems["rsem"] = ec(nc.semaphore("rsem"))
        sems["lsem"] = ec(nc.semaphore("lsem"))

        def sbuf(name, shape, dt):
            return ec(nc.sbuf_tensor(name, shape, dt))

        a2 = sbuf("a2", [128, NCH * HD], BF16)
        ffn_acc = sbuf("ffn_acc", [128, NOT * D], BF16)
        vt = sbuf("vt", [128, NCH * 132], BF16)
        khat = sbuf("khat", [128, T], BF16)
        wcb = sbuf("wcb", [128, T], BF16)
        wpre = sbuf("wpre", [128, 131], F32)
        cvs = sbuf("cvs", [128, 256], F32)
        xnt = [sbuf(f"xnt{i}", [128, ND * 128], BF16) for i in range(2)]
        xin = [sbuf(f"xin{i}", [128, D], BF16) for i in range(2)]
        names128 = ["q_tok", "k_tok", "qT", "kT", "w_tok",
                    "bw_tok", "wT", "A1", "A1T", "R1", "Ak", "AkT", "Ak2", "Ak2T",
                    "PTa", "PTb", "Wbar", "Xc", "WQm", "scrbf"]
        t128 = {}
        for nm in names128:
            for p in range(2):
                t128[(nm, p)] = sbuf(f"{nm}{p}", [128, 128], BF16)
        names8 = ["QtilT", "KtilT", "PhiT", "Phi", "expD"]
        t8 = {}
        for nm in names8:
            for sl in range(8):
                t8[(nm, sl)] = sbuf(f"{nm}s{sl}", [128, 128], BF16)
        expS = [sbuf(f"expS{i}", [128, 512], BF16) for i in range(3)]
        tmpS1 = sbuf("tmpS0", [128, 512], F32)
        tmpS = [tmpS1, tmpS1]
        expI = [sbuf(f"expI{i}", [128, 128], BF16) for i in range(2)]
        tmpI = [sbuf(f"tmpI{i}", [128, 128], F32) for i in range(2)]
        tmpD = [sbuf(f"tmpD{i}", [128, 128], F32) for i in range(2)]
        qch = sbuf("qch", [128, 128], BF16)
        qch2 = sbuf("qch2", [128, 128], BF16)
        qhat = [sbuf(f"qhat{i}", [128, 512], BF16) for i in range(2)]
        khat_loc = sbuf("khat_loc", [128, 384], BF16)
        psT = sbuf("psT", [128, 128], BF16)
        psT2 = sbuf("psT2", [128, 128], BF16)
        gbc = [sbuf(f"gbc{i}", [128, 512], F32) for i in range(2)]
        vpre = sbuf("vpre", [128, 512], BF16)
        bias4 = [sbuf(f"bias4_{i}", [128, NCH], F32) for i in range(4)]
        nrm = [sbuf(f"nrm{i}", [128, 4], F32) for i in range(2)]
        nsc = [sbuf(f"nsc{i}", [128, 4], F32) for i in range(2)]
        nt2 = [sbuf(f"nt2{i}", [128, 4], F32) for i in range(2)]
        gsg = [sbuf(f"gsg{i}", [1, 128], F32) for i in range(2)]
        ident_bf = sbuf("ident_bf", [128, 128], BF16)
        ident_f = sbuf("ident_f", [128, 128], F32)
        maskSL = sbuf("maskSL", [128, 128], BF16)
        triuK = sbuf("triuK", [128, 128], BF16)
        maskneg = sbuf("maskneg", [128, 128], F32)
        ones1 = sbuf("ones1", [1, 128], F32)
        ones1b = sbuf("ones1b", [1, 128], BF16)
        qn_row = sbuf("qn_row", [1, d], F32)
        kn_row = sbuf("kn_row", [1, d], F32)
        qn_bc = sbuf("qn_bc", [128, d], F32)
        kn_bc = sbuf("kn_bc", [128, d], F32)
        rb_row = sbuf("rb_row", [1, E], F32)
        rb_bc = sbuf("rb_bc", [128, E], F32)
        bnst4 = sbuf("bnst4", [128, 48], F32)
        bnag4 = sbuf("bnag4", [128, 8], F32)
        lnn = [sbuf(f"lnn{i}", [128, 1], F32) for i in range(4)]
        lnsc = [sbuf(f"lnsc{i}", [128, 1], F32) for i in range(4)]
        lnt = [sbuf(f"lnt{i}", [128, 1], F32) for i in range(4)]
        lnscr = sbuf("lnscr", [128, 512], BF16)
        qst = sbuf("qst", [128, 2], F32)
        kst = sbuf("kst", [128, 2], F32)
        wst = sbuf("wst", [128, 2], F32)
        orc = sbuf("orc", [128, 1], F32)
        rts = sbuf("rts", [128, 4], F32)
        GnP = [sbuf(f"GnC{i}", [128, NCH], F32) for i in range(2)]
        btP = [sbuf(f"btC{i}", [128, NCH], F32) for i in range(2)]
        bnP = [sbuf(f"bnC{i}", [128, NCH], F32) for i in range(2)]
        gseg = sbuf("gseg", [NCH, 128], F32)
        wg_sb = sbuf("wg_sb", [128, ND * NH], BF16)
        wb_sb = sbuf("wb_sb", [128, ND * NH], BF16)
        cw_sb = sbuf("cw_sb", [128, NH * KC], F32)
        idx_sb = sbuf("idx_sb", [128, 3 * NOT], U32)
        w4buf = sbuf("w4buf", [128, E], F32)
        rtf = [sbuf(f"rtf{i}", [128, E], F32) for i in range(3)]
        rt_sb = sbuf("rt_sb", [128, ND * E], BF16)

        AR_BYTES = 62 * 1024
        arena = sbuf("arena", [128, AR_BYTES], mybir.dt.uint8)
        ar_f32 = arena.ap().bitcast(F32)
        ar_bf = arena.ap().bitcast(BF16)

        def av(off, cols, dt):
            if dt == F32:
                assert off % 4 == 0
                return ar_f32[:, off // 4: off // 4 + cols]
            assert off % 2 == 0
            return ar_bf[:, off // 2: off // 2 + cols]

        AT_W = {}
        offa = 0
        for nm in ("wq", "wk", "wv", "ww"):
            for p in range(2):
                AT_W[(nm, p)] = av(offa, ND * 128, BF16)
                offa += ND * 128 * 2
        AT_KEYS = [f"{nm}{p}" for nm in ("wq", "wk", "wv", "ww") for p in range(2)]
        stripe = av(0, 3 * T, F32)
        assert 3 * T * 4 <= AR_BYTES
        assert offa <= AR_BYTES

        offf = 0

        def fav(cols, dt):
            nonlocal offf
            r = av(offf, cols, dt)
            offf += cols * (4 if dt == F32 else 2)
            return r

        eWg = [[fav(ND * HME, BF16) for _ in range(NEB)] for _ in range(2)]
        eWu1 = [fav(ND * HME, BF16) for _ in range(NEB)]
        eWu = [eWu1, eWu1]
        eWd1 = [fav(D, BF16) for _ in range(NEB)]
        eWd = [eWd1, eWd1]
        hw_all = fav(NEB * OWN, BF16)
        w4T = fav(OWN, BF16)
        w4row = [fav(OWN, BF16) for _ in range(2)]
        FFN_KEYS = ["eWg0", "eWg1", "eWu0", "eWd0",
                    "hw_all", "w4T", "w4row0", "w4row1"]
        assert offf <= AR_BYTES, offf

        offz = 0

        def zav(cols, dt):
            nonlocal offz
            r = av(offz, cols, dt)
            offz += cols * (4 if dt == F32 else 2)
            return r

        wo_sb = zav(ND * D, BF16)
        og = [zav(D, BF16) for _ in range(2)]
        oTt = [zav(ND * 128, BF16) for _ in range(2)]
        xo = [zav(D, F32) for _ in range(2)]
        sfin = zav(D, F32)
        outt = [zav(D, F32) for _ in range(2)]
        FIN_KEYS = ["wo_sb", "og0", "og1", "oTt0", "oTt1", "xo0", "xo1",
                    "sfin", "outt0", "outt1"]
        assert offz <= AR_BYTES, offz

        pm = [ec(nc.psum_tensor(f"pm{i}", [128, 512], F32)) for i in range(4)]
        pb = [ec(nc.psum_tensor(f"pb{i}", [128, 1024], BF16)) for i in range(2)]
        po2 = [ec(nc.psum_tensor(f"po2{i}", [128, 288], F32)) for i in range(2)]

        cvals = (0.0, 1.0, 1e-5, 1e-6)
        cap = sbuf("caps", [128, len(cvals)], F32)
        for i_, v_ in enumerate(cvals):
            nc.const_aps.register(F32, v_, cap[:, i_:i_ + 1])

        COLLECT = [None]

        def op(eng, meth, *args, reads=(), writes=(), inc=1, lane=None, **kw):
            if meth == "activation":
                reads = tuple(reads) + ("consts",)
            def fn(EE, _m=meth, _a=args, _k=kw, _r=reads, _w=writes):
                try:
                    return getattr(EE, _m)(*_a, **_k)
                except Exception:
                    import sys
                    shapes = [getattr(x, "shape", None) for x in _a]
                    print(f"FAILING OP: {_m} args={shapes} kw={list(_k)} "
                          f"reads={_r} writes={_w}", file=sys.stderr)
                    raise
            if COLLECT[0] is not None:
                COLLECT[0].append(
                    lambda _f=fn, _r=reads, _w=writes, _i=inc, _l=lane, _e=eng:
                    sch.emit(_e, _f, _r, _w, _i, lane=_l))
                return None
            return sch.emit(eng, fn, reads, writes, inc, lane=lane)

        def collect(builder):
            lst = []
            COLLECT[0] = lst
            builder()
            COLLECT[0] = None
            return lst

        PIPELINE = True

        def newton_rsqrt(nr, sc, tt_, key, skey, tkey, n):
            x = nr[:, 0:n]
            xi = nr.bitcast(I32)[:, 0:n]
            yi = sc.bitcast(I32)[:, 0:n]
            y = sc[:, 0:n]
            t = tt_[:, 0:n]
            op("V", "tensor_scalar", yi, xi, 1, None, OP.arith_shift_right,
               reads=(key,), writes=(skey,))
            op("V", "tensor_scalar", yi, yi, float(0x5F3759DF), -1.0,
               OP.subtract, op1=OP.mult, reads=(skey,), writes=(skey,))
            for _ in range(3):
                op("V", "tensor_tensor", t, y, y, OP.mult,
                   reads=(skey,), writes=(tkey,))
                op("V", "tensor_tensor", t, t, x, OP.mult,
                   reads=(tkey, key), writes=(tkey,))
                op("V", "tensor_scalar", t, t, -0.5, 1.5, OP.mult, op1=OP.add,
                   reads=(tkey,), writes=(tkey,))
                op("V", "tensor_tensor", y, y, t, OP.mult,
                   reads=(skey, tkey), writes=(skey,))

        def interleave_emit(lb, la):
            if not PIPELINE:
                for f_ in lb:
                    f_()
                for f_ in la:
                    f_()
                return
            nb, na = len(lb), len(la)
            ib = ia = 0
            while ib < nb or ia < na:
                if ia >= na or (ib < nb and ib * (na + 1) <= ia * (nb + 1)):
                    lb[ib](); ib += 1
                else:
                    la[ia](); ia += 1

        def mm(ps, lhsT, rhs, reads, writes, start=True, stop=True,
               skip_group_check=False):
            return op("T", "matmul", ps, lhsT, rhs, start=start, stop=stop,
                      skip_group_check=skip_group_check,
                      reads=reads, writes=writes)

        def tr(ps, in_ap, ident_ap, reads, writes):
            return op("T", "transpose", ps, in_ap, ident_ap,
                      reads=tuple(reads) + ("ident",), writes=writes)

        def dma(q, out_ap, in_ap, reads, writes):
            return op(q, "dma_start", reads=reads, writes=writes, inc=16,
                      out=out_ap, in_=in_ap)

        def dma_w(q, out_ap, in_2d, nch, reads, writes, slow=False):
            o3 = out_ap.rearrange("p (c n) -> p c n", c=nch)
            i3 = in_2d.rearrange("(c p) n -> p c n", p=128)
            kw = {"allow_slow_non_contiguous": True} if slow else {}
            return op(q, "dma_start", reads=reads, writes=writes, inc=16,
                      out=o3, in_=i3, **kw)

        def mask_init(tile, key, setval, cmpop, fill, cm=1, pat=-1):
            op("P", "memset", tile[:, :], setval, writes=(key,))
            op("P", "affine_select", tile[:, :], tile[:, :], [[pat, 128]],
               cmpop, fill, base=0, channel_multiplier=cm,
               reads=(key,), writes=(key,))

        mask_init(ident_bf, "ident", 0.0, OP.not_equal, 1.0)
        mask_init(ident_f, "ident", 0.0, OP.not_equal, 1.0)
        mask_init(maskSL, "maskSL", 1.0, OP.is_gt, 0.0)
        mask_init(triuK, "triuK", 1.0, OP.is_ge, 0.0, cm=-1, pat=1)
        mask_init(maskneg, "maskneg", 0.0, OP.is_ge, -1e30, cm=-1, pat=1)
        op("P", "memset", ones1[:, :], 1.0, writes=("ones1",))
        op("P", "memset", ones1b[:, :], 1.0, writes=("ones1",))
        for i_, v_ in enumerate(cvals):
            op("P", "memset", cap[:, i_:i_ + 1], v_, writes=("consts",))


        dma("Q", qn_row[:, :], qn_in.ap(), reads=(), writes=("qn_row",))
        dma("Q", kn_row[:, :], kn_in.ap(), reads=(), writes=("kn_row",))
        dma("Q", rb_row[:, :], rbias.ap(), reads=(), writes=("rb_row",))
        dma("Q", idx_sb[:, :], gidx.ap(), reads=(), writes=("idx_sb",))
        mm(pm[0][:, 0:d], ones1[:, 0:128], qn_row[:, :], reads=("ones1", "qn_row"), writes=("pm0",))
        op("V", "tensor_copy", qn_bc[:, :], pm[0][:, 0:d], reads=("pm0",), writes=("qn_bc",))
        mm(pm[0][:, 0:d], ones1[:, 0:128], kn_row[:, :], reads=("ones1", "kn_row"), writes=("pm0",))
        op("V", "tensor_copy", kn_bc[:, :], pm[0][:, 0:d], reads=("pm0",), writes=("kn_bc",))
        mm(pm[0][:, 0:E], ones1[:, 0:128], rb_row[:, :], reads=("ones1", "rb_row"), writes=("pm0",))
        op("V", "tensor_copy", rb_bc[:, :], pm[0][:, 0:E], reads=("pm0",), writes=("rb_bc",))

        dma_w("P", wg_sb[:, :], Wg4.ap(), ND, reads=(), writes=("wg_sb",), slow=True)
        dma_w("P", wb_sb[:, :], Wb4.ap(), ND, reads=(), writes=("wb_sb",), slow=True)
        for h in range(NH):
            dma("P", cw_sb[:, h * KC:(h + 1) * KC], conv4.ap()[h * 128:(h + 1) * 128, :],
                reads=(), writes=("cw_sb",))

        SKIP_MAIN = (mode == "p2")
        def s_zg(c0, c1):
            return stripe[0:NH, c0:c1]

        def s_zb(c0, c1):
            return stripe[0:NH, T + c0:T + c1]

        def s_sc(c0, c1):
            return stripe[0:NH, 2 * T + c0:2 * T + c1]

        wcb_bf = wcb[:, :]
        khat_bf = khat[:, :]
        xinX = [xin[0][:, :], xin[1][:, :], wcb_bf[:, 0:D], wcb_bf[:, D:2 * D]]
        stgX = [vt[:, 0:D], wcb_bf[:, 2 * D:3 * D]]
        xntX = [xnt[0][:, :], xnt[1][:, :], khat_bf[:, 0:D], khat_bf[:, D:2 * D]]
        trbank = [(pb[0], "pb0"), (pb[1], "pb1")]

        def ln_tile(src_dram, ti, full):
            b = ti % 4
            b2 = ti % 2
            xk = f"xin{b}"
            sk = f"stg_tok{b2}"
            dma("P", xinX[b], src_dram.ap()[ti * 128:(ti + 1) * 128, :],
                reads=(), writes=(xk,))
            nbst = (D + 511) // 512
            bns, bnsk = bnst4[:, b * 12:(b + 1) * 12], f"bnst{b}"
            bna, bnak = bnag4[:, b * 2:(b + 1) * 2], f"bnag{b}"
            for j in range(nbst):
                c0, c1 = j * 512, min(D, (j + 1) * 512)
                op("V", "bn_stats", bns[:, j * 6:(j + 1) * 6], xinX[b][:, c0:c1],
                   reads=(xk,), writes=(bnsk,))
            op("V", "bn_aggr", bna[:, :], bns[:, 0:6 * nbst],
               reads=(bnsk,), writes=(bnak,))
            op("V", "tensor_scalar", lnn[b][:, :], bna[:, 1:2], 1e-5, None, OP.add,
               reads=(bnak,), writes=(f"lnn{b}",))
            newton_rsqrt(lnn[b], lnsc[b], lnt[b], f"lnn{b}", f"lnsc{b}", f"lnt{b}", 1)
            op("V", "tensor_scalar", stgX[b2][:, 0:D], xinX[b], bna[:, 0:1],
               lnsc[b][:, 0:1], OP.subtract, op1=OP.mult,
               reads=(xk, bnak, f"lnsc{b}"), writes=(sk,))
            for c in range(ND):
                pbt, pbk = trbank[c % 2]
                tr(pbt[:, 0:128], stgX[b2][:, c * 128:(c + 1) * 128], ident_bf[:, :],
                   reads=(sk,), writes=(pbk,))
                if full:
                    op("V", "tensor_copy", xntX[b][:, c * 128:(c + 1) * 128],
                       pbt[:, 0:128], reads=(pbk,), writes=(f"xnt{b}",))
                else:
                    op("V", "tensor_copy", a2[:, c * OWN + ti * 128: c * OWN + (ti + 1) * 128],
                       pbt[:, 0:128], reads=(pbk,), writes=("a2",))
            if full:
                dma("Q", xnT_dram.ap()[ti], xntX[b], reads=(f"xnt{b}",),
                    writes=("xnT_dram",))
                for c in range(ND):
                    mm(pm[1][0:NH, 0:128], wg_sb[:, c * NH:(c + 1) * NH],
                       xntX[b][:, c * 128:(c + 1) * 128],
                       reads=("wg_sb", f"xnt{b}"), writes=("pm1",),
                       start=(c == 0), stop=(c == ND - 1))
                op("V", "tensor_copy", s_zg(ti * 128, (ti + 1) * 128), pm[1][0:NH, 0:128],
                   reads=("pm1",), writes=("stripe",))
                for c in range(ND):
                    mm(pm[3][0:NH, 0:128], wb_sb[:, c * NH:(c + 1) * NH],
                       xntX[b][:, c * 128:(c + 1) * 128],
                       reads=("wb_sb", f"xnt{b}"), writes=("pm3",),
                       start=(c == 0), stop=(c == ND - 1))
                op("V", "tensor_copy", s_zb(ti * 128, (ti + 1) * 128), pm[3][0:NH, 0:128],
                   reads=("pm3",), writes=("stripe",))

        N_EARLY = min(NOT, (NEB * OWN) // D)

        def ln_early(ti):
            b = 2 + ti % 2
            xk = f"xin{b}"
            dma("P", xinX[b], x_own.ap()[ti * 128:(ti + 1) * 128, :],
                reads=(), writes=(xk,))
            bns, bnsk = bnst4[:, b * 12:(b + 1) * 12], f"bnst{b}"
            bna, bnak = bnag4[:, b * 2:(b + 1) * 2], f"bnag{b}"
            nbst = (D + 511) // 512
            for j in range(nbst):
                c0, c1 = j * 512, min(D, (j + 1) * 512)
                op("V", "bn_stats", bns[:, j * 6:(j + 1) * 6], xinX[b][:, c0:c1],
                   reads=(xk,), writes=(bnsk,))
            op("V", "bn_aggr", bna[:, :], bns[:, 0:6 * nbst],
               reads=(bnsk,), writes=(bnak,))
            op("V", "tensor_scalar", lnn[b][:, :], bna[:, 1:2], 1e-5, None, OP.add,
               reads=(bnak,), writes=(f"lnn{b}",))
            newton_rsqrt(lnn[b], lnsc[b], lnt[b], f"lnn{b}", f"lnsc{b}", f"lnt{b}", 1)
            op("V", "tensor_scalar", stgX[1][:, 0:D], xinX[b], bna[:, 0:1],
               lnsc[b][:, 0:1], OP.subtract, op1=OP.mult,
               reads=(xk, bnak, f"lnsc{b}"), writes=("stg_tok1",))
            for c in range(ND):
                pbt, pbk = trbank[c % 2]
                tr(pbt[:, 0:128], stgX[1][:, c * 128:(c + 1) * 128], ident_bf[:, :],
                   reads=("stg_tok1",), writes=(pbk,))
                op("V", "tensor_copy",
                   hw_all[:, ti * D + c * 128: ti * D + (c + 1) * 128],
                   pbt[:, 0:128], reads=(pbk,), writes=("hw_all",))

        for ti in range(NTI if not SKIP_MAIN else 0):
            ln_tile(x_b, ti, True)

        zg, zb, sc = s_zg(0, T), s_zb(0, T), s_sc(0, T)
        if not SKIP_MAIN:
            op("S", "activation", sc, zg, AF.Abs, reads=("stripe",), writes=("stripe",))
            op("S", "activation", sc, sc, AF.Exp, scale=-1.0, reads=("stripe",), writes=("stripe",))
            op("S", "activation", sc, sc, AF.Ln, bias=1.0, reads=("stripe",), writes=("stripe",))
            op("V", "tensor_scalar", zg, zg, 0.0, None, OP.min, reads=("stripe",), writes=("stripe",))
            op("V", "tensor_tensor", zg, zg, sc, OP.subtract, reads=("stripe",), writes=("stripe",))
            op("V", "tensor_tensor_scan", sc, zg, zg, 0.0, OP.add, OP.bypass,
               reads=("stripe",), writes=("stripe",))
            dma("Q", gdram.ap()[0], sc, reads=("stripe",), writes=("gdram",))
            op("V", "tensor_scalar", zg, sc, -1.0, None, OP.mult, reads=("stripe",), writes=("stripe",))
            dma("Q", gdram.ap()[1], zg, reads=("stripe",), writes=("gdram",))
            dma("Q", gdram.ap()[2], zb, reads=("stripe",), writes=("gdram",))

        for k in AT_KEYS:
            sch.merge_deps(k, ["stripe"])
        sch.merge_deps("wcb", ["xin2", "xin3", "stg_tok1"])
        sch.merge_deps("khat", ["xnt2", "xnt3"])
        sch.merge_deps("vt", ["stg_tok0"])
        for ci in range(NCH):
            op("P", "memset", vt[:, ci * 132 + 128: ci * 132 + 129], 1.0, writes=("vt",))

        def head_weights(h):
            hp = h % 2
            for nm, src in (("wq", Wqh), ("wk", Wkh), ("wv", Wvh), ("ww", Wwh)):
                dma_w("P", AT_W[(nm, hp)], src.ap()[:, h * 128:(h + 1) * 128],
                      ND, reads=(), writes=(f"{nm}{hp}",))

        def head_gcols(h, pmx, pmxk):
            hg = h % 2
            for gi, dstc, dck in ((1, GnP[hg], f"GnC{hg}"), (2, bnP[hg], f"bnC{hg}")):
                dma("P", gseg[:, :],
                    gdram.ap()[gi, h].rearrange("(c t) -> c t", c=NCH),
                    reads=("gdram",), writes=("gseg",))
                tr(pmx[0:128, 0:NCH], gseg[:, :], ident_f[0:NCH, 0:NCH],
                   reads=("gseg",), writes=(pmxk,))
                op("V", "tensor_copy", dstc[:, :], pmx[0:128, 0:NCH],
                   reads=(pmxk,), writes=(dck,))
            bnc, btc = bnP[hg], btP[hg]
            bnk, btk = f"bnC{hg}", f"btC{hg}"
            op("S", "activation", bnc[:, :], bnc[:, :], AF.Exp, scale=-1.0,
               reads=(bnk,), writes=(bnk,))
            op("V", "tensor_scalar", bnc[:, :], bnc[:, :], 1.0, None, OP.add,
               reads=(bnk,), writes=(bnk,))
            op("V", "reciprocal", bnc[:, :], bnc[:, :], reads=(bnk,), writes=(bnk,))
            op("V", "tensor_scalar", btc[:, :], bnc[:, :], 2.0, None, OP.mult,
               reads=(bnk,), writes=(btk,))
            op("V", "tensor_scalar", bnc[:, :], bnc[:, :], -2.0, None, OP.mult,
               reads=(bnk,), writes=(bnk,))

        def pass1_convs(h):
            hp = h % 2
            for ci in range(NCH):
                b = ci % 2
                xk = f"xnt{b}"
                pmx, pmxk = (pm[0], "pm0") if b == 0 else (pm[1], "pm1")
                dma("Q", xnt[b][:, :], xnT_dram.ap()[ci], reads=("xnT_dram",), writes=(xk,))
                for c in range(ND):
                    mm(pmx[:, 0:128], AT_W[("ww", hp)][:, c * 128:(c + 1) * 128],
                       xnt[b][:, c * 128:(c + 1) * 128],
                       reads=(f"ww{hp}", xk), writes=(pmxk,),
                       start=(c == 0), stop=(c == ND - 1))
                if ci == 0:
                    op("V", "memset", wpre[:, 0:3], 0.0, writes=("wpre",))
                else:
                    op("V", "tensor_copy", wpre[:, 0:3], wpre[:, 128:131],
                       reads=("wpre",), writes=("wpre",))
                op("V", "tensor_copy", wpre[:, 3:131], pmx[:, 0:128],
                   reads=(pmxk,), writes=("wpre",))
                ca0, ca1 = cvs[:, 0:128], cvs[:, 128:256]
                op("V", "tensor_scalar", ca0, wpre[:, 0:128],
                   cw_sb[:, h * KC: h * KC + 1], None, OP.mult,
                   reads=("wpre", "cw_sb"), writes=("cvs",))
                for i in (1, 2):
                    src_, dst_ = (ca0, ca1) if i % 2 == 1 else (ca1, ca0)
                    op("V", "scalar_tensor_tensor", dst_, wpre[:, i:i + 128],
                       cw_sb[:, h * KC + i: h * KC + i + 1], src_, OP.mult, OP.add,
                       reads=("wpre", "cw_sb", "cvs"), writes=("cvs",))
                op("V", "scalar_tensor_tensor", wcb[:, ci * 128:(ci + 1) * 128],
                   wpre[:, 3:131], cw_sb[:, h * KC + 3: h * KC + 4], ca0,
                   OP.mult, OP.add,
                   reads=("wpre", "cw_sb", "cvs"), writes=("wcb",))

        def pass1_silus(h):
            for ci in range(NCH):
                op("S", "activation", wcb[:, ci * 128:(ci + 1) * 128],
                   wcb[:, ci * 128:(ci + 1) * 128], AF.Silu,
                   reads=("wcb",), writes=("wcb",))

        def head_prefetch(h, pmx, pmxk):
            head_weights(h)
            head_gcols(h, pmx, pmxk)
            pass1_convs(h)
            pass1_silus(h)

        for h in range(NH if not SKIP_MAIN else 0):
            hp = h % 2
            hg = h % 2
            if h == 0:
                head_prefetch(0, pm[2], "pm2")

            SS = 4
            NSUP = NCH // SS

            def stage_A(ah, ci, vpr=False):
                hp = ah % 2
                hg = ah % 2
                p = ci % 2
                sp = (ci // SS) % 2
                li = ci % SS
                slot = li + SS * sp
                tk = lambda nm: t128[(nm, p)]
                kk = lambda nm: f"{nm}{p}"
                t8k = lambda nm: f"{nm}s{slot}"
                xk = f"xnt{p}"
                dma("Q", xnt[p][:, :], xnT_dram.ap()[ci], reads=("xnT_dram",), writes=(xk,))
                dma("Q", gsg[p][:, :], gdram.ap()[0, ah:ah + 1, ci * 128:(ci + 1) * 128],
                    reads=("gdram",), writes=(f"gsg{p}",))
                mm(pm[1][:, 0:128], ones1[:, 0:128], gsg[p][:, :],
                   reads=("ones1", f"gsg{p}"), writes=("pm1",))
                op("V", "tensor_copy", gbc[sp][:, li * 128:(li + 1) * 128], pm[1][:, 0:128],
                   reads=("pm1",), writes=(f"gbc{sp}",))
                for c in range(ND):
                    mm(pm[0][:, 0:128], xnt[p][:, c * 128:(c + 1) * 128],
                       AT_W[("wq", hp)][:, c * 128:(c + 1) * 128],
                       reads=(xk, f"wq{hp}"), writes=("pm0",),
                       start=(c == 0), stop=(c == ND - 1))
                nr, sc_, tt_ = nrm[p], nsc[p], nt2[p]
                nrk, sck, ttk = f"nrm{p}", f"nsc{p}", f"nt2{p}"
                op("S", "activation", tk("scrbf")[:, :], pm[0][:, 0:128], AF.Square,
                   scale=float(d) ** -0.5, accum_out=nr[:, 0:1],
                   reads=("pm0",), writes=(kk("scrbf"), nrk))
                for c in range(ND):
                    mm(pm[1][:, 0:128], xnt[p][:, c * 128:(c + 1) * 128],
                       AT_W[("wk", hp)][:, c * 128:(c + 1) * 128],
                       reads=(xk, f"wk{hp}"), writes=("pm1",),
                       start=(c == 0), stop=(c == ND - 1))
                op("S", "activation", tk("scrbf")[:, :], pm[1][:, 0:128], AF.Square,
                   scale=float(d) ** -0.5, accum_out=nr[:, 1:2],
                   reads=("pm1",), writes=(kk("scrbf"), nrk))
                tr(pb[0][:, 0:128], wcb[:, ci * 128:(ci + 1) * 128], ident_bf[:, :],
                   reads=("wcb",), writes=("pb0",))
                op("S", "activation", tk("scrbf")[:, :], pb[0][:, 0:128], AF.Square,
                   accum_out=nr[:, 2:3], reads=("pb0",), writes=(kk("scrbf"), nrk))
                op("V", "tensor_scalar", nr[:, 0:3], nr[:, 0:3], 1e-6, None, OP.add,
                   reads=(nrk,), writes=(nrk,))
                newton_rsqrt(nr, sc_, tt_, nrk, sck, ttk, 3)
                op("V", "scalar_tensor_tensor", tk("q_tok")[:, :], pm[0][:, 0:128],
                   sc_[:, 0:1], qn_bc[:, :], OP.mult, OP.mult,
                   reads=("pm0", sck, "qn_bc"), writes=(kk("q_tok"),))
                op("V", "scalar_tensor_tensor", tk("k_tok")[:, :], pm[1][:, 0:128],
                   sc_[:, 1:2], kn_bc[:, :], OP.mult, OP.mult,
                   reads=("pm1", sck, "kn_bc"), writes=(kk("k_tok"),))
                op("V", "tensor_scalar", tk("w_tok")[:, :], pb[0][:, 0:128],
                   sc_[:, 2:3], None, OP.mult, reads=("pb0", sck), writes=(kk("w_tok"),))
                for c in range(ND):
                    mm(pm[0][:, 0:128], xnt[p][:, c * 128:(c + 1) * 128],
                       AT_W[("wv", hp)][:, c * 128:(c + 1) * 128],
                       reads=(xk, f"wv{hp}"), writes=("pm0",),
                       start=(c == 0), stop=(c == ND - 1))
                if vpr:
                    op("V", "tensor_copy", vpre[:, li * 128:(li + 1) * 128], pm[0][:, 0:128],
                       reads=("pm0",), writes=("vpre",))
                else:
                    op("V", "tensor_copy", vt[:, ci * 132: ci * 132 + 128], pm[0][:, 0:128],
                       reads=("pm0",), writes=("vt",))
                op("V", "tensor_scalar", tk("bw_tok")[:, :], tk("w_tok")[:, :],
                   btP[hg][:, ci:ci + 1], None, OP.mult,
                   reads=(kk("w_tok"), f"btC{hg}"), writes=(kk("bw_tok"),))
                tr(pb[0][:, 0:128], tk("q_tok")[:, :], ident_bf[:, :],
                   reads=(kk("q_tok"),), writes=("pb0",))
                op("V", "tensor_copy", tk("qT")[:, :], pb[0][:, 0:128],
                   reads=("pb0",), writes=(kk("qT"),))
                tr(pb[1][:, 0:128], tk("k_tok")[:, :], ident_bf[:, :],
                   reads=(kk("k_tok"),), writes=("pb1",))
                op("V", "tensor_copy", tk("kT")[:, :], pb[1][:, 0:128],
                   reads=("pb1",), writes=(kk("kT"),))
                tr(pb[0][:, 0:128], tk("w_tok")[:, :], ident_bf[:, :],
                   reads=(kk("w_tok"),), writes=("pb0",))
                op("V", "tensor_copy", tk("wT")[:, :], pb[0][:, 0:128],
                   reads=("pb0",), writes=(kk("wT"),))
                mm(pm[0][:, 0:128], tk("wT")[:, :], tk("wT")[:, :],
                   reads=(kk("wT"),), writes=("pm0",))
                op("V", "scalar_tensor_tensor", tk("A1")[:, :], pm[0][:, 0:128],
                   bnP[hg][:, ci:ci + 1], maskSL[:, :], OP.mult, OP.mult,
                   reads=("pm0", f"bnC{hg}", "maskSL"), writes=(kk("A1"),))
                mm(pm[1][:, 0:128], tk("wT")[:, :], tk("kT")[:, :],
                   reads=(kk("wT"), kk("kT")), writes=("pm1",))
                op("V", "scalar_tensor_tensor", tk("R1")[:, :], pm[1][:, 0:128],
                   btP[hg][:, ci:ci + 1], maskSL[:, :], OP.mult, OP.mult,
                   reads=("pm1", f"btC{hg}", "maskSL"), writes=(kk("R1"),))
                tr(pb[0][:, 0:128], tk("A1")[:, :], ident_bf[:, :],
                   reads=(kk("A1"),), writes=("pb0",))
                op("V", "tensor_copy", tk("A1T")[:, :], pb[0][:, 0:128],
                   reads=("pb0",), writes=(kk("A1T"),))
                op("V", "tensor_tensor", tk("PTa")[:, :], pb[0][:, 0:128], ident_bf[:, :],
                   OP.add, reads=("pb0", "ident"), writes=(kk("PTa"),))
                curA, curAT = "A1", "A1T"
                pt_cur, pt_nxt = "PTa", "PTb"
                lvl = 2
                while lvl <= C // 2:
                    nA = "Ak2" if curA != "Ak2" else "Ak"
                    nAT = "Ak2T" if curA != "Ak2" else "AkT"
                    mm(pm[0][:, 0:128], tk(curAT)[:, :], tk(curA)[:, :],
                       reads=(kk(curAT), kk(curA)), writes=("pm0",))
                    op("V", "tensor_copy", tk(nA)[:, :], pm[0][:, 0:128],
                       reads=("pm0",), writes=(kk(nA),))
                    if lvl < C // 2:
                        mm(pm[1][:, 0:128], tk(curA)[:, :], tk(curAT)[:, :],
                           reads=(kk(curA), kk(curAT)), writes=("pm1",))
                        op("V", "tensor_copy", tk(nAT)[:, :], pm[1][:, 0:128],
                           reads=("pm1",), writes=(kk(nAT),))
                    mm(pm[0][:, 0:128], tk(nA)[:, :], tk(pt_cur)[:, :],
                       reads=(kk(nA), kk(pt_cur)), writes=("pm0",))
                    op("V", "tensor_tensor", tk(pt_nxt)[:, :], pm[0][:, 0:128],
                       tk(pt_cur)[:, :], OP.add,
                       reads=("pm0", kk(pt_cur)), writes=(kk(pt_nxt),))
                    curA, curAT = nA, nAT
                    pt_cur, pt_nxt = pt_nxt, pt_cur
                    lvl *= 2
                TinvT, TinvTk = tk(pt_cur), kk(pt_cur)
                mm(pm[0][:, 0:128], TinvT[:, :], tk("bw_tok")[:, :],
                   reads=(TinvTk, kk("bw_tok")), writes=("pm0",))
                op("V", "tensor_copy", tk("Wbar")[:, :], pm[0][:, 0:128],
                   reads=("pm0",), writes=(kk("Wbar"),))
                mm(pm[1][:, 0:128], TinvT[:, :], tk("R1")[:, :],
                   reads=(TinvTk, kk("R1")), writes=("pm1",))
                op("V", "tensor_copy", tk("Xc")[:, :], pm[1][:, 0:128],
                   reads=("pm1",), writes=(kk("Xc"),))
                mm(pm[0][:, 0:128], tk("wT")[:, :], tk("qT")[:, :],
                   reads=(kk("wT"), kk("qT")), writes=("pm0",))
                op("V", "scalar_tensor_tensor", tk("WQm")[:, :], pm[0][:, 0:128],
                   -1.0, triuK[:, :], OP.mult, OP.mult,
                   reads=("pm0", "triuK"), writes=(kk("WQm"),))
                mm(pm[1][:, 0:128], tk("Wbar")[:, :], tk("WQm")[:, :],
                   reads=(kk("Wbar"), kk("WQm")), writes=("pm1",))
                op("V", "scalar_tensor_tensor", t8[("QtilT", slot)][:, :], pm[1][:, 0:128],
                   1.0, tk("qT")[:, :], OP.mult, OP.add,
                   reads=("pm1", kk("qT")), writes=(t8k("QtilT"),))
                mm(pm[0][:, 0:128], tk("w_tok")[:, :], tk("Xc")[:, :],
                   reads=(kk("w_tok"), kk("Xc")), writes=("pm0",))
                op("V", "scalar_tensor_tensor", t8[("KtilT", slot)][:, :], pm[0][:, 0:128],
                   -1.0, tk("kT")[:, :], OP.mult, OP.add,
                   reads=("pm0", kk("kT")), writes=(t8k("KtilT"),))
                mm(pm[1][:, 0:128], tk("Wbar")[:, :], tk("w_tok")[:, :],
                   reads=(kk("Wbar"), kk("w_tok")), writes=("pm1",))
                op("V", "scalar_tensor_tensor", t8[("PhiT", slot)][:, :], pm[1][:, 0:128],
                   -1.0, ident_bf[:, :], OP.mult, OP.add,
                   reads=("pm1", "ident"), writes=(t8k("PhiT"),))
                if li < SS - 1 or True:
                    mm(pm[0][:, 0:128], tk("w_tok")[:, :], tk("Wbar")[:, :],
                       reads=(kk("w_tok"), kk("Wbar")), writes=("pm0",))
                    op("V", "scalar_tensor_tensor", t8[("Phi", slot)][:, :], pm[0][:, 0:128],
                       -1.0, ident_bf[:, :], OP.mult, OP.add,
                       reads=("pm0", "ident"), writes=(t8k("Phi"),))
                mm(pm[1][:, 0:128], tk("kT")[:, :], tk("qT")[:, :],
                   reads=(kk("kT"), kk("qT")), writes=("pm1",), start=True, stop=False)
                mm(pm[1][:, 0:128], tk("Xc")[:, :], tk("WQm")[:, :],
                   reads=(kk("Xc"), kk("WQm")), writes=("pm1",), start=False, stop=True)
                op("V", "tensor_tensor", nsc[p][:, 3:4], GnP[hg][:, ci:ci + 1],
                   gbc[sp][:, li * 128 + 64: li * 128 + 65], OP.add,
                   reads=(f"GnC{hg}", f"gbc{sp}"), writes=(sck,))
                op("V", "scalar_tensor_tensor", tmpD[p][:, :], pm[1][:, 0:128],
                   scale, maskneg[:, :], OP.mult, OP.add,
                   reads=("pm1", "maskneg"), writes=(f"tmpD{p}",))
                op("S", "activation", t8[("expD", slot)][:, :], tmpD[p][:, :], AF.Exp,
                   bias=nsc[p][:, 3:4],
                   reads=(f"tmpD{p}", sck), writes=(t8k("expD"),))

            def stage_B(bh, s):
                hg = bh % 2
                sp = s % 2
                slx = lambda li: li + SS * sp
                t8s = lambda nm, li: t8[(nm, slx(li))]
                t8sk = lambda nm, li: f"{nm}s{slx(li)}"
                qh, qhk = qhat[sp], f"qhat{sp}"
                for li_ in range(SS):
                    op("V", "tensor_scalar", bias4[li_][:, :], GnP[hg][:, :],
                       gbc[sp][:, li_ * 128 + 64: li_ * 128 + 65], None, OP.add,
                       reads=(f"GnC{hg}", f"gbc{sp}"), writes=(f"bias4_{li_}",))
                op("V", "tensor_copy", qh[:, 0:128], t8s("QtilT", 0)[:, :],
                   reads=(t8sk("QtilT", 0),), writes=(qhk,))
                mm(pm[2][:, 0:128], t8s("Phi", 0)[:, :], t8s("QtilT", 1)[:, :],
                   reads=(t8sk("Phi", 0), t8sk("QtilT", 1)), writes=("pm2",))
                op("V", "tensor_copy", qh[:, 128:256], pm[2][:, 0:128],
                   reads=("pm2",), writes=(qhk,))
                mm(pm[3][:, 0:128], t8s("Phi", 1)[:, :], t8s("QtilT", 2)[:, :],
                   reads=(t8sk("Phi", 1), t8sk("QtilT", 2)), writes=("pm3",))
                op("V", "tensor_copy", qch[:, :], pm[3][:, 0:128],
                   reads=("pm3",), writes=("qch",))
                mm(pm[2][:, 0:128], t8s("Phi", 0)[:, :], qch[:, :],
                   reads=(t8sk("Phi", 0), "qch"), writes=("pm2",))
                op("V", "tensor_copy", qh[:, 256:384], pm[2][:, 0:128],
                   reads=("pm2",), writes=(qhk,))
                mm(pm[3][:, 0:128], t8s("Phi", 2)[:, :], t8s("QtilT", 3)[:, :],
                   reads=(t8sk("Phi", 2), t8sk("QtilT", 3)), writes=("pm3",))
                op("V", "tensor_copy", qch[:, :], pm[3][:, 0:128],
                   reads=("pm3",), writes=("qch",))
                mm(pm[2][:, 0:128], t8s("Phi", 1)[:, :], qch[:, :],
                   reads=(t8sk("Phi", 1), "qch"), writes=("pm2",))
                op("V", "tensor_copy", qch2[:, :], pm[2][:, 0:128],
                   reads=("pm2",), writes=("qch2",))
                mm(pm[3][:, 0:128], t8s("Phi", 0)[:, :], qch2[:, :],
                   reads=(t8sk("Phi", 0), "qch2"), writes=("pm3",))
                op("V", "tensor_copy", qh[:, 384:512], pm[3][:, 0:128],
                   reads=("pm3",), writes=(qhk,))

                bank_started = [False, False]

                def po_region(li):
                    return po2[li // 2][:, (li % 2) * 144: (li % 2) * 144 + 129], f"po2{li // 2}"

                def po_flags(li, last=False):
                    bk = li // 2
                    st = not bank_started[bk]
                    bank_started[bk] = True
                    return st, last

                def ammS(sb):
                    pmx, pmxk = (pm[2], "pm2") if sb % 2 == 0 else (pm[3], "pm3")
                    mm(pmx[:, 0:512], khat[:, sb * 128:(sb + 1) * 128], qh[:, :],
                       reads=("khat", qhk), writes=(pmxk,))
                    for li_ in range(SS):
                        op("S", "activation", expS[sb % 3][:, li_ * 128:(li_ + 1) * 128],
                           pmx[:, li_ * 128:(li_ + 1) * 128], AF.Exp,
                           scale=scale, bias=bias4[li_][:, sb:sb + 1],
                           reads=(pmxk, f"bias4_{li_}"), writes=(f"expS{sb % 3}",))

                def pvS(sb):
                    for li in range(SS):
                        reg, rk = po_region(li)
                        st, _ = po_flags(li)
                        mm(reg, expS[sb % 3][:, li * 128:(li + 1) * 128],
                           vt[:, sb * 132: sb * 132 + 129],
                           reads=(f"expS{sb % 3}", "vt"), writes=(rk,),
                           start=st, stop=False, skip_group_check=True)

                npre = SS * s
                if npre > 0:
                    ammS(0)
                    if npre > 1:
                        ammS(1)
                    for sb in range(2, npre):
                        ammS(sb)
                        pvS(sb - 2)
                    for sb in range(max(0, npre - 2), npre):
                        pvS(sb)

                for li in range(SS):
                    ci = SS * s + li
                    for lj in range(li):
                        pmx, pmxk = (pm[2], "pm2") if lj % 2 == 0 else (pm[3], "pm3")
                        mm(pmx[:, 0:128], khat_loc[:, lj * 128:(lj + 1) * 128],
                           t8s("QtilT", li)[:, :],
                           reads=("khat_loc", t8sk("QtilT", li)), writes=(pmxk,))
                        op("S", "activation", expI[lj % 2][:, :], pmx[:, 0:128],
                           AF.Exp, scale=scale,
                           bias=bias4[li][:, SS * s + lj: SS * s + lj + 1],
                           reads=(pmxk, f"bias4_{li}"), writes=(f"expI{lj % 2}",))
                        reg, rk = po_region(li)
                        st, _ = po_flags(li)
                        mm(reg, expI[lj % 2][:, :],
                           vt[:, (SS * s + lj) * 132: (SS * s + lj) * 132 + 129],
                           reads=(f"expI{lj % 2}", "vt"), writes=(rk,),
                           start=st, stop=False, skip_group_check=True)
                    reg, rk = po_region(li)
                    st, _ = po_flags(li)
                    mm(reg, t8s("expD", li)[:, :], vt[:, ci * 132: ci * 132 + 129],
                       reads=(t8sk("expD", li), "vt"), writes=(rk,),
                       start=st, stop=(li % 2 == 1), skip_group_check=True)
                    if li > 0:
                        mm(pm[2][:, 0:li * 128], t8s("PhiT", li)[:, :],
                           khat_loc[:, 0:li * 128],
                           reads=(t8sk("PhiT", li), "khat_loc"), writes=("pm2",))
                        op("V", "tensor_copy", khat_loc[:, 0:li * 128],
                           pm[2][:, 0:li * 128], reads=("pm2",), writes=("khat_loc",))
                    if li < SS - 1:
                        op("V", "tensor_copy", khat_loc[:, li * 128:(li + 1) * 128],
                           t8s("KtilT", li)[:, :],
                           reads=(t8sk("KtilT", li),), writes=("khat_loc",))
                for li in range(SS):
                    ci = SS * s + li
                    reg, rk = po_region(li)
                    op("V", "reciprocal", orc[:, :], reg[:, 128:129],
                       reads=(rk,), writes=("orc",))
                    op("V", "tensor_scalar",
                       a2[:, ci * HD + h * 128: ci * HD + (h + 1) * 128],
                       reg[:, 0:128], orc[:, :], None, OP.mult,
                       reads=(rk, "orc"), writes=("a2",))
                if s < NSUP - 1:
                    mm(pm[2][:, 0:128], t8s("Phi", 2)[:, :], t8s("PhiT", 3)[:, :],
                       reads=(t8sk("Phi", 2), t8sk("PhiT", 3)), writes=("pm2",))
                    op("V", "tensor_copy", psT[:, :], pm[2][:, 0:128],
                       reads=("pm2",), writes=("psT",))
                    mm(pm[3][:, 0:128], t8s("Phi", 1)[:, :], psT[:, :],
                       reads=(t8sk("Phi", 1), "psT"), writes=("pm3",))
                    op("V", "tensor_copy", psT2[:, :], pm[3][:, 0:128],
                       reads=("pm3",), writes=("psT2",))
                    mm(pm[2][:, 0:128], t8s("Phi", 0)[:, :], psT2[:, :],
                       reads=(t8sk("Phi", 0), "psT2"), writes=("pm2",))
                    op("V", "tensor_copy", psT[:, :], pm[2][:, 0:128],
                       reads=("pm2",), writes=("psT",))
                    for tt in range(s):
                        pmx, pmxk = (pm[3], "pm3") if tt % 2 == 0 else (pm[2], "pm2")
                        mm(pmx[:, 0:512], psT[:, :], khat[:, tt * 512:(tt + 1) * 512],
                           reads=("psT", "khat"), writes=(pmxk,))
                        op("V", "tensor_copy", khat[:, tt * 512:(tt + 1) * 512],
                           pmx[:, 0:512], reads=(pmxk,), writes=("khat",))
                    op("V", "tensor_copy", khat[:, s * 512: s * 512 + 384],
                       khat_loc[:, :], reads=("khat_loc",), writes=("khat",))
                    op("V", "tensor_copy", khat[:, s * 512 + 384: (s + 1) * 512],
                       t8s("KtilT", 3)[:, :], reads=(t8sk("KtilT", 3),), writes=("khat",))

            def vpre_restore():
                for cci in range(SS):
                    op("V", "tensor_copy", vt[:, cci * 132: cci * 132 + 128],
                       vpre[:, cci * 128:(cci + 1) * 128],
                       reads=("vpre",), writes=("vt",))

            if h == 0:
                for f_ in collect(lambda: [stage_A(h, c) for c in range(SS)]):
                    f_()
            for s in range(NSUP):
                if s == 0 and h > 0:
                    lb = collect(lambda: (vpre_restore(), stage_B(h, 0)))
                else:
                    lb = collect(lambda s=s: stage_B(h, s))
                if s + 1 < NSUP:
                    la = collect(lambda s=s: [stage_A(h, SS * (s + 1) + l) for l in range(SS)])
                elif h + 1 < NH:
                    la = collect(lambda h2=h + 1: (head_prefetch(h2, pm[0], "pm0"),
                                                   [stage_A(h2, c, vpr=True)
                                                    for c in range(SS)]))
                else:
                    la = []
                interleave_emit(lb, la)

        for k in FFN_KEYS:
            sch.merge_deps(k, AT_KEYS + ["stripe"])
        if not SKIP_MAIN and mode == "fused":
            dma_w("P", rt_sb[:, :], routerT.ap(), ND, reads=(), writes=("rt_sb",))
            for i in range(NHS):
                dma_w("P", eWg[0][i], sgT.ap()[:, i * 128:(i + 1) * 128],
                      ND, reads=(), writes=("eWg0",))
                dma_w("P", eWu[0][i], suT.ap()[:, i * 128:(i + 1) * 128],
                      ND, reads=(), writes=("eWu0",))
                dma("P", eWd[0][i], sdT.ap()[i * 128:(i + 1) * 128, :],
                    reads=(), writes=("eWd0",))

        if not SKIP_MAIN and mode == "fused":
            for k in ("xin2", "xin3", "stg_tok1"):
                sch.merge_deps(k, ["wcb"])
            sch.merge_deps("hw_all", ["stripe"])
            for t_ in range(N_EARLY):
                ln_early(t_)

        if mode == "fused":
            for ci in range(NCH):
                dma("Q", o_gath.ap()[ci * 128:(ci + 1) * 128, :],
                    a2[:, ci * HD:(ci + 1) * HD], reads=("a2",), writes=("o_gath",))
            def rdb_fn(EE):
                return EE.remote_dma_broadcast(
                    ffn_acc[:, :], a2[:, :], sems["rsem"], sems["lsem"],
                    rdests=[(0, 1)] + [None] * 7)
            sch.emit("P", rdb_fn, reads=("a2",), writes=("rdbring",), inc=1)
            op("P", "trigger_dma", reads=("rdbring",), writes=("rdbtrig",))
            nop1 = op("P", "engine_nop", reads=("a2", "rdbtrig"), writes=())
            nop1.pre_wait = ("lsem", 16)
            nop2 = op("P", "engine_nop", reads=("rdbtrig",), writes=("ffn_acc",))
            nop2.pre_wait = ("rsem", 2)
            for ci in range(NCH):
                dma("Q", o_gath.ap()[T + ci * 128: T + (ci + 1) * 128, :],
                    ffn_acc[:, ci * HD:(ci + 1) * HD],
                    reads=("ffn_acc",), writes=("o_gath",))
        elif mode == "p1":
            for ci in range(NCH):
                dma("Q", o_bounce.ap()[ci * 128:(ci + 1) * 128, :],
                    a2[:, ci * HD:(ci + 1) * HD], reads=("a2",), writes=("o_bounce",))
        cc_op = None

        if not SKIP_MAIN:
            if mode != "fused":
                dma_w("P", rt_sb[:, :], routerT.ap(), ND, reads=(), writes=("rt_sb",))
        for k in ("xin2", "xin3", "stg_tok1"):
            sch.merge_deps(k, ["wcb"])
        sch.merge_deps("stg_tok0", ["vt"])
        ti0 = 0
        if mode == "fused" and not SKIP_MAIN:
            ti0 = N_EARLY
            for ti in range(N_EARLY):
                for c in range(ND):
                    op("V", "tensor_copy",
                       a2[:, c * OWN + ti * 128: c * OWN + (ti + 1) * 128],
                       hw_all[:, ti * D + c * 128: ti * D + (c + 1) * 128],
                       reads=("hw_all",), writes=("a2",))
        for ti in range(ti0, NOT if not SKIP_MAIN else 0):
            ln_tile(x_own, ti, False)

        for ti in range(NOT if not SKIP_MAIN else 0):
            for c in range(ND):
                mm(pm[0][:, 0:E], a2[:, c * OWN + ti * 128: c * OWN + (ti + 1) * 128],
                   rt_sb[:, c * E:(c + 1) * E],
                   reads=("a2", "rt_sb"), writes=("pm0",),
                   start=(c == 0), stop=(c == ND - 1))
            op("V", "tensor_tensor", rtf[0][:, 0:E], pm[0][:, 0:E], rb_bc[:, :],
               OP.add, reads=("pm0", "rb_bc"), writes=("rtf0",))
            op("V", "reduce_max", rts[:, 0:1], rtf[0][:, 0:E], AX.X,
               negate=True, reads=("rtf0",), writes=("rts",))
            op("S", "activation", rtf[1][:, 0:E], rtf[0][:, 0:E], AF.Exp,
               bias=rts[:, 0:1], reads=("rtf0", "rts"), writes=("rtf1",))
            op("V", "tensor_copy", rtf[2][:, 0:E], rtf[1][:, 0:E],
               reads=("rtf1",), writes=("rtf2",))
            for _ in range(3):
                op("V", "reduce_max", rts[:, 1:2], rtf[2][:, 0:E], AX.X,
                   reads=("rtf2",), writes=("rts",))
                op("V", "tensor_scalar", rtf[0][:, 0:E], rtf[2][:, 0:E],
                   rts[:, 1:2], None, OP.is_ge, reads=("rtf2", "rts"), writes=("rtf0",))
                op("V", "scalar_tensor_tensor", rtf[2][:, 0:E], rtf[0][:, 0:E],
                   -1e30, rtf[2][:, 0:E], OP.mult, OP.add,
                   reads=("rtf0", "rtf2"), writes=("rtf2",))
            op("V", "reduce_max", rts[:, 1:2], rtf[2][:, 0:E], AX.X,
               reads=("rtf2",), writes=("rts",))
            op("V", "tensor_scalar", rtf[0][:, 0:E], rtf[1][:, 0:E],
               rts[:, 1:2], None, OP.is_ge, reads=("rtf1", "rts"), writes=("rtf0",))
            op("V", "tensor_tensor", rtf[1][:, 0:E], rtf[1][:, 0:E], rtf[0][:, 0:E],
               OP.mult, reads=("rtf1", "rtf0"), writes=("rtf1",))
            op("V", "reduce_sum", rts[:, 2:3], rtf[1][:, 0:E], AX.X,
               reads=("rtf1",), writes=("rts",))
            op("V", "reciprocal", rts[:, 2:3], rts[:, 2:3], reads=("rts",), writes=("rts",))
            op("V", "tensor_scalar", w4buf[:, :], rtf[1][:, 0:E],
               rts[:, 2:3], None, OP.mult, reads=("rtf1", "rts"), writes=("w4buf",))
            tr(pm[2][0:E, 0:128], w4buf[:, :], ident_f[:, :],
               reads=("w4buf",), writes=("pm2",))
            op("V", "tensor_copy", w4T[0:E, ti * 128:(ti + 1) * 128], pm[2][0:E, 0:128],
               reads=("pm2",), writes=("w4T",))
        if not SKIP_MAIN:
            dma("Q", w4dram.ap(), w4T[0:E, :], reads=("w4T",), writes=("w4dram",))

        blocks = ([("sh", NHS)] + [(bb, EBLK) for bb in range(NBLK)]) if not SKIP_MAIN else []
        for bi, (tag, nE) in enumerate(blocks):
            bp = bi % 2
            for i in range(nE):
                if tag == "sh":
                    if mode == "fused":
                        continue
                    dma_w("P", eWg[bp][i], sgT.ap()[:, i * 128:(i + 1) * 128],
                          ND, reads=(), writes=(f"eWg{bp}",))
                    dma_w("P", eWu[bp][i], suT.ap()[:, i * 128:(i + 1) * 128],
                          ND, reads=(), writes=("eWu0",))
                    dma("P", eWd[bp][i], sdT.ap()[i * 128:(i + 1) * 128, :],
                        reads=(), writes=("eWd0",))
                else:
                    e = tag * EBLK + i
                    dma_w("P", eWg[bp][i], gateT.ap()[e], ND, reads=(), writes=(f"eWg{bp}",))
                    dma_w("P", eWu[bp][i], upT.ap()[e], ND, reads=(), writes=("eWu0",))
                    dma("P", eWd[bp][i], downT.ap()[e], reads=(), writes=("eWd0",))
            for i in range(nE):
                rb_ = i % 2
                if tag != "sh":
                    e = tag * EBLK + i
                    dma("Q", w4row[rb_][0:1, :], w4dram.ap()[e:e + 1, :],
                        reads=("w4dram",), writes=(f"w4row{rb_}",))
                for tg in range(NGR):
                    for c in range(ND):
                        mm(pm[0][:, 0:512], eWg[bp][i][:, c * HME: c * HME + 128],
                           a2[:, c * OWN + tg * 512: c * OWN + (tg + 1) * 512],
                           reads=(f"eWg{bp}", "a2"), writes=("pm0",),
                           start=(c == 0), stop=(c == ND - 1))
                    for c in range(ND):
                        mm(pm[1][:, 0:512], eWu[bp][i][:, c * HME: c * HME + 128],
                           a2[:, c * OWN + tg * 512: c * OWN + (tg + 1) * 512],
                           reads=("eWu0", "a2"), writes=("pm1",),
                           start=(c == 0), stop=(c == ND - 1))
                    op("S", "activation", lnscr[:, 0:512], pm[0][:, 0:512], AF.Silu,
                       reads=("pm0",), writes=("lnscr",))
                    op("V", "scalar_tensor_tensor",
                       hw_all[:, i * OWN + tg * 512: i * OWN + (tg + 1) * 512],
                       pm[1][:, 0:512], 1.0, lnscr[:, 0:512], OP.mult, OP.mult,
                       reads=("pm1", "lnscr"), writes=("hw_all",))
                    if tag != "sh":
                        mm(pm[3][:, 0:512], ones1b[:, :],
                           w4row[rb_][0:1, tg * 512:(tg + 1) * 512],
                           reads=("ones1", f"w4row{rb_}"), writes=("pm3",))
                        op("V", "tensor_tensor",
                           hw_all[:, i * OWN + tg * 512: i * OWN + (tg + 1) * 512],
                           hw_all[:, i * OWN + tg * 512: i * OWN + (tg + 1) * 512],
                           pm[3][:, 0:512], OP.mult,
                           reads=("hw_all", "pm3"), writes=("hw_all",))
            for tt in range(NOT):
                for half in range(NHALF):
                    pma, pk = (pm[2], "pm2") if half % 2 == 0 else (pm[3], "pm3")
                    for i in range(nE):
                        mm(pma[:, 0:HSZ],
                           hw_all[:, i * OWN + tt * 128: i * OWN + (tt + 1) * 128],
                           eWd[bp][i][:, half * HSZ:(half + 1) * HSZ],
                           reads=("hw_all", "eWd0"), writes=(pk,),
                           start=(i == 0), stop=(i == nE - 1))
                    dst = ffn_acc[:, tt * D + half * HSZ: tt * D + (half + 1) * HSZ]
                    if bi == 0:
                        op("V", "tensor_copy", dst, pma[:, 0:HSZ],
                           reads=(pk,), writes=("ffn_acc",))
                    else:
                        op("V", "tensor_tensor", dst, pma[:, 0:HSZ], dst, OP.add,
                           reads=(pk, "ffn_acc"), writes=("ffn_acc",))

        for tt in range(NOT if not SKIP_MAIN else 0):
            for half in range(NHALF):
                sl_ = ffn_acc[:, tt * D + half * HSZ: tt * D + (half + 1) * HSZ]
                op("S", "activation", sl_, sl_, AF.Silu,
                   reads=("ffn_acc",), writes=("ffn_acc",))

        if mode == "p1":
            for ti in range(NOT):
                dma("Q", g_out.ap()[ti * 128:(ti + 1) * 128, :],
                    ffn_acc[:, ti * D:(ti + 1) * D], reads=("ffn_acc",), writes=("g_out",))

        for k in FIN_KEYS:
            sch.merge_deps(k, FFN_KEYS)
        if mode != "p1":
            dma_w("P", wo_sb, Wo_in.ap(), ND, reads=(), writes=("wo_sb",))
        if cc_op is not None:
            sch.barrier_wait("P", cc_op)
        for ti in range(NOT if mode != "p1" else 0):
            b = ti % 2
            ogk = f"og{b}"
            if mode == "p2":
                dma("Q", og[b][:, :], o_own_in.ap()[ti * 128:(ti + 1) * 128, :],
                    reads=(), writes=(ogk,))
            else:
                op("P", "indirect_dma_start", og[b][:, 0:HD], None, o_gath.ap(),
                   bass.IndirectOffsetOnAxis(ap=idx_sb[:, 2 * ti:2 * ti + 1], axis=0),
                   reads=("o_gath", "idx_sb"), writes=(ogk,), inc=16)
                op("P", "indirect_dma_start", og[b][:, HD:2 * HD], None, o_gath.ap(),
                   bass.IndirectOffsetOnAxis(ap=idx_sb[:, 2 * ti + 1:2 * ti + 2], axis=0),
                   reads=("o_gath", "idx_sb"), writes=(ogk,), inc=16)
            dma("Q", xo[b][:, :], x_own.ap()[ti * 128:(ti + 1) * 128, :],
                reads=(), writes=(f"xo{b}",))
            for c in range(ND):
                pbk = c % 2
                tr(pb[pbk][:, 0:128], og[b][:, c * 128:(c + 1) * 128], ident_bf[:, :],
                   reads=(ogk,), writes=(f"pb{pbk}",))
                op("V", "tensor_copy", oTt[b][:, c * 128:(c + 1) * 128], pb[pbk][:, 0:128],
                   reads=(f"pb{pbk}",), writes=(f"oTt{b}",))
            if mode == "p2" and ti < 2:
                pass
            if mode == "p2":
                dma("Q", ffn_acc[:, ti * D:(ti + 1) * D],
                    g_in.ap()[ti * 128:(ti + 1) * 128, :],
                    reads=(), writes=("ffn_acc",))
            for half in range(NHALF):
                pma, pk = pm[half % 2], f"pm{half % 2}"
                for c in range(ND):
                    mm(pma[:, 0:HSZ], oTt[b][:, c * 128:(c + 1) * 128],
                       wo_sb[:, c * D + half * HSZ: c * D + (half + 1) * HSZ],
                       reads=(f"oTt{b}", "wo_sb"), writes=(pk,),
                       start=(c == 0), stop=(c == ND - 1))
                op("V", "tensor_tensor", sfin[:, half * HSZ:(half + 1) * HSZ],
                   pma[:, 0:HSZ],
                   ffn_acc[:, ti * D + half * HSZ: ti * D + (half + 1) * HSZ],
                   OP.mult, reads=(pk, "ffn_acc"), writes=("sfin",))
            op("V", "tensor_tensor", outt[b][:, :], sfin[:, 0:D], xo[b][:, :],
               OP.add, reads=("sfin", f"xo{b}"), writes=(f"outt{b}",))
            dma("Q", out_d.ap()[ti * 128:(ti + 1) * 128, :], outt[b][:, :],
                reads=(f"outt{b}",), writes=("out_d",))

        for l in range(NLANES):
            slot = ("Q", l)
            k = sch.lane_cnt.get(slot, 0)
            if k:
                def wfn(EE, _s=slot, _v=16 * k):
                    return None
                wop = Op("P", None, 0, len(sch.q["P"]))
                wop.pre_wait = (slot, 16 * k)
                sch.q["P"].append(wop)

        sch.finalize()

        def runner(eng_obj, ename):
            for o in sch.q[ename]:
                if o.pre_wait is not None:
                    eng_obj.wait_ge(sems[o.pre_wait[0]], o.pre_wait[1])
                for pop in o.waits:
                    eng_obj.wait_ge(sems[pop.slot], pop.semval)
                if o.fn is not None:
                    ins = o.fn(eng_obj)
                    if o.marked:
                        ins.then_inc(sems[o.slot], o.inc)

        if mode == "fused" and not globals().get("_STUB_BARRIER_OFF", True):
            nc.bir_kernel_barrier_wait(replica_groups, engines=[nc.gpsimd])
        with nc.Block() as block:
            @block.tensor
            def _(e):
                runner(e, "T")

            @block.vector
            def _(e):
                runner(e, "V")

            @block.scalar
            def _(e):
                runner(e, "S")

            @block.gpsimd
            def _(e):
                runner(e, "P")

            @block.sync
            def _(e):
                runner(e, "Q")

    nc.finalize()
    return nc


def make_inmaps(inputs, cfg):
    T, D, NH, d = cfg["T"], cfg["D"], cfg["NH"], cfg["d"]
    E, KC = cfg["E"], cfg["KC"]
    n_cores = cfg["n_cores"]
    OWN = T // 2
    NOT = OWN // 128
    HD = NH * d
    x = np.asarray(inputs["L0_attn_x"], np.float32)
    maps = []
    for c in range(n_cores):
        b, j = c // 2, c % 2
        cols = slice(j * HD, (j + 1) * HD)
        hs = slice(j * NH, (j + 1) * NH)
        idx = np.zeros((128, 3 * NOT), np.uint32)
        for ti in range(NOT):
            base = j * OWN + ti * 128
            idx[:, 2 * ti] = (j * T) + base + np.arange(128)
            idx[:, 2 * ti + 1] = ((1 - j) * T) + base + np.arange(128)
            idx[:, 2 * NOT + ti] = (j * NOT + ti) * 128 + np.arange(128)
        m = {
            "x_b": x[b],
            "x_own": x[b, j * OWN:(j + 1) * OWN],
            "Wqh": np.asarray(inputs["L0_attn_Wq"])[:, cols],
            "Wkh": np.asarray(inputs["L0_attn_Wk"])[:, cols],
            "Wvh": np.asarray(inputs["L0_attn_Wv"])[:, cols],
            "Wwh": np.asarray(inputs["L0_attn_Ww"])[:, cols],
            "Wg4": np.asarray(inputs["L0_attn_Wg"])[:, hs],
            "Wb4": np.asarray(inputs["L0_attn_Wbeta"])[:, hs],
            "conv4": np.asarray(inputs["L0_attn_conv_w"])[cols, :],
            "qn": np.asarray(inputs["L0_attn_qn_w"]).reshape(1, d),
            "kn": np.asarray(inputs["L0_attn_kn_w"]).reshape(1, d),
            "Wo": np.asarray(inputs["L0_attn_Wo"]),
            "routerT": np.asarray(inputs["L0_ffn_router_w"], np.float32).T,
            "rbias": np.asarray(inputs["L0_ffn_router_bias"]).reshape(1, E),
            "gateT": np.asarray(inputs["L0_ffn_gate_w"], np.float32).transpose(0, 2, 1),
            "upT": np.asarray(inputs["L0_ffn_up_w"], np.float32).transpose(0, 2, 1),
            "downT": np.asarray(inputs["L0_ffn_down_w"], np.float32).transpose(0, 2, 1),
            "sgT": np.asarray(inputs["L0_ffn_sgate_w"], np.float32).T,
            "suT": np.asarray(inputs["L0_ffn_sup_w"], np.float32).T,
            "sdT": np.asarray(inputs["L0_ffn_sdown_w"], np.float32).T,
            "gidx": idx,
        }
        m = {k: np.ascontiguousarray(np.asarray(
            v, dtype=(np.uint32 if k == "gidx" else np.float32))) for k, v in m.items()}
        maps.append(m)
    return maps


KERNEL_MODE = "fused"


def p2_inmaps(in_maps, res1, cfg):
    OWN = cfg["T"] // 2
    nc_n = cfg["n_cores"]
    o_list = [np.asarray(res1[c]["o_bounce"]) for c in range(nc_n)]
    maps2 = []
    for c in range(nc_n):
        b, j = c // 2, c % 2
        o_own = np.concatenate(
            [o_list[2 * b][j * OWN:(j + 1) * OWN, :],
             o_list[2 * b + 1][j * OWN:(j + 1) * OWN, :]], axis=1)
        m = dict(in_maps[c])
        m["o_own"] = np.ascontiguousarray(o_own)
        m["g_in"] = np.ascontiguousarray(np.asarray(res1[c]["g_out"]))
        maps2.append(m)
    return maps2


def kernel(**inputs):
    cfg = full_cfg()
    T, D = cfg["T"], cfg["D"]
    OWN = T // 2
    in_maps = make_inmaps(inputs, cfg)
    cores = list(range(cfg["n_cores"]))
    if KERNEL_MODE == "fused":
        nc = build_nc(cfg, "fused")
        res = run_bass_kernel_spmd(nc, in_maps, core_ids=cores)
        final = res.results
    else:
        nc1 = build_nc(cfg, "p1")
        res1 = run_bass_kernel_spmd(nc1, in_maps, core_ids=cores)
        maps2 = p2_inmaps(in_maps, res1.results, cfg)
        nc2 = build_nc(cfg, "p2")
        res2 = run_bass_kernel_spmd(nc2, maps2, core_ids=cores)
        final = res2.results
    B = np.asarray(inputs["L0_attn_x"]).shape[0]
    out = np.zeros((B, T, D), np.float32)
    for c in range(cfg["n_cores"]):
        b, j = c // 2, c % 2
        out[b, j * OWN:(j + 1) * OWN] = final[c]["out"]
    return out
